# version 10
# baseline (speedup 1.0000x reference)
import numpy as np
import ml_dtypes
from contextlib import ExitStack
import concourse.bass as bass
import concourse.mybir as mybir
from concourse.bass_utils import run_bass_kernel_spmd

BF = ml_dtypes.bfloat16
F32 = mybir.dt.float32
BF16 = mybir.dt.bfloat16
AF = mybir.ActivationFunctionType
ALU = mybir.AluOpType
AX = mybir.AxisListType

ENGINES = ("pe", "act", "dve", "pool", "sp")
BLOCK_ATTR = {"pe": "tensor", "act": "scalar", "dve": "vector",
              "pool": "gpsimd", "sp": "sync"}
DMA_QUEUES = ("pool", "sp")

NCORES = 8
B_PER_CORE = 2
T = 2048
D = 1024


class Rec:

    def __init__(self, nc):
        self.nc = nc
        self.ops = []

    def barrier(self):
        self.ops.append(["BARRIER"])

    @staticmethod
    def _psum_keys(ap):
        try:
            t = ap.tensor
            if t.__class__.__name__ != "PSumTensorHandle":
                return ()
            dims = list(ap.ap)
            free_len = None
            shape = list(t.shape)
            free_len = 1
            for d in shape[1:]:
                free_len *= d
            off = ap.offset % free_len if free_len else 0
            ext = 1
            for step, count in dims[1:]:
                ext += (count - 1) * abs(step)
            isz = 2 if str(t.dtype) == "dt.bfloat16" else 4
            b0 = (off * isz) // 2048
            b1 = ((off + ext) * isz - 1) // 2048
            nm = t.name
            return tuple(f"{nm}:b{i}" for i in range(b0, b1 + 1))
        except Exception:
            return ("PSANY",)

    def op(self, eng, method, *, reads=(), writes=(), **kwargs):
        inc = 16 if method in ("dma_start", "dma_start_transpose", "indirect_dma_start") else 1
        reads = tuple(k for k in reads if k != "PS")
        writes = tuple(k for k in writes if k != "PS")
        wk = ("out", "ap", "accum_out")
        for kname, v in kwargs.items():
            if v.__class__.__name__ == "AP":
                pk = self._psum_keys(v)
                if pk:
                    if kname in wk:
                        writes = writes + pk
                    else:
                        reads = reads + pk
        def fn(engine, method=method, kwargs=kwargs):
            return getattr(engine, method)(**kwargs)
        self.ops.append([eng, fn, tuple(reads), tuple(writes), inc])

    def pe(self, method, **kw):   self.op("pe", method, **kw)
    def act(self, method, **kw):  self.op("act", method, **kw)
    def dve(self, method, **kw):  self.op("dve", method, **kw)
    def pool(self, method, **kw): self.op("pool", method, **kw)

    def load(self, out, in_, key, reads=(), q="sp"):
        self.op(q, "dma_start", out=out, in_=in_, reads=reads, writes=(key,))

    def store(self, out, in_, reads, key="::out", q="pool"):
        self.op(q, "dma_start", out=out, in_=in_, reads=tuple(reads), writes=(key,))

    def finalize(self, ctx):
        nc = self.nc
        sems = {e: ctx.enter_context(nc.semaphore(f"sem_{e}"))
                for e in ENGINES}
        cum = {e: 0 for e in ENGINES}
        last_writer = {}
        readers = {}
        waits_for_op = []
        barrier_snap = None
        for opi in self.ops:
            if opi[0] == "BARRIER":
                barrier_snap = dict(cum)
                waits_for_op.append(None)
                continue
            eng, fn, reads, writes, inc = opi
            w = {}
            if barrier_snap:
                for e2, c2 in barrier_snap.items():
                    if c2 > 0 and (e2 != eng or eng != "pe") and c2 > w.get(e2, 0):
                        w[e2] = c2
            def need(e2, c2):
                if e2 != eng or eng != "pe":
                    if c2 > w.get(e2, 0):
                        w[e2] = c2
            for k in reads:
                lw = last_writer.get(k)
                if lw:
                    need(*lw)
            for k in writes:
                lw = last_writer.get(k)
                if lw:
                    need(*lw)
                for e2, c2 in readers.get(k, {}).items():
                    need(e2, c2)
            cum[eng] += inc
            for k in reads:
                readers.setdefault(k, {})[eng] = cum[eng]
            for k in writes:
                last_writer[k] = (eng, cum[eng])
                readers[k] = {}
            if eng in w:
                pass
            waits_for_op.append(w)
        totals = dict(cum)
        block = ctx.enter_context(nc.Block())
        for eng_name in ENGINES:
            idxs = [i for i, o in enumerate(self.ops) if o[0] == eng_name]
            if not idxs and eng_name != "pool":
                continue

            def emit(engine, idxs=idxs, eng_name=eng_name):
                satisfied = {e: 0 for e in ENGINES}
                own = 0
                for i in idxs:
                    _, fn, _, _, inc = self.ops[i]
                    if eng_name in DMA_QUEUES and own > 0 and satisfied.get(eng_name, 0) < own:
                        engine.wait_ge(sems[eng_name], own)
                        satisfied[eng_name] = own
                    for e2, v in sorted(waits_for_op[i].items()):
                        if satisfied.get(e2, 0) < v:
                            engine.wait_ge(sems[e2], v)
                            satisfied[e2] = v
                    try:
                        inst = fn(engine)
                    except Exception as ex:
                        raise RuntimeError(f"op #{i} eng={eng_name} tag={self.ops[i][2]}/{self.ops[i][3]}: {ex}") from ex
                    inst.then_inc(sems[eng_name], inc)
                    own += inc
                if eng_name == "pool":
                    engine.wait_ge(sems["pool"], totals["pool"])

            getattr(block, BLOCK_ATTR[eng_name])(emit)


class Launch:

    def __init__(self, name):
        self.nc = bass.Bass("TRN2", target_bir_lowering=False)
        self.rec = Rec(self.nc)
        self.ctx = ExitStack()
        self.ins = {}
        self.outs = {}
        self._sb = {}
        self._ps = {}
        self._scope = None
        self._prefix = ""

    def scope(self, prefix):
        from contextlib import contextmanager

        @contextmanager
        def _s():
            old_stack, old_pref = self._scope, self._prefix
            self._scope = ExitStack()
            self._prefix = prefix + "."
            try:
                yield
            finally:
                self.rec.barrier()
                self._scope.close()
                self._scope, self._prefix = old_stack, old_pref
        return _s()

    def dram_in(self, name, shape, dtype=F32):
        if name in self.ins:
            return self.ins[name]
        ap = self.nc.dram_tensor(name, list(shape), dtype, kind="ExternalInput").ap()
        self.ins[name] = ap
        return ap

    def dram_tmp(self, name, shape, dtype=F32):
        return self.nc.dram_tensor(name, list(shape), dtype, kind="Internal").ap()

    def dram_out(self, name, shape, dtype=F32):
        ap = self.nc.dram_tensor(name, list(shape), dtype, kind="ExternalOutput").ap()
        self.outs[name] = ap
        return ap

    def sb(self, name, shape, dtype):
        stack = self._scope if self._scope is not None else self.ctx
        t = stack.enter_context(
            self.nc.sbuf_tensor(self._prefix + name, list(shape), dtype))
        self._sb[name] = t
        return t

    def ps(self, name, shape, dtype=F32):
        stack = self._scope if self._scope is not None else self.ctx
        t = stack.enter_context(
            self.nc.psum_tensor(self._prefix + name, list(shape), dtype))
        self._ps[name] = t
        return t

    def run(self, in_maps, **kw):
        self.rec.finalize(self.ctx)
        res = run_bass_kernel_spmd(self.nc, in_maps, core_ids=list(range(NCORES)), **kw)
        self.ctx.close()
        return res.results



def emit_const_setup(L):
    r = L.rec
    ones_col = L.sb("ones_col", [128, 1], F32)
    ones_row = L.sb("ones_row", [1, 128], F32)
    eps_ln = L.sb("eps_ln", [1, 1], F32)
    r.dve("memset", ap=ones_col[:], constant=1.0, writes=("ones_col",))
    r.dve("memset", ap=ones_row[:], constant=1.0, writes=("ones_row",))
    r.dve("memset", ap=eps_ln[:], constant=1e-5, writes=("eps_ln",))
    return ones_col, ones_row, eps_ln


def emit_ln_cm(L, xs, xkey, ys, ykey, Tc, consts, nchunk=8, eps_key="eps_ln",
               gain=None, bias=None, gkey=(), psk=()):
    r = L.rec
    ones_col, ones_row, eps_ln = consts
    pstat_a = L._ps["pstat_a"]; pstat_b = L._ps["pstat_b"]; pbc = L._ps["pbc"]
    lnrow = L._sb["lnrow"]; sqt = L._sb["sqt"]
    if isinstance(xs, (list, tuple)):
        xin = lambda c: xs[c]
    else:
        xin = lambda c: xs[:, c, :Tc]
    for c in range(nchunk):
        r.act("activation", out=sqt[:, :Tc], in_=xin(c), func=AF.Square,
              reads=(xkey,), writes=("sqt",))
        r.pe("matmul", out=pstat_a[:, 0:Tc], lhsT=ones_col[:], rhs=xin(c),
             start=(c == 0), stop=(c == nchunk - 1),
             reads=("ones_col", xkey), writes=("pstat0",) + psk)
        r.pe("matmul", out=pstat_b[:, 0:Tc], lhsT=ones_col[:], rhs=sqt[:, :Tc],
             start=(c == 0), stop=(c == nchunk - 1),
             reads=("ones_col", "sqt"), writes=("pstat1",) + psk)
    r.dve("tensor_scalar_mul", out=lnrow[:, 0:Tc], in0=pstat_a[:, 0:Tc],
          scalar1=1.0 / (128 * nchunk), reads=("pstat0",) + psk, writes=("lnmu",))
    r.dve("tensor_scalar_mul", out=lnrow[:, 2 * Tc:3 * Tc], in0=pstat_b[:, 0:Tc],
          scalar1=1.0 / (128 * nchunk), reads=("pstat1",) + psk, writes=("lnex2",))
    r.dve("tensor_tensor", out=lnrow[:, 3 * Tc:4 * Tc], in0=lnrow[:, 0:Tc],
          in1=lnrow[:, 0:Tc], op=ALU.mult, reads=("lnmu",), writes=("lnmu2",))
    r.dve("tensor_tensor", out=lnrow[:, 2 * Tc:3 * Tc], in0=lnrow[:, 2 * Tc:3 * Tc],
          in1=lnrow[:, 3 * Tc:4 * Tc], op=ALU.subtract,
          reads=("lnex2", "lnmu2"), writes=("lnvar",))
    r.act("activation", out=lnrow[:, 3 * Tc:4 * Tc], in_=lnrow[:, 2 * Tc:3 * Tc],
          func=AF.Sqrt, bias=L._sb[eps_key][:], reads=("lnvar", eps_key), writes=("lnstd",))
    r.dve("reciprocal", out=lnrow[:, Tc:2 * Tc], in_=lnrow[:, 3 * Tc:4 * Tc],
          reads=("lnstd",), writes=("lnrstd",))
    r.pe("matmul", out=pbc[:, 0:Tc], lhsT=ones_row[:], rhs=lnrow[:, 0:Tc],
         start=True, stop=True, reads=("ones_row", "lnmu"), writes=("pbc0",) + psk)
    r.pe("matmul", out=pbc[:, Tc:2 * Tc], lhsT=ones_row[:], rhs=lnrow[:, Tc:2 * Tc],
         start=True, stop=True, reads=("ones_row", "lnrstd"), writes=("pbc1",) + psk)
    for c in range(nchunk):
        r.dve("tensor_tensor", out=sqt[:, :Tc], in0=xin(c), in1=pbc[:, 0:Tc],
              op=ALU.subtract, reads=(xkey, "pbc0") + psk, writes=("sqt",))
        if gain is None:
            r.dve("tensor_tensor", out=ys[:, c, :Tc], in0=sqt[:, :Tc],
                  in1=pbc[:, Tc:2 * Tc], op=ALU.mult,
                  reads=("sqt", "pbc1") + psk, writes=(ykey,))
        else:
            r.dve("tensor_tensor", out=sqt[:, :Tc], in0=sqt[:, :Tc],
                  in1=pbc[:, Tc:2 * Tc], op=ALU.mult,
                  reads=("sqt", "pbc1") + psk, writes=("sqt",))
            r.dve("tensor_scalar", out=ys[:, c, :Tc], in0=sqt[:, :Tc],
                  scalar1=gain[:, c, :], scalar2=bias[:, c, :],
                  op0=ALU.mult, op1=ALU.add,
                  reads=("sqt",) + tuple(gkey), writes=(ykey,))


def ensure_ln_scratch(L, Tc, views=None):
    if views is None:
        L.ps("pstat_a", [1, 512], F32)
        L.ps("pstat_b", [1, 512], F32)
        L.ps("pbc", [128, 2 * Tc], F32)
    else:
        L._ps["pstat_a"] = views["pstat_a"]
        L._ps["pstat_b"] = views["pstat_b"]
        L._ps["pbc"] = views["pbc"]
    L.sb("lnrow", [1, 4 * Tc], F32)
    L.sb("sqt", [128, Tc], F32)


def load_weight_cm(L, name, host_shape, nchunk=None, dtype=BF16):
    K, M = host_shape
    ap = L.dram_in(name, [K, M], dtype)
    if K < 128:
        t = L.sb("w_" + name, [K, 1, M], dtype)
        L.rec.load(t[:, 0, :], ap[:, :], "w_" + name)
        return t
    nchunk = K // 128
    t = L.sb("w_" + name, [128, nchunk, M], dtype)
    L.rec.load(t[:, :, :], ap.rearrange("(c p) m -> p c m", p=128), "w_" + name)
    return t


def load_bias_col(L, name, K, dtype=F32):
    nchunk = K // 128
    ap = L.dram_in(name, [128, nchunk], dtype)
    t = L.sb("b_" + name, [128, nchunk, 1], dtype)
    L.rec.load(t[:, :, 0], ap[:, :], "b_" + name)
    return t


def host_bias(v):
    v = np.ascontiguousarray(v).astype(np.float32)
    return np.ascontiguousarray(v.reshape(-1, 128).T)



def build_l0ffn(L, x, xo):
    Tc = 256
    nT = T // Tc
    r = L.rec

    wU1 = load_weight_cm(L, "U1", (1024, 256))
    wV1 = load_weight_cm(L, "V1", (256, 4096))
    wU2 = load_weight_cm(L, "U2", (4096, 256))
    wV2 = load_weight_cm(L, "V2", (256, 1024))
    b1 = load_bias_col(L, "b1", 4096)
    b2 = load_bias_col(L, "b2", 1024)

    consts = emit_const_setup(L)
    ensure_ln_scratch(L, Tc)

    xs = L.sb("xs", [128, 8, Tc], F32)
    ys = L.sb("ys", [128, 8, Tc], BF16)
    Psb = L.sb("Psb", [128, 2, Tc], BF16)
    Hsb = L.sb("Hsb", [128, Tc], BF16)
    Qsb = L.sb("Qsb", [128, 2, Tc], BF16)
    pP = L.ps("pP", [128, 512], F32)
    pH = L.ps("pH", [128, 512], F32)
    pQ0 = L.ps("pQ0", [128, 512], F32)
    pQ1 = L.ps("pQ1", [128, 512], F32)
    pO = L.ps("pO", [128, 512], F32)
    pQm = [pQ0, pQ1]

    for b in range(B_PER_CORE):
        for t0 in range(nT):
            sl = slice(t0 * Tc, (t0 + 1) * Tc)
            r.load(xs[:, :, :], x[b, :, sl].rearrange("(c p) t -> p c t", p=128), "xs")
            emit_ln_cm(L, xs, "xs", ys, "ys", Tc, consts)
            for m in range(2):
                for c in range(8):
                    r.pe("matmul", out=pP[:, :Tc], lhsT=wU1[:, c, m * 128:(m + 1) * 128],
                         rhs=ys[:, c, :], start=(c == 0), stop=(c == 7),
                         reads=("w_U1", "ys"), writes=("pP",))
                r.act("activation", out=Psb[:, m, :], in_=pP[:, :Tc], func=AF.Copy,
                      reads=("pP",), writes=("Psb",))
            for ht in range(32):
                for c in range(2):
                    r.pe("matmul", out=pH[:, :Tc], lhsT=wV1[:, c, ht * 128:(ht + 1) * 128],
                         rhs=Psb[:, c, :], start=(c == 0), stop=(c == 1),
                         reads=("w_V1", "Psb"), writes=("pH",))
                r.act("activation", out=Hsb[:], in_=pH[:, :Tc], func=AF.Gelu_apprx_tanh,
                      bias=b1[:, ht, :], reads=("pH", "b_b1"), writes=("Hsb",))
                for m in range(2):
                    r.pe("matmul", out=pQm[m][:, :Tc], lhsT=wU2[:, ht, m * 128:(m + 1) * 128],
                         rhs=Hsb[:], start=(ht == 0), stop=(ht == 31),
                         reads=("w_U2", "Hsb"), writes=("pQ%d" % m,))
            for m in range(2):
                r.act("activation", out=Qsb[:, m, :], in_=pQm[m][:, :Tc], func=AF.Copy,
                      reads=("pQ%d" % m,), writes=("Qsb",))
            for mo in range(8):
                for c in range(2):
                    r.pe("matmul", out=pO[:, :Tc], lhsT=wV2[:, c, mo * 128:(mo + 1) * 128],
                         rhs=Qsb[:, c, :], start=(c == 0), stop=(c == 1),
                         reads=("w_V2", "Qsb"), writes=("pO",))
                r.dve("scalar_tensor_tensor", out=xs[:, mo, :], in0=pO[:, :Tc],
                      scalar=b2[:, mo, :], in1=xs[:, mo, :],
                      op0=ALU.add, op1=ALU.add,
                      reads=("pO", "b_b2", "xs"), writes=("xs",))
            r.store(xo[b, :, sl].rearrange("(c p) t -> p c t", p=128), xs[:, :, :],
                    reads=("xs",))
    return L


def host_prep_l0ffn(inp):
    w = {}
    w["U1"] = np.ascontiguousarray(inp["L0_ffn_U1"]).astype(BF)
    w["V1"] = np.ascontiguousarray(inp["L0_ffn_V1"]).astype(BF)
    w["U2"] = np.ascontiguousarray(inp["L0_ffn_U2"]).astype(BF)
    w["V2"] = np.ascontiguousarray(inp["L0_ffn_V2"]).astype(BF)
    w["b1"] = host_bias(inp["L0_ffn_b1"])
    w["b2"] = host_bias(inp["L0_ffn_b2"])
    return w


def launch_l0ffn(x_shards, inp):
    L = Launch("l0ffn")
    x = L.dram_in("x", [B_PER_CORE, D, T], F32)
    xo = L.dram_out("xo", [B_PER_CORE, D, T], F32)
    build_l0ffn(L, x, xo)
    w = host_prep_l0ffn(inp)
    in_maps = [{"x": x_shards[c], **w} for c in range(NCORES)]
    res = L.run(in_maps)
    return [res[c]["xo"] for c in range(NCORES)]



def build_gla(L, x, xo):
    r = L.rec
    QT = 512
    NC = 4
    wQ = load_weight_cm(L, "Wq", (D, D))
    wK = load_weight_cm(L, "Wk", (D, D))
    wV = load_weight_cm(L, "Wv", (D, D))
    wO = load_weight_cm(L, "Wo", (D, D))
    wF1 = load_weight_cm(L, "Wf1", (D, 64))
    wF2 = load_weight_cm(L, "Wf2", (64, D))
    wG1 = load_weight_cm(L, "Wg1", (D, 64))
    wG2 = load_weight_cm(L, "Wg2", (64, D))
    nw = load_bias_col(L, "nw", D)
    tri = L.dram_in("tri", [128, 128], F32)
    e2f = L.dram_in("e2f", [128, 2], F32)
    f2f = L.dram_in("f2f", [2, 128], F32)

    consts = emit_const_setup(L)
    ones_col, ones_row, eps_ln = consts

    xs = L.sb("xs", [128, 8, QT], F32)
    ys = L.sb("ys", [128, 8, QT], BF16)
    qt = L.sb("qt", [128, 8, QT], BF16)
    kt = L.sb("kt", [128, 8, QT], BF16)
    vt = L.sb("vt", [128, NC, D], BF16)
    gg = L.sb("gg", [128, 8, QT], F32)
    ogb = L.sb("ogb", [128, 8, QT], BF16)
    Gf = L.sb("Gf", [128, 8, QT], F32)
    logf = L.sb("logf", [128, QT], F32)
    scr = L.sb("scr", [128, QT], F32)
    scr2 = L.sb("scr2", [128, QT], F32)
    Af = L.sb("Af", [128, QT], F32)
    Ai = L.sb("Ai", [128, QT], F32)
    zf1 = L.sb("zf1", [64, QT], BF16)
    Ssb = L.sb("Ssb", [128, 8, 64], F32)
    Sbf = L.sb("Sbf", [128, 8, 64], BF16)
    Glast = L.sb("Glast", [128, 8, 1], F32)
    Glast2 = L.sb("Glast2", [128, 8, 1], F32)
    eps2 = L.sb("eps2", [2, 1], F32)
    ecol = L.sb("ecol", [128, 8, 2], F32)
    probs = L.sb("probs", [128, 2, 128], BF16)
    ktok = L.sb("ktok", [128, 2, 64], BF16)
    trim = L.sb("trim", [128, 128], F32)
    e2s = L.sb("e2s", [128, 2], F32)
    f2s = L.sb("f2s", [2, 128], F32)
    ident = L.sb("ident", [128, 128], BF16)
    ln8 = L.sb("ln8", [128, 1], F32)
    one1 = L.sb("one1", [128, 1], F32)
    rr = L.sb("rr", [2, QT], F32)
    identF = L.dram_in("identF", [128, 128], BF16)

    psAll = L.ps("psAll", [128, 3584], F32)
    pTr = L.ps("pTr", [128, 1024], BF16)
    ensure_ln_scratch(L, QT, views={
        "pstat_a": psAll[0:1, 0:512],
        "pstat_b": psAll[0:1, 512:1024],
        "pbc": psAll[:, 1024:2048],
    })
    pA = psAll[:, 2048:2560]
    pV2 = psAll[:, 2048:3072]
    pS0 = psAll[:, 0:128]
    pS1 = psAll[:, 512:640]
    pJ0 = psAll[:, 1024:1152]
    pJ1 = psAll[:, 1536:1664]
    pSt = psAll[:, 2048:2112]
    pSt2 = psAll[:, 2304:2368]
    pR = psAll[0:2, 2560:3072]
    pB = psAll[:, 3072:3584]
    pO = psAll[:, 1024:1536]

    r.load(trim[:], tri[:, :], "trim")
    r.load(e2s[:], e2f[:, :], "e2s")
    r.load(f2s[:], f2f[:, :], "f2s")
    r.load(ident[:], identF[:, :], "ident")
    r.dve("memset", ap=ln8[:], constant=-2.0794415416798357, writes=("ln8",))
    r.dve("memset", ap=one1[:], constant=1.0, writes=("one1",))

    for b in range(B_PER_CORE):
        r.dve("memset", ap=Ssb[:], constant=0.0, writes=("Ssb",))
        r.dve("memset", ap=Glast[:], constant=0.0, writes=("Glast",))
        r.dve("memset", ap=Glast2[:], constant=0.0, writes=("Glast2",))
        r.dve("memset", ap=eps2[:], constant=1e-5, writes=("eps2",))
        for qtr in range(T // QT):
            sl = slice(qtr * QT, (qtr + 1) * QT)
            r.load(xs[:, :, :], x[b, :, sl].rearrange("(c p) t -> p c t", p=128), "xs")
            emit_ln_cm(L, xs, "xs", ys, "ys", QT, consts, psk=("PS",))
            for c in range(8):
                r.pe("matmul", out=pA[:64, 0:QT], lhsT=wF1[:, c, :], rhs=ys[:, c, :],
                     start=(c == 0), stop=(c == 7), reads=("w_Wf1", "ys"),
                     writes=("PS",))
            r.act("activation", out=zf1[:], in_=pA[0:64, 0:QT], func=AF.Copy,
                  reads=("PS",), writes=("zf1",))
            for m in range(8):
                r.pe("matmul", out=pA[:, 0:QT], lhsT=wF2[0:64, 0, m * 128:(m + 1) * 128],
                     rhs=zf1[:], start=True, stop=True,
                     reads=("w_Wf2", "zf1"), writes=("PS",))
                r.act("activation", out=scr[:], in_=pA[:, 0:QT], func=AF.Abs,
                      reads=("PS",), writes=("scr",))
                r.act("activation", out=scr[:], in_=scr[:], func=AF.Exp, scale=-1.0,
                      reads=("scr",), writes=("scr",))
                r.act("activation", out=scr[:], in_=scr[:], func=AF.Ln, bias=one1[:],
                      reads=("scr", "one1"), writes=("scr",))
                r.act("activation", out=scr2[:], in_=pA[:, 0:QT], func=AF.Relu,
                      scale=-1.0, reads=("PS",), writes=("scr2",))
                r.dve("tensor_tensor", out=scr[:], in0=scr[:], in1=scr2[:],
                      op=ALU.add, reads=("scr", "scr2"), writes=("scr",))
                r.dve("tensor_scalar_mul", out=logf[:], in0=scr[:], scalar1=-1.0,
                      reads=("scr",), writes=("logf",))
                r.dve("tensor_tensor_scan", out=Gf[:, m, :],
                      data0=one1[:].broadcast_to((128, QT)), data1=logf[:],
                      initial=Glast[:, m, :], op0=ALU.mult, op1=ALU.add,
                      reads=("logf", "Glast", "one1"), writes=("Gf",))
                r.dve("tensor_copy", out=Glast[:, m, :], in_=Gf[:, m, QT - 1:QT],
                      reads=("Gf",), writes=("Glast",))
            for c in range(8):
                r.pe("matmul", out=pA[:64, 0:QT], lhsT=wG1[:, c, :], rhs=ys[:, c, :],
                     start=(c == 0), stop=(c == 7), reads=("w_Wg1", "ys"),
                     writes=("PS",))
            r.act("activation", out=zf1[:], in_=pA[0:64, 0:QT], func=AF.Copy,
                  reads=("PS",), writes=("zf1",))
            for m in range(8):
                r.pe("matmul", out=pA[:, 0:QT], lhsT=wG2[0:64, 0, m * 128:(m + 1) * 128],
                     rhs=zf1[:], start=True, stop=True,
                     reads=("w_Wg2", "zf1"), writes=("PS",))
                r.act("activation", out=gg[:, m, :], in_=pA[:, 0:QT], func=AF.Sigmoid,
                      reads=("PS",), writes=("gg",))
            for m in range(8):
                for cs in range(NC):
                    csl = slice(cs * 128, (cs + 1) * 128)
                    gcol = Gf[:, m, cs * 128 + 63:cs * 128 + 64]
                    r.dve("tensor_scalar", out=scr[:, csl], in0=Gf[:, m, csl],
                          scalar1=gcol, scalar2=None, op0=ALU.subtract,
                          reads=("Gf",), writes=("scr",))
                r.act("activation", out=Af[:], in_=scr[:], func=AF.Exp, bias=ln8[:],
                      reads=("scr", "ln8"), writes=("Af",))
                r.act("activation", out=Ai[:], in_=scr[:], func=AF.Exp, scale=-1.0,
                      reads=("scr",), writes=("Ai",))
                for c in range(8):
                    r.pe("matmul", out=pA[:, 0:QT], lhsT=wQ[:, c, m * 128:(m + 1) * 128],
                         rhs=ys[:, c, :], start=(c == 0), stop=(c == 7),
                         reads=("w_Wq", "ys"), writes=("PS",))
                r.act("activation", out=scr2[:], in_=pA[:, 0:QT], func=AF.Silu,
                      reads=("PS",), writes=("scr2",))
                r.dve("tensor_tensor", out=qt[:, m, :], in0=scr2[:], in1=Af[:],
                      op=ALU.mult, reads=("scr2", "Af"), writes=("qt",))
                for c in range(8):
                    r.pe("matmul", out=pA[:, 0:QT], lhsT=wK[:, c, m * 128:(m + 1) * 128],
                         rhs=ys[:, c, :], start=(c == 0), stop=(c == 7),
                         reads=("w_Wk", "ys"), writes=("PS",))
                r.act("activation", out=scr2[:], in_=pA[:, 0:QT], func=AF.Silu,
                      reads=("PS",), writes=("scr2",))
                r.dve("tensor_tensor", out=kt[:, m, :], in0=scr2[:], in1=Ai[:],
                      op=ALU.mult, reads=("scr2", "Ai"), writes=("kt",))
            for tc in range(NC):
                tsl = slice(tc * 128, (tc + 1) * 128)
                for half in range(2):
                    vsl = slice(half * 512, (half + 1) * 512)
                    for c in range(8):
                        r.pe("matmul", out=pV2[:, half * 512:half * 512 + 512],
                             lhsT=ys[:, c, tsl], rhs=wV[:, c, vsl],
                             start=(c == 0), stop=(c == 7),
                             reads=("ys", "w_Wv"), writes=("PS",))
                    r.act("activation", out=vt[:, tc, vsl], in_=pV2[:, half * 512:half * 512 + 512],
                          func=AF.Copy, reads=("PS",), writes=("vt",))
            for cs in range(NC):
                isl = slice(cs * 128, (cs + 1) * 128)
                for m in range(8):
                    gcol = Gf[:, m, cs * 128 + 63:cs * 128 + 64]
                    if cs == 0 and qtr == 0:
                        r.dve("tensor_copy", out=ecol[:, m, 0:1], in_=gcol,
                              reads=("Gf",), writes=("ecol",))
                    else:
                        pcol = (Gf[:, m, cs * 128 - 1:cs * 128] if cs > 0
                                else Glast2[:, m, :])
                        r.dve("tensor_tensor", out=ecol[:, m, 0:1], in0=gcol,
                              in1=pcol, op=ALU.subtract,
                              reads=("Gf", "Glast2"), writes=("ecol",))
                    r.dve("tensor_tensor", out=ecol[:, m, 1:2],
                          in0=Gf[:, m, cs * 128 + 127:cs * 128 + 128], in1=gcol,
                          op=ALU.subtract, reads=("Gf",), writes=("ecol",))
                    r.act("activation", out=ecol[:, m, :], in_=ecol[:, m, :],
                          func=AF.Exp, reads=("ecol",), writes=("ecol",))
                    r.dve("tensor_scalar_mul", out=Ssb[:, m, :], in0=Ssb[:, m, :],
                          scalar1=ecol[:, m, 0:1], reads=("Ssb", "ecol"),
                          writes=("Ssb",))
                    r.act("activation", out=Sbf[:, m, :], in_=Ssb[:, m, :],
                          func=AF.Copy, reads=("Ssb",), writes=("Sbf",))
                    r.pe("matmul", out=pS0[:, :], lhsT=kt[0:64, m, isl],
                         rhs=qt[0:64, m, isl], start=True, stop=True,
                         tile_position=(0, 0), reads=("kt", "qt"), writes=("PS",))
                    r.pe("matmul", out=pS1[:, :], lhsT=kt[64:128, m, isl],
                         rhs=qt[64:128, m, isl], start=True, stop=True,
                         tile_position=(64, 0), reads=("kt", "qt"), writes=("PS",))
                    r.dve("tensor_tensor", out=probs[:, 0, :], in0=pS0[:, :],
                          in1=trim[:], op=ALU.mult, reads=("PS", "trim"),
                          writes=("probs",))
                    r.dve("tensor_tensor", out=probs[:, 1, :], in0=pS1[:, :],
                          in1=trim[:], op=ALU.mult, reads=("PS", "trim"),
                          writes=("probs",))
                    r.pe("matmul", out=pJ0[0:64, :], lhsT=vt[:, cs, m * 128:m * 128 + 64],
                         rhs=probs[:, 0, :], start=True, stop=False,
                         reads=("vt", "probs"), writes=("PS",))
                    r.pe("matmul", out=pJ0[0:64, :], lhsT=Sbf[0:64, m, :],
                         rhs=qt[0:64, m, isl], start=False, stop=True,
                         tile_position=(0, 0), reads=("Sbf", "qt"), writes=("PS",))
                    r.pe("matmul", out=pJ1[0:64, :], lhsT=vt[:, cs, m * 128 + 64:m * 128 + 128],
                         rhs=probs[:, 1, :], start=True, stop=False,
                         reads=("vt", "probs"), writes=("PS",))
                    r.pe("matmul", out=pJ1[0:64, :], lhsT=Sbf[64:128, m, :],
                         rhs=qt[64:128, m, isl], start=False, stop=True,
                         tile_position=(64, 0), reads=("Sbf", "qt"), writes=("PS",))
                    r.dve("tensor_tensor", out=gg[0:64, m, isl], in0=pJ0[0:64, :],
                          in1=gg[0:64, m, isl], op=ALU.mult,
                          reads=("PS", "gg"), writes=("gg",))
                    r.dve("tensor_tensor", out=gg[64:128, m, isl], in0=pJ1[0:64, :],
                          in1=gg[64:128, m, isl], op=ALU.mult,
                          reads=("PS", "gg"), writes=("gg",))
                    r.pe("transpose", out=pTr[:, 0:64], in_=kt[0:64, m, isl],
                         identity=ident[0:64, 0:64], reads=("kt", "ident"), writes=("PS",))
                    r.act("activation", out=ktok[:, 0, :], in_=pTr[:, 0:64],
                          func=AF.Copy, reads=("PS",), writes=("ktok",))
                    r.pe("transpose", out=pTr[:, 512:576], in_=kt[64:128, m, isl],
                         identity=ident[64:128, 64:128], reads=("kt", "ident"), writes=("PS",))
                    r.act("activation", out=ktok[:, 1, :], in_=pTr[:, 512:576],
                          func=AF.Copy, reads=("PS",), writes=("ktok",))
                    r.pe("matmul", out=pSt[0:64, :], lhsT=ktok[:, 0, :],
                         rhs=vt[:, cs, m * 128:m * 128 + 64], start=True, stop=True,
                         reads=("ktok", "vt"), writes=("PS",))
                    r.pe("matmul", out=pSt2[0:64, :], lhsT=ktok[:, 1, :],
                         rhs=vt[:, cs, m * 128 + 64:m * 128 + 128], start=True,
                         stop=True, reads=("ktok", "vt"), writes=("PS",))
                    r.dve("tensor_tensor", out=Ssb[0:64, m, :], in0=Ssb[0:64, m, :],
                          in1=pSt[0:64, :], op=ALU.add, reads=("Ssb", "PS"),
                          writes=("Ssb",))
                    r.dve("tensor_tensor", out=Ssb[64:128, m, :], in0=Ssb[64:128, m, :],
                          in1=pSt2[0:64, :], op=ALU.add, reads=("Ssb", "PS"),
                          writes=("Ssb",))
                    r.dve("tensor_scalar_mul", out=Ssb[:, m, :], in0=Ssb[:, m, :],
                          scalar1=ecol[:, m, 1:2], reads=("Ssb", "ecol"),
                          writes=("Ssb",))
            r.dve("tensor_copy", out=Glast2[:, :, 0:1], in_=Gf[:, :, QT - 1:QT],
                  reads=("Gf",), writes=("Glast2",))
            for m in range(8):
                r.act("activation", out=scr[:], in_=gg[:, m, :], func=AF.Square,
                      reads=("gg",), writes=("scr",))
                r.pe("matmul", out=pR[:, 0:QT], lhsT=e2s[:], rhs=scr[:],
                     start=True, stop=True, reads=("e2s", "scr"), writes=("PS",))
                r.dve("tensor_scalar", out=rr[:], in0=pR[:, 0:QT],
                      scalar1=1.0 / 64, scalar2=None, op0=ALU.mult,
                      reads=("PS",), writes=("rr",))
                r.act("activation", out=rr[:], in_=rr[:], func=AF.Sqrt,
                      bias=eps2[:], reads=("rr", "eps2"), writes=("rr",))
                r.dve("reciprocal", out=rr[:], in_=rr[:], reads=("rr",),
                      writes=("rr",))
                r.pe("matmul", out=pB[:, 0:QT], lhsT=f2s[:], rhs=rr[:],
                     start=True, stop=True, reads=("f2s", "rr"), writes=("PS",))
                r.dve("tensor_tensor", out=scr[:], in0=gg[:, m, :], in1=pB[:, 0:QT],
                      op=ALU.mult, reads=("gg", "PS"), writes=("scr",))
                r.dve("tensor_scalar_mul", out=ogb[:, m, :], in0=scr[:],
                      scalar1=nw[:, m, :], reads=("scr", "b_nw"), writes=("ogb",))
            for mo in range(8):
                for c in range(8):
                    r.pe("matmul", out=pO[:, 0:QT], lhsT=wO[:, c, mo * 128:(mo + 1) * 128],
                         rhs=ogb[:, c, :], start=(c == 0), stop=(c == 7),
                         reads=("w_Wo", "ogb"), writes=("PS",))
                r.dve("tensor_tensor", out=xs[:, mo, :], in0=xs[:, mo, :],
                      in1=pO[:, 0:QT], op=ALU.add, reads=("xs", "PS"),
                      writes=("xs",))
            r.store(xo[b, :, sl].rearrange("(c p) t -> p c t", p=128), xs[:, :, :],
                    reads=("xs",))
    return L


def host_prep_gla(inp):
    w = {}
    for nm, key in (("Wq", "L0_attn_a_Wq"), ("Wk", "L0_attn_a_Wk"),
                    ("Wv", "L0_attn_a_Wv"), ("Wo", "L0_attn_a_Wo"),
                    ("Wf1", "L0_attn_a_Wf1"), ("Wf2", "L0_attn_a_Wf2"),
                    ("Wg1", "L0_attn_a_Wg1"), ("Wg2", "L0_attn_a_Wg2")):
        w[nm] = np.ascontiguousarray(inp[key]).astype(BF)
    w["nw"] = host_bias(inp["L0_attn_a_norm_weight"])
    jj, ii = np.meshgrid(np.arange(128), np.arange(128), indexing="ij")
    w["tri"] = (jj <= ii).astype(np.float32)
    e2 = np.zeros((128, 2), np.float32)
    e2[0:64, 0] = 1.0
    e2[64:128, 1] = 1.0
    w["e2f"] = e2
    w["f2f"] = np.ascontiguousarray(e2.T)
    w["identF"] = np.eye(128).astype(BF)
    return w


def launch_gla(x_shards, inp):
    L = Launch("gla")
    x = L.dram_in("x", [B_PER_CORE, D, T], F32)
    xo = L.dram_out("xo", [B_PER_CORE, D, T], F32)
    build_gla(L, x, xo)
    w = host_prep_gla(inp)
    in_maps = [{"x": x_shards[c], **w} for c in range(NCORES)]
    res = L.run(in_maps)
    return [res[c]["xo"] for c in range(NCORES)]



def build_attnb(L, x, xo):
    r = L.rec
    wq = L.dram_in("WqT", [D, D], BF16)
    wk = L.dram_in("WkT", [D, D], BF16)
    wv = L.dram_in("WvT", [D, D], BF16)
    wo = L.dram_in("WoT", [D, D], BF16)
    gband = L.dram_in("gband", [16, 128, 256], F32)
    lam = L.dram_in("lam", [1, 1], F32)
    bq = load_bias_col(L, "bq", D)
    bk = load_bias_col(L, "bk", D)
    bv_row = L.dram_in("bv_row", [1, D], BF16)
    bo = load_bias_col(L, "bo", D)

    consts = emit_const_setup(L)
    ones_col, ones_row, eps_ln = consts

    xs = L.sb("xs", [128, 8, T], F32)
    ys = L.sb("ys", [128, 8, T], BF16)
    Gb = L.sb("Gb", [128, 16, 256], F32)
    lam1 = L.sb("lam1", [1, 1], F32)
    lam128 = L.sb("lam128", [128, 1], F32)
    bs40 = L.sb("bs40", [128, 1], F32)
    bq8 = L.sb("bq8", [128, 8, 1], F32)
    wqkv = L.sb("wqkv", [128, 8, 3, 128], BF16)
    wosb = L.sb("wosb", [128, 8, 128], BF16)
    bvr = L.sb("bvr", [1, D], BF16)
    onesbf = L.sb("onesbf", [1, 128], BF16)
    Qp = L.sb("Qp", [128, T], BF16)
    Kp = L.sb("Kp", [128, T], BF16)
    Vt = L.sb("Vt", [128, 16, 130], BF16)
    probsT = L.sb("probsT", [128, 2, 16, 128], BF16)
    r64 = L.sb("r64", [1, 128], F32)
    nbc = L.sb("nbc", [64, 128], F32)
    ctxu = L.sb("ctxu", [128, 8, T], BF16)

    psAll = L.ps("psAll", [128, 4096], F32)
    pSc = psAll[:].rearrange("p (h kt q) -> p h kt q", h=2, kt=16)
    pAV0 = psAll[:, 0:128]
    pAV1 = psAll[:, 512:640]
    pBC = psAll[:, 1024:1152]
    ensure_ln_scratch(L, 512, views={
        "pstat_a": psAll[0:1, 0:512],
        "pstat_b": psAll[0:1, 512:1024],
        "pbc": psAll[:, 1024:2048],
    })

    r.load(lam1[:], lam[:, :], "lam1")
    r.load(bvr[:], bv_row[:, :], "bvr")
    r.dve("memset", ap=bs40[:], constant=-40.0, writes=("bs40",))
    r.dve("memset", ap=onesbf[:], constant=1.0, writes=("onesbf",))
    r.dve("memset", ap=Vt[:, :, 64:65], constant=1.0, writes=("Vt",))
    r.dve("memset", ap=Vt[:, :, 129:130], constant=1.0, writes=("Vt",))
    r.load(Gb[:, :, :], gband.rearrange("kt p m -> p kt m"), "Gb_raw")
    r.pe("matmul", out=psAll[0:128, 0:1], lhsT=ones_row[:], rhs=lam1[:],
         start=True, stop=True, reads=("ones_row", "lam1"), writes=("PS",))
    r.act("activation", out=lam128[:], in_=psAll[0:128, 0:1], func=AF.Copy,
          reads=("PS",), writes=("lam128",))
    for kt in range(16):
        r.dve("tensor_scalar_mul", out=Gb[:, kt, :], in0=Gb[:, kt, :],
              scalar1=lam128[:], reads=("Gb_raw", "lam128"), writes=("Gb",))
    for c in range(8):
        r.dve("tensor_scalar_mul", out=bq8[:, c, :], in0=L._sb["b_bq"][:, c, :],
              scalar1=0.125, reads=("b_bq",), writes=("bq8",))

    for b in range(B_PER_CORE):
        for t0 in range(4):
            sl = slice(t0 * 512, (t0 + 1) * 512)
            r.load(xs[:, :, sl], x[b, :, sl].rearrange("(c p) t -> p c t", p=128), "xs")
            emit_ln_cm(L, xs[:, :, sl], "xs", ys[:, :, sl], "ys", 512, consts,
                       psk=("PS",))
        for hp in range(8):
            hsl = slice(hp * 128, (hp + 1) * 128)
            r.load(wqkv[:, :, 0, :], wq[:, hsl].rearrange("(c p) m -> p c m", p=128), "wqkv")
            r.load(wqkv[:, :, 1, :], wk[:, hsl].rearrange("(c p) m -> p c m", p=128), "wqkv")
            r.load(wqkv[:, :, 2, :], wv[:, hsl].rearrange("(c p) m -> p c m", p=128), "wqkv")
            for t0 in range(4):
                sl = slice(t0 * 512, (t0 + 1) * 512)
                for c in range(8):
                    r.pe("matmul", out=psAll[:, 2048:2560], lhsT=wqkv[:, c, 0, :],
                         rhs=ys[:, c, sl], start=(c == 0), stop=(c == 7),
                         reads=("wqkv", "ys"), writes=("PS",))
                r.dve("tensor_scalar", out=Qp[:, sl], in0=psAll[:, 2048:2560],
                      scalar1=0.125, scalar2=bq8[:, hp, :],
                      op0=ALU.mult, op1=ALU.add,
                      reads=("PS", "bq8"), writes=("Qp",))
                for c in range(8):
                    r.pe("matmul", out=psAll[:, 2560:3072], lhsT=wqkv[:, c, 1, :],
                         rhs=ys[:, c, sl], start=(c == 0), stop=(c == 7),
                         reads=("wqkv", "ys"), writes=("PS",))
                r.dve("tensor_scalar_add", out=Kp[:, sl], in0=psAll[:, 2560:3072],
                      scalar1=L._sb["b_bk"][:, hp, :],
                      reads=("PS", "b_bk"), writes=("Kp",))
            for kt in range(16):
                tsl = slice(kt * 128, (kt + 1) * 128)
                for c in range(8):
                    r.pe("matmul", out=psAll[:, 3072:3200], lhsT=ys[:, c, tsl],
                         rhs=wqkv[:, c, 2, :], start=(c == 0), stop=False,
                         reads=("ys", "wqkv"), writes=("PS",))
                r.pe("matmul", out=psAll[:, 3072:3200], lhsT=onesbf[:, 0:128],
                     rhs=bvr[:, hsl], start=False, stop=True,
                     reads=("onesbf", "bvr"), writes=("PS",))
                r.act("activation", out=Vt[:, kt, 0:64], in_=psAll[:, 3072:3136],
                      func=AF.Copy, reads=("PS",), writes=("Vt",))
                r.act("activation", out=Vt[:, kt, 65:129], in_=psAll[:, 3136:3200],
                      func=AF.Copy, reads=("PS",), writes=("Vt",))
            for qc in range(16):
                qsl = slice(qc * 128, (qc + 1) * 128)
                for kt in range(16):
                    tsl = slice(kt * 128, (kt + 1) * 128)
                    r.pe("matmul", out=pSc[:, 0, kt, :], lhsT=Kp[0:64, tsl],
                         rhs=Qp[0:64, qsl], start=True, stop=True,
                         tile_position=(0, 0),
                         reads=("Kp", "Qp"), writes=("PS",))
                    r.pe("matmul", out=pSc[:, 1, kt, :], lhsT=Kp[64:128, tsl],
                         rhs=Qp[64:128, qsl], start=True, stop=True,
                         tile_position=(64, 0),
                         reads=("Kp", "Qp"), writes=("PS",))
                for kt, gsl, ssl in (
                    (qc - 1, slice(192, 256), slice(0, 64)),
                    (qc, slice(64, 192), slice(0, 128)),
                    (qc + 1, slice(0, 64), slice(64, 128)),
                ):
                    if 0 <= kt < 16:
                        for h in range(2):
                            r.dve("tensor_tensor", out=pSc[:, h, kt, ssl],
                                  in0=pSc[:, h, kt, ssl], in1=Gb[:, kt, gsl],
                                  op=ALU.add, reads=("PS", "Gb"), writes=("PS",))
                r.act("activation", out=probsT[:], in_=psAll[:], func=AF.Exp,
                      bias=bs40[:], reads=("PS", "bs40"), writes=("probsT",))
                for h, pAV in ((0, pAV0), (1, pAV1)):
                    for kt in range(16):
                        r.pe("matmul", out=pAV[0:65, :],
                             lhsT=Vt[:, kt, h * 65:h * 65 + 65],
                             rhs=probsT[:, h, kt, :], start=(kt == 0), stop=(kt == 15),
                             reads=("Vt", "probsT"), writes=("PS",))
                    r.dve("reciprocal", out=r64[:], in_=pAV[64:65, :],
                          reads=("PS",), writes=("r64",))
                    r.pe("matmul", out=pBC[0:64, :], lhsT=ones_row[:, 0:64],
                         rhs=r64[:], start=True, stop=True,
                         reads=("ones_row", "r64"), writes=("PS",))
                    r.act("activation", out=nbc[:], in_=pBC[0:64, :], func=AF.Copy,
                          reads=("PS",), writes=("nbc",))
                    r.dve("tensor_tensor", out=ctxu[h * 64:(h + 1) * 64, hp, qsl],
                          in0=pAV[0:64, :], in1=nbc[:], op=ALU.mult,
                          reads=("PS", "nbc"), writes=("ctxu",))
        for mo in range(8):
            msl = slice(mo * 128, (mo + 1) * 128)
            r.load(wosb[:, :, :], wo[:, msl].rearrange("(c p) m -> p c m", p=128), "wosb")
            for t0 in range(4):
                sl = slice(t0 * 512, (t0 + 1) * 512)
                for c in range(8):
                    r.pe("matmul", out=psAll[:, 3584:4096], lhsT=wosb[:, c, :],
                         rhs=ctxu[:, c, sl], start=(c == 0), stop=(c == 7),
                         reads=("wosb", "ctxu"), writes=("PS",))
                r.dve("scalar_tensor_tensor", out=xs[:, mo, sl],
                      in0=psAll[:, 3584:4096], scalar=L._sb["b_bo"][:, mo, :],
                      in1=xs[:, mo, sl], op0=ALU.add, op1=ALU.add,
                      reads=("PS", "b_bo", "xs"), writes=("xs",))
        r.store(xo[b, :, :].rearrange("(c p) t -> p c t", p=128), xs[:, :, :],
                reads=("xs",))
    return L


def host_band_attnb():
    idx = np.arange(T, dtype=np.float64)
    g = np.zeros((16, 128, 256), np.float32)
    for kt in range(16):
        kk = 128 * kt + np.arange(128)
        qq = 128 * kt - 64 + np.arange(256)
        d = np.abs(kk[:, None] - qq[None, :]).astype(np.float64)
        m = 0.8 * np.exp(-((d - 1.0) ** 2) / 0.5) + 0.4 * np.exp(-((d - 30.0) ** 2) / 8.0)
        m[d > 64] = 0.0
        m[:, (qq < 0) | (qq >= T)] = 0.0
        g[kt] = m.astype(np.float32)
    return g


def host_prep_attnb(inp):
    w = {}
    w["WqT"] = np.ascontiguousarray(inp["L0_attn_b_Wq"].T).astype(BF)
    w["WkT"] = np.ascontiguousarray(inp["L0_attn_b_Wk"].T).astype(BF)
    w["WvT"] = np.ascontiguousarray(inp["L0_attn_b_Wv"].T).astype(BF)
    w["WoT"] = np.ascontiguousarray(inp["L0_attn_b_Wo"].T).astype(BF)
    w["gband"] = host_band_attnb()
    w["lam"] = np.asarray(inp["L0_attn_b_prior_lambda"], np.float32).reshape(1, 1)
    w["bq"] = host_bias(inp["L0_attn_b_bq"])
    w["bk"] = host_bias(inp["L0_attn_b_bk"])
    w["bv_row"] = np.ascontiguousarray(inp["L0_attn_b_bv"]).astype(BF).reshape(1, D)
    w["bo"] = host_bias(inp["L0_attn_b_bo"])
    return w


def launch_attnb(x_shards, inp):
    L = Launch("attnb")
    x = L.dram_in("x", [B_PER_CORE, D, T], F32)
    xo = L.dram_out("xo", [B_PER_CORE, D, T], F32)
    build_attnb(L, x, xo)
    w = host_prep_attnb(inp)
    in_maps = [{"x": x_shards[c], **w} for c in range(NCORES)]
    res = L.run(in_maps)
    return [res[c]["xo"] for c in range(NCORES)]



def build_rwkv(L, x, xo):
    r = L.rec
    HU = 4096
    NHT = 32
    tWk = L.dram_in("tWk", [D, HU], BF16)
    tWv = L.dram_in("tWv", [D, HU], BF16)
    tWr = L.dram_in("tWr", [D, HU], BF16)
    tWo = L.dram_in("tWo", [HU, D], BF16)
    cWk = L.dram_in("cWk", [D, HU], BF16)
    cWv = L.dram_in("cWv", [HU, D], BF16)
    cWr = L.dram_in("cWr", [D, D], BF16)
    ln0g = load_bias_col(L, "ln0g", D)
    ln0b = load_bias_col(L, "ln0b", D)
    preg = load_bias_col(L, "preg", D)
    preb = load_bias_col(L, "preb", D)
    postg = load_bias_col(L, "postg", D)
    postb = load_bias_col(L, "postb", D)
    dcay = L.dram_in("dcay", [128, 32], F32)
    frst = L.dram_in("frst", [128, 32], F32)
    mixk = load_bias_col(L, "mixk", D)
    mixv = load_bias_col(L, "mixv", D)
    mixr = load_bias_col(L, "mixr", D)
    cmixk = load_bias_col(L, "cmixk", D)
    cmixr = load_bias_col(L, "cmixr", D)

    consts = emit_const_setup(L)
    ones_col, ones_row, eps_ln = consts

    xs = L.sb("xs", [128, 8, 512], F32)
    xp = L.sb("xp", [128, 8, T], F32)
    xn = L.sb("xn", [128, 8, T + 1], BF16)
    wt = [L.sb(f"wt{i}", [128, 8, 3, 128], BF16) for i in range(2)]
    wA = [L.sb(f"wA{i}", [128, 8, 3, 128], BF16) for i in range(2)]
    wB = [L.sb(f"wB{i}", [128, 8, 3, 128], BF16) for i in range(2)]
    wOt = L.sb("wOt", [128, 1, 1024], BF16)
    ks = L.sb("ks", [128, T], F32)
    vs = L.sb("vs", [128, T], F32)
    srs = L.sb("srs", [128, T], F32)
    sck = L.sb("sck", [128, T + 1], F32)
    sckv = L.sb("sckv", [128, T + 1], F32)
    rwk = L.sb("rwk", [128, T], BF16)
    hb = rwk
    dcs = L.sb("dcs", [128, 32], F32)
    frs = L.sb("frs", [128, 32], F32)
    imixs = L.sb("imixs", [128, 8, 5], F32)
    mix_all = [mixk, mixv, mixr]

    psAll = L.ps("psAll", [128, 4096], F32)
    ensure_ln_scratch(L, 512, views={
        "pstat_a": psAll[0:1, 0:512],
        "pstat_b": psAll[0:1, 512:1024],
        "pbc": psAll[:, 1024:2048],
    })
    pP = psAll[:, 2048:2560]
    pW = psAll[:, 2560:3072]
    pR2 = psAll[:, 3072:3584]

    r.load(dcs[:], dcay[:, :], "dcs")
    r.load(frs[:], frst[:, :], "frs")
    r.act("activation", out=dcs[:], in_=dcs[:], func=AF.Exp, reads=("dcs",),
          writes=("dcs",))
    r.act("activation", out=dcs[:], in_=dcs[:], func=AF.Exp, scale=-1.0,
          reads=("dcs",), writes=("dcs",))
    r.act("activation", out=frs[:], in_=frs[:], func=AF.Exp, reads=("frs",),
          writes=("frs",))
    for i, mx in enumerate([mixk, mixv, mixr, cmixk, cmixr]):
        for c in range(8):
            r.dve("tensor_scalar", out=imixs[:, c, i:i + 1], in0=mx[:, c, :],
                  scalar1=-1.0, scalar2=1.0, op0=ALU.mult, op1=ALU.add,
                  reads=("b_mixk", "b_mixv", "b_mixr", "b_cmixk", "b_cmixr"),
                  writes=("imixs",))

    def dual_proj(dst_ps, wa, wb, msl, nsl, extra=("wA0", "wB0")):
        for c in range(8):
            r.pe("matmul", out=dst_ps, lhsT=wa[:, c, msl],
                 rhs=xn[:, c, 1 + nsl.start:1 + nsl.stop],
                 start=(c == 0), stop=False, reads=("xn",) + extra,
                 writes=())
        for c in range(8):
            r.pe("matmul", out=dst_ps, lhsT=wb[:, c, msl],
                 rhs=xn[:, c, nsl.start:nsl.stop],
                 start=False, stop=(c == 7), reads=("xn",) + extra,
                 writes=())

    for b in range(B_PER_CORE):
        r.dve("memset", ap=xn[:, :, 0:1], constant=0.0, writes=("xn",))
        r.dve("memset", ap=sck[:, 0:1], constant=0.0, writes=("sck",))
        r.dve("memset", ap=sckv[:, 0:1], constant=0.0, writes=("sckv",))
        for t0 in range(4):
            sl = slice(t0 * 512, (t0 + 1) * 512)
            r.load(xs[:, :, :], x[b, :, sl].rearrange("(c p) t -> p c t", p=128), "xs")
            emit_ln_cm(L, xs, "xs", xs, "xs", 512, consts, psk=("PS",))
            emit_ln_cm(L, xs, "xs", xp[:, :, sl], "xp", 512, consts,
                       gain=ln0g, bias=ln0b, gkey=("b_ln0g", "b_ln0b"),
                       psk=("PS",))
            emit_ln_cm(L, xp[:, :, sl], "xp", xn[:, :, 1 + sl.start:1 + sl.stop],
                       "xn", 512, consts, gain=preg, bias=preb,
                       gkey=("b_preg", "b_preb"), psk=("PS",))
        for ht in range(NHT):
            pr = ht % 2
            wtp, wAp, wBp = wt[pr], wA[pr], wB[pr]
            kw, kA, kB = f"wt{pr}", f"wA{pr}", f"wB{pr}"
            hsl = slice(ht * 128, (ht + 1) * 128)
            r.load(wtp[:, :, 0, :], tWk[:, hsl].rearrange("(c p) m -> p c m", p=128), kw)
            r.load(wtp[:, :, 1, :], tWv[:, hsl].rearrange("(c p) m -> p c m", p=128), kw)
            r.load(wtp[:, :, 2, :], tWr[:, hsl].rearrange("(c p) m -> p c m", p=128), kw)
            r.load(wOt[:, 0, :], tWo[hsl, :], "wOt")
            for mat in range(3):
                mx = mix_all[mat]
                for c in range(8):
                    r.dve("tensor_scalar_mul", out=wAp[:, c, mat, :],
                          in0=wtp[:, c, mat, :], scalar1=mx[:, c, :],
                          reads=(kw, "b_mixk", "b_mixv", "b_mixr"),
                          writes=(kA,))
                    r.dve("tensor_scalar_mul", out=wBp[:, c, mat, :],
                          in0=wtp[:, c, mat, :], scalar1=imixs[:, c, mat:mat + 1],
                          reads=(kw, "imixs"), writes=(kB,))
            msl = slice(0, 128)
            for n4 in range(4):
                nsl = slice(n4 * 512, (n4 + 1) * 512)
                dual_proj(pP, wAp[:, :, 0, :], wBp[:, :, 0, :], msl, nsl,
                          extra=(kA, kB))
                r.dve("tensor_scalar_min", out=ks[:, nsl], in0=pP,
                      scalar1=60.0, reads=(), writes=("ks",))
                dual_proj(pW, wAp[:, :, 1, :], wBp[:, :, 1, :], msl, nsl,
                          extra=(kA, kB))
                r.act("activation", out=vs[:, nsl], in_=pW, func=AF.Copy,
                      reads=(), writes=("vs",))
                dual_proj(pR2, wAp[:, :, 2, :], wBp[:, :, 2, :], msl, nsl,
                          extra=(kA, kB))
                r.act("activation", out=srs[:, nsl], in_=pR2, func=AF.Sigmoid,
                      reads=(), writes=("srs",))
            r.act("activation", out=ks[:], in_=ks[:], func=AF.Exp,
                  reads=("ks",), writes=("ks",))
            r.dve("tensor_tensor", out=vs[:], in0=ks[:], in1=vs[:], op=ALU.mult,
                  reads=("ks", "vs"), writes=("vs",))
            dcol = dcs[:, ht:ht + 1]
            fcol = frs[:, ht:ht + 1]
            r.dve("tensor_tensor_scan", out=sck[:, 1:T + 1],
                  data0=dcol.broadcast_to((128, T)), data1=ks[:],
                  initial=0.0, op0=ALU.mult, op1=ALU.add,
                  reads=("ks", "dcs"), writes=("sck",))
            r.dve("tensor_tensor_scan", out=sckv[:, 1:T + 1],
                  data0=dcol.broadcast_to((128, T)), data1=vs[:],
                  initial=0.0, op0=ALU.mult, op1=ALU.add,
                  reads=("vs", "dcs"), writes=("sckv",))
            r.dve("scalar_tensor_tensor", out=ks[:], in0=ks[:], scalar=fcol,
                  in1=sck[:, 0:T], op0=ALU.mult, op1=ALU.add,
                  reads=("ks", "frs", "sck"), writes=("ks",))
            r.dve("scalar_tensor_tensor", out=vs[:], in0=vs[:], scalar=fcol,
                  in1=sckv[:, 0:T], op0=ALU.mult, op1=ALU.add,
                  reads=("vs", "frs", "sckv"), writes=("vs",))
            r.dve("tensor_scalar_add", out=ks[:], in0=ks[:], scalar1=1e-8,
                  reads=("ks",), writes=("ks",))
            r.dve("reciprocal", out=ks[:], in_=ks[:], reads=("ks",), writes=("ks",))
            r.dve("tensor_tensor", out=vs[:], in0=vs[:], in1=ks[:], op=ALU.mult,
                  reads=("vs", "ks"), writes=("vs",))
            r.dve("tensor_tensor", out=rwk[:], in0=vs[:], in1=srs[:], op=ALU.mult,
                  reads=("vs", "srs"), writes=("rwk",))
            for mo in range(8):
                mosl = slice(mo * 128, (mo + 1) * 128)
                for n4 in range(4):
                    nsl = slice(n4 * 512, (n4 + 1) * 512)
                    pdst = pW if (mo * 4 + n4) % 2 == 0 else pR2
                    r.pe("matmul", out=pdst, lhsT=wOt[:, 0, mosl], rhs=rwk[:, nsl],
                         start=True, stop=True, reads=("wOt", "rwk"),
                         writes=())
                    r.dve("tensor_tensor", out=xp[:, mo, nsl], in0=xp[:, mo, nsl],
                          in1=pdst, op=ALU.add, reads=("xp",), writes=("xp",))
        for t0 in range(4):
            sl = slice(t0 * 512, (t0 + 1) * 512)
            emit_ln_cm(L, xp[:, :, sl], "xp", xn[:, :, 1 + sl.start:1 + sl.stop],
                       "xn", 512, consts, gain=postg, bias=postb,
                       gkey=("b_postg", "b_postb"), psk=("PS",))
        for ht in range(NHT):
            pr = ht % 2
            wtp, wAp, wBp = wt[pr], wA[pr], wB[pr]
            kw, kA, kB = f"wt{pr}", f"wA{pr}", f"wB{pr}"
            hsl = slice(ht * 128, (ht + 1) * 128)
            r.load(wtp[:, :, 0, :], cWk[:, hsl].rearrange("(c p) m -> p c m", p=128), kw)
            r.load(wOt[:, 0, :], cWv[hsl, :], "wOt")
            for c in range(8):
                r.dve("tensor_scalar_mul", out=wAp[:, c, 0, :],
                      in0=wtp[:, c, 0, :], scalar1=cmixk[:, c, :],
                      reads=(kw, "b_cmixk"), writes=(kA,))
                r.dve("tensor_scalar_mul", out=wBp[:, c, 0, :],
                      in0=wtp[:, c, 0, :], scalar1=imixs[:, c, 3:4],
                      reads=(kw, "imixs"), writes=(kB,))
            for n4 in range(4):
                nsl = slice(n4 * 512, (n4 + 1) * 512)
                dual_proj(pP if n4 % 2 == 0 else pR2,
                          wAp[:, :, 0, :], wBp[:, :, 0, :], slice(0, 128), nsl,
                          extra=(kA, kB))
                r.act("activation", out=hb[:, nsl],
                      in_=pP if n4 % 2 == 0 else pR2, func=AF.Silu,
                      reads=(), writes=("rwk",))
            for mo in range(8):
                mosl = slice(mo * 128, (mo + 1) * 128)
                for n4 in range(4):
                    nsl = slice(n4 * 512, (n4 + 1) * 512)
                    pdst = pW if (mo * 4 + n4) % 2 == 0 else pR2
                    r.pe("matmul", out=pdst, lhsT=wOt[:, 0, mosl], rhs=hb[:, nsl],
                         start=True, stop=True, reads=("wOt", "rwk"),
                         writes=())
                    r.dve("tensor_tensor", out=xp[:, mo, nsl], in0=xp[:, mo, nsl],
                          in1=pdst, op=ALU.add, reads=("xp",), writes=("xp",))
        for mo in range(8):
            pr = mo % 2
            wtp, wAp, wBp = wt[pr], wA[pr], wB[pr]
            kw, kA, kB = f"wt{pr}", f"wA{pr}", f"wB{pr}"
            mosl = slice(mo * 128, (mo + 1) * 128)
            r.load(wtp[:, :, 0, :],
                   cWr[:, mosl].rearrange("(c p) m -> p c m", p=128), kw)
            for c in range(8):
                r.dve("tensor_scalar_mul", out=wAp[:, c, 0, :],
                      in0=wtp[:, c, 0, :], scalar1=cmixr[:, c, :],
                      reads=(kw, "b_cmixr"), writes=(kA,))
                r.dve("tensor_scalar_mul", out=wBp[:, c, 0, :],
                      in0=wtp[:, c, 0, :], scalar1=imixs[:, c, 4:5],
                      reads=(kw, "imixs"), writes=(kB,))
            for n4 in range(4):
                nsl = slice(n4 * 512, (n4 + 1) * 512)
                pdst = pP if n4 % 2 == 0 else pR2
                dual_proj(pdst, wAp[:, :, 0, :], wBp[:, :, 0, :], slice(0, 128),
                          nsl, extra=(kA, kB))
                r.act("activation", out=srs[:, n4 * 512:n4 * 512 + 512],
                      in_=pdst, func=AF.Sigmoid, reads=(), writes=("srs",))
                r.dve("tensor_tensor", out=xp[:, mo, nsl], in0=xp[:, mo, nsl],
                      in1=srs[:, n4 * 512:n4 * 512 + 512], op=ALU.add,
                      reads=("xp", "srs"), writes=("xp",))
        for t0 in range(4):
            sl = slice(t0 * 512, (t0 + 1) * 512)
            r.load(xs[:, :, :], x[b, :, sl].rearrange("(c p) t -> p c t", p=128), "xs")
            for c in range(8):
                r.dve("tensor_tensor", out=xs[:, c, :], in0=xs[:, c, :],
                      in1=xp[:, c, sl], op=ALU.add, reads=("xs", "xp"),
                      writes=("xs",))
            r.store(xo[b, :, sl].rearrange("(c p) t -> p c t", p=128), xs[:, :, :],
                    reads=("xs",))
    return L


def host_prep_rwkv(inp):
    w = {}
    w["tWk"] = np.ascontiguousarray(inp["L1_attn_b_tm_Wk"]).astype(BF)
    w["tWv"] = np.ascontiguousarray(inp["L1_attn_b_tm_Wv"]).astype(BF)
    w["tWr"] = np.ascontiguousarray(inp["L1_attn_b_tm_Wr"]).astype(BF)
    w["tWo"] = np.ascontiguousarray(inp["L1_attn_b_tm_Wo"]).astype(BF)
    w["cWk"] = np.ascontiguousarray(inp["L1_attn_b_cm_Wk"]).astype(BF)
    w["cWv"] = np.ascontiguousarray(inp["L1_attn_b_cm_Wv"]).astype(BF)
    w["cWr"] = np.ascontiguousarray(inp["L1_attn_b_cm_Wr"]).astype(BF)
    for nm, key in (("ln0g", "L1_attn_b_ln0_g"), ("ln0b", "L1_attn_b_ln0_b"),
                    ("preg", "L1_attn_b_pre_g"), ("preb", "L1_attn_b_pre_b"),
                    ("postg", "L1_attn_b_post_g"), ("postb", "L1_attn_b_post_b"),
                    ("mixk", "L1_attn_b_tm_mix_k"), ("mixv", "L1_attn_b_tm_mix_v"),
                    ("mixr", "L1_attn_b_tm_mix_r"), ("cmixk", "L1_attn_b_cm_mix_k"),
                    ("cmixr", "L1_attn_b_cm_mix_r")):
        w[nm] = host_bias(inp[key])
    w["dcay"] = np.ascontiguousarray(
        np.asarray(inp["L1_attn_b_tm_decay"], np.float32).reshape(32, 128).T)
    w["frst"] = np.ascontiguousarray(
        np.asarray(inp["L1_attn_b_tm_first"], np.float32).reshape(32, 128).T)
    return w


def launch_rwkv(x_shards, inp):
    L = Launch("rwkv")
    x = L.dram_in("x", [B_PER_CORE, D, T], F32)
    xo = L.dram_out("xo", [B_PER_CORE, D, T], F32)
    build_rwkv(L, x, xo)
    w = host_prep_rwkv(inp)
    in_maps = [{"x": x_shards[c], **w} for c in range(NCORES)]
    res = L.run(in_maps)
    return [res[c]["xo"] for c in range(NCORES)]



def build_ffn1(L, x, xo, dbg=False):
    r = L.rec
    if dbg:
        zdump = L.dram_out("zdump", [8, 128, T], F32)
        cdump = L.dram_out("cdump", [4, 128, T], BF16)
    HID = 2816
    NHT = HID // 128
    wqkv_d = [L.dram_in(f"wqkv{l}", [D, 1536], BF16) for l in range(2)]
    wo_d = [L.dram_in(f"wo{l}", [512, D], BF16) for l in range(2)]
    w1_d = [L.dram_in(f"w1p{l}", [D, 2 * HID], BF16) for l in range(2)]
    w2_d = [L.dram_in(f"w2p{l}", [HID, D], BF16) for l in range(2)]
    b1a_d = [load_bias_col(L, f"b1a{l}", 128 * NHT) for l in range(2)]
    b1g_d = [load_bias_col(L, f"b1g{l}", 128 * NHT) for l in range(2)]
    b2_d = [load_bias_col(L, f"b2{l}", D) for l in range(2)]
    lg_d = [load_bias_col(L, f"lg{l}", D) for l in range(2)]
    lb_d = [load_bias_col(L, f"lb{l}", D) for l in range(2)]
    qs_d = [L.dram_in(f"qs{l}", [128, 1], F32) for l in range(2)]
    ks_d = [L.dram_in(f"ks{l}", [128, 1], F32) for l in range(2)]
    rotq_d = L.dram_in("rotq", [2, 128, T], BF16)
    rotk_d = L.dram_in("rotk", [2, 128, T], BF16)
    tri_d = L.dram_in("triB", [128, 128], BF16)
    triT_d = L.dram_in("triT", [128, 128], BF16)
    r2_d = L.dram_in("r2", [128, 128], BF16)
    idf_d = L.dram_in("idf_in", [128, 128], F32)
    e2f = L.dram_in("e2f", [128, 2], F32)
    f2f = L.dram_in("f2f", [2, 128], F32)

    consts = emit_const_setup(L)
    ones_col, ones_row, eps_ln = consts

    z = L.sb("z", [128, 8, T + 1], F32)
    zb = L.sb("zb", [128, 8, T + 1], BF16)
    xs = L.sb("xs", [128, 8, 512], F32)
    raw = xs[:].rearrange("p c t -> p (c t)")[:, 0:2048]
    h2b = L.sb("h2b", [128, 8, 512], BF16)
    wqkvt = L.sb("wqkvt", [128, 8, 384], BF16)
    wosb = L.sb("wosb", [128, 1024], BF16)
    w1t = [L.sb(f"w1t{i}", [128, 8, 256], BF16) for i in range(2)]
    w2t = [L.sb("w2t0", [128, NHT, 128], BF16)] * 2
    hfull = L.sb("hfull", [128, NHT, 512], BF16)
    Vt = L.sb("Vt", [128, 16, 130], BF16)
    qp = L.sb("qp", [128, T], BF16)
    kp = L.sb("kp", [128, T], BF16)
    ctx = L.sb("ctx", [128, T], BF16)
    probsT = L.sb("probsT", [128, 2, 8, 256], BF16)
    rot = L.sb("rot", [128, 2, 512], BF16)
    trim = L.sb("trim", [128, 128], BF16)
    trimT = L.sb("trimT", [128, 128], BF16)
    r2s = L.sb("r2s", [128, 128], BF16)
    idf = L.sb("idf", [128, 128], F32)
    e2s = L.sb("e2s", [128, 2], F32)
    f2s = L.sb("f2s", [2, 128], F32)
    qscol = L.sb("qscol", [128, 2], F32)
    rrnb = L.sb("rrnb", [64, 512], F32)
    rr = rrnb[0:2, 0:512]
    sb512 = L.sb("sb512", [128, 512], BF16)
    sb512b = L.sb("sb512b", [128, 512], BF16)
    sb512c, sb512d = sb512, sb512b
    nb = rrnb[0:64, 0:256]
    r64 = rrnb[0:1, 256:512]
    mx1 = L.sb("mx1", [128, 1], F32)
    sqs = None
    m1 = L.sb("m1", [1, 1], F32)
    nmx = L.sb("nmx", [128, 1], F32)

    psAll = L.ps("psAll", [128, 4096], F32)
    ensure_ln_scratch(L, 512, views={
        "pstat_a": psAll[0:1, 0:512],
        "pstat_b": psAll[0:1, 512:1024],
        "pbc": psAll[:, 1024:2048],
    })
    vA = psAll[:, 0:2048]
    vB = psAll[:, 2048:4096]
    pV = psAll[:, 2048:2176]
    pP = psAll[:, 2560:3072]
    pR = psAll[0:2, 3072:3584]
    pB = psAll[:, 3584:4096]
    pRh = psAll[:, 2048:2560]
    pT1 = psAll[0:1, 3072:3200]
    pM = psAll[0:128, 3328:3329]
    pAVa = psAll[0:65, 0:256]
    pAVb = psAll[0:65, 2048:2304]
    pBC = psAll[0:64, 512:768]
    psA = psAll[:, 0:512]
    psB = psAll[:, 512:1024]
    pW = psAll[:, 2560:3072]

    r.load(trim[:], tri_d[:, :], "trim")
    r.load(trimT[:], triT_d[:, :], "trimT")
    r.load(r2s[:], r2_d[:, :], "r2s")
    r.load(idf[:], idf_d[:, :], "idf")
    r.load(e2s[:], e2f[:, :], "e2s")
    r.load(f2s[:], f2f[:, :], "f2s")

    def zsl(c, nsl):
        if c >= 4:
            return zb[:, c, nsl.start:nsl.stop]
        return zb[:, c, 1 + nsl.start:1 + nsl.stop]

    def zslf(c, nsl):
        if c >= 4:
            return z[:, c, nsl.start:nsl.stop]
        return z[:, c, 1 + nsl.start:1 + nsl.stop]

    for b in range(B_PER_CORE):
        r.dve("memset", ap=z[:, :, 0:1], constant=0.0, writes=("z",))
        r.dve("memset", ap=zb[:, :, 0:1], constant=0.0, writes=("zb",))
        r.dve("memset", ap=Vt[:, :, 64:65], constant=1.0, writes=("Vt",))
        r.dve("memset", ap=Vt[:, :, 129:130], constant=1.0, writes=("Vt",))
        for t0 in range(4):
            sl = slice(t0 * 512, (t0 + 1) * 512)
            r.load(xs[:, :, :], x[b, :, sl].rearrange("(c p) t -> p c t", p=128), "xs")
            emit_ln_cm(L, xs, "xs", z[:, :, 1 + sl.start:1 + sl.stop], "z",
                       512, consts, psk=("PS",))
            for c in range(8):
                r.act("activation", out=zb[:, c, 1 + sl.start:1 + sl.stop],
                      in_=z[:, c, 1 + sl.start:1 + sl.stop], func=AF.Copy,
                      reads=("z",), writes=("zb",))
        for l in range(2):
            r.load(qscol[:, 0:1], qs_d[l][:, :], "qscol")
            r.load(qscol[:, 1:2], ks_d[l][:, :], "qscol")
            r.dve("tensor_scalar_mul", out=qscol[:, 0:1], in0=qscol[:, 0:1],
                  scalar1=8.0, reads=("qscol",), writes=("qscol",))
            for hp in range(4):
                for i3, base in enumerate((0, 512, 1024)):
                    csl = slice(base + hp * 128, base + (hp + 1) * 128)
                    r.load(wqkvt[:, :, i3 * 128:(i3 + 1) * 128],
                           wqkv_d[l][:, csl].rearrange("(c p) m -> p c m", p=128),
                           "wqkvt")
                r.load(wosb[:, :], wo_d[l][hp * 128:(hp + 1) * 128, :], "wosb")
                for tt in range(16):
                    tsl = slice(tt * 128, (tt + 1) * 128)
                    for c in range(8):
                        r.pe("matmul", out=pV, lhsT=zsl(c, tsl),
                             rhs=wqkvt[:, c, 256:384],
                             start=(c == 0), stop=(c == 7),
                             reads=("zb", "wqkvt"), writes=("PS",))
                    r.act("activation", out=Vt[:, tt, 0:64], in_=pV[:, 0:64],
                          func=AF.Copy, reads=("PS",), writes=("Vt",))
                    r.act("activation", out=Vt[:, tt, 65:129], in_=pV[:, 64:128],
                          func=AF.Copy, reads=("PS",), writes=("Vt",))
                for qk in range(2):
                    dst = qp if qk == 0 else kp
                    dkey = "qp" if qk == 0 else "kp"
                    for n4 in range(4):
                        nsl = slice(n4 * 512, (n4 + 1) * 512)
                        for c in range(8):
                            r.pe("matmul", out=pP,
                                 lhsT=wqkvt[:, c, qk * 128:(qk + 1) * 128],
                                 rhs=zsl(c, nsl), start=(c == 0), stop=(c == 7),
                                 reads=("wqkvt", "zb"), writes=("PS",))
                        r.act("activation", out=raw[:, nsl], in_=pP, func=AF.Copy,
                              reads=("PS",), writes=("xs",))
                    sqt_ = L._sb["sqt"]
                    for n4 in range(4):
                        nsl = slice(n4 * 512, (n4 + 1) * 512)
                        r.load(rot[:, :, :],
                               (rotq_d if qk == 0 else rotk_d)[:, :, nsl].rearrange(
                                   "two p t -> p two t"), "rot")
                        r.act("activation", out=sqt_[:, 0:512], in_=raw[:, nsl],
                              func=AF.Square, reads=("xs",), writes=("sqt",))
                        r.pe("matmul", out=pR[:, 0:512], lhsT=e2s[:],
                             rhs=sqt_[:, 0:512], start=True, stop=True,
                             reads=("e2s", "sqt"), writes=("PS",))
                        r.act("activation", out=rr[:], in_=pR[:, 0:512],
                              func=AF.Sqrt, reads=("PS",), writes=("rrnb",))
                        r.dve("tensor_scalar_max", out=rr[:], in0=rr[:],
                              scalar1=1e-12, reads=("rrnb",), writes=("rrnb",))
                        r.dve("reciprocal", out=rr[:], in_=rr[:], reads=("rrnb",),
                              writes=("rrnb",))
                        r.pe("matmul", out=pB[:, 0:512], lhsT=f2s[:], rhs=rr[:],
                             start=True, stop=True, reads=("f2s", "rrnb"),
                             writes=("PS",))
                        r.dve("tensor_tensor", out=sb512[:], in0=raw[:, nsl],
                              in1=pB[:, 0:512], op=ALU.mult,
                              reads=("xs", "PS"), writes=("sb512",))
                        r.dve("tensor_scalar_mul", out=sb512[:], in0=sb512[:],
                              scalar1=qscol[:, qk:qk + 1],
                              reads=("sb512", "qscol"), writes=("sb512",))
                        r.pe("matmul", out=pRh, lhsT=r2s[:], rhs=sb512[:],
                             start=True, stop=True, reads=("r2s", "sb512"),
                             writes=("PS",))
                        r.act("activation", out=sb512b[:], in_=pRh, func=AF.Copy,
                              reads=("PS",), writes=("sb512b",))
                        r.dve("tensor_tensor", out=sb512[:], in0=sb512[:],
                              in1=rot[:, 0, :], op=ALU.mult,
                              reads=("sb512", "rot"), writes=("sb512",))
                        r.dve("tensor_tensor", out=sb512b[:], in0=sb512b[:],
                              in1=rot[:, 1, :], op=ALU.mult,
                              reads=("sb512b", "rot"), writes=("sb512b",))
                        r.dve("tensor_tensor", out=dst[:, nsl], in0=sb512[:],
                              in1=sb512b[:], op=ALU.add,
                              reads=("sb512", "sb512b"), writes=(dkey,))
                for w in range(4):
                    for qh in range(2):
                        q0 = w * 512 + qh * 256
                        kts = []
                        for i in range(2 * qh, 4):
                            if w > 0:
                                kts.append((i, 4 * (w - 1) + i))
                        for j in range(0, 2 * qh + 2):
                            kts.append((4 + j, 4 * w + j))
                        for h2, vv, tp in ((0, vA, (0, 0)), (1, vB, (64, 0))):
                            hb_ = slice(h2 * 64, (h2 + 1) * 64)
                            for ktL, kta in kts:
                                r.pe("matmul",
                                     out=vv[:, ktL * 256:(ktL + 1) * 256],
                                     lhsT=kp[hb_, kta * 128:(kta + 1) * 128],
                                     rhs=qp[hb_, q0:q0 + 256],
                                     start=True, stop=True, tile_position=tp,
                                     reads=("kp", "qp"), writes=("PS",))
                        if qh == 0:
                            pT1q = psAll[0:1, 1536:1664]
                            pMq = psAll[0:128, 1792:1793]
                        else:
                            pT1q = psAll[0:1, 0:128]
                            pMq = psAll[0:128, 256:257]
                        for h2, vv, pAV in ((0, vA, pAVa), (1, vB, pAVb)):
                            lo = kts[0][0] * 256
                            hi = (kts[-1][0] + 1) * 256
                            r.dve("tensor_reduce", out=mx1[:, 0:1],
                                  in_=vv[:, lo:hi], axis=AX.X, op=ALU.max,
                                  reads=("PS",), writes=("mx1",))
                            r.pe("transpose", out=pT1q, in_=mx1[:, 0:1],
                                 identity=idf[:], reads=("mx1", "idf"),
                                 writes=("PS",))
                            r.dve("tensor_reduce", out=m1[:, 0:1],
                                  in_=pT1q[0:1, :], axis=AX.X, op=ALU.max,
                                  reads=("PS",), writes=("m1",))
                            r.pe("matmul", out=pMq, lhsT=ones_row[:], rhs=m1[:],
                                 start=True, stop=True, reads=("ones_row", "m1"),
                                 writes=("PS",))
                            r.dve("tensor_scalar_mul", out=nmx[:], in0=pMq,
                                  scalar1=-1.0, reads=("PS",), writes=("nmx",))
                            r.act("activation", out=probsT[:, h2].rearrange(
                                      "p kt q -> p (kt q)")[:, lo:hi],
                                  in_=vv[:, lo:hi], func=AF.Exp, bias=nmx[:],
                                  reads=("PS", "nmx"), writes=("probsT",))
                            for j in (2 * qh, 2 * qh + 1):
                                qq = slice((j - 2 * qh) * 128,
                                           (j - 2 * qh + 1) * 128)
                                r.dve("tensor_tensor",
                                      out=probsT[:, h2, 4 + j, qq],
                                      in0=probsT[:, h2, 4 + j, qq], in1=trim[:],
                                      op=ALU.mult, reads=("probsT", "trim"),
                                      writes=("probsT",))
                                if w > 0:
                                    r.dve("tensor_tensor",
                                          out=probsT[:, h2, j, qq],
                                          in0=probsT[:, h2, j, qq], in1=trimT[:],
                                          op=ALU.mult, reads=("probsT", "trimT"),
                                          writes=("probsT",))
                            r.dve("memset",
                                  ap=probsT[:, h2, 4 + 2 * qh + 1, 0:128],
                                  constant=0.0, writes=("probsT",))
                            if w > 0:
                                r.dve("memset",
                                      ap=probsT[:, h2, 2 * qh, 128:256],
                                      constant=0.0, writes=("probsT",))
                            for ii, (ktL, kta) in enumerate(kts):
                                r.pe("matmul", out=pAV,
                                     lhsT=Vt[:, kta, h2 * 65:h2 * 65 + 65],
                                     rhs=probsT[:, h2, ktL, :],
                                     start=(ii == 0), stop=(ii == len(kts) - 1),
                                     reads=("Vt", "probsT"), writes=("PS",))
                            r.dve("reciprocal", out=r64[:, 0:256], in_=pAV[64:65, :],
                                  reads=("PS",), writes=("rrnb",))
                            r.pe("matmul", out=pBC, lhsT=ones_row[:, 0:64],
                                 rhs=r64[:, 0:256], start=True, stop=True,
                                 reads=("ones_row", "rrnb"), writes=("PS",))
                            r.act("activation", out=nb[:], in_=pBC, func=AF.Copy,
                                  reads=("PS",), writes=("rrnb",))
                            r.dve("tensor_tensor",
                                  out=ctx[h2 * 64:(h2 + 1) * 64, q0:q0 + 256],
                                  in0=pAV[0:64, :], in1=nb[:], op=ALU.mult,
                                  reads=("PS", "rrnb"), writes=("ctx",))
                if dbg and b == 0 and l == 0:
                    r.store(cdump[hp, :, :], ctx[:, :], reads=("ctx",))
                for mo in range(8):
                    mosl = slice(mo * 128, (mo + 1) * 128)
                    for n4 in range(4):
                        nsl = slice(n4 * 512, (n4 + 1) * 512)
                        r.pe("matmul", out=pW, lhsT=wosb[:, mosl],
                             rhs=ctx[:, nsl], start=True, stop=True,
                             reads=("wosb", "ctx"), writes=("PS",))
                        r.dve("tensor_tensor",
                              out=z[:, mo, 1 + nsl.start:1 + nsl.stop],
                              in0=z[:, mo, 1 + nsl.start:1 + nsl.stop],
                              in1=pW, op=ALU.add, reads=("z", "PS"),
                              writes=("z",))
            if dbg and b == 0 and l == 0:
                r.store(zdump.rearrange("c p t -> p c t"), z[:, :, 1:T + 1],
                        reads=("z",))
            for c in range(8):
                r.act("activation", out=zb[:, c, 1:T + 1], in_=z[:, c, 1:T + 1],
                      func=AF.Copy, reads=("z",), writes=("zb",))
            for n4 in reversed(range(4)):
                nsl = slice(n4 * 512, (n4 + 1) * 512)
                chunk_aps = [zslf(c, nsl) for c in range(8)]
                emit_ln_cm(L, chunk_aps, "z", h2b, "h2b", 512, consts,
                           gain=lg_d[l], bias=lb_d[l],
                           gkey=(f"b_lg{l}", f"b_lb{l}"), psk=("PS",))
                for ht in range(NHT):
                    pr = ht % 2
                    w1p = w1t[pr]
                    k1 = f"w1t{pr}"
                    pA = psA if pr == 0 else psAll[:, 1024:1536]
                    pB = psB if pr == 0 else psAll[:, 1536:2048]
                    sbA, sbB = sb512, sb512b
                    kA, kB = "sb512", "sb512b"
                    r.load(w1p[:, :, :],
                           w1_d[l][:, ht * 256:(ht + 1) * 256].rearrange(
                               "(c p) m -> p c m", p=128), k1)
                    for c in range(8):
                        r.pe("matmul", out=pA, lhsT=w1p[:, c, 0:128],
                             rhs=h2b[:, c, :], start=(c == 0), stop=(c == 7),
                             reads=(k1, "h2b"), writes=())
                    for c in range(8):
                        r.pe("matmul", out=pB, lhsT=w1p[:, c, 128:256],
                             rhs=h2b[:, c, :], start=(c == 0), stop=(c == 7),
                             reads=(k1, "h2b"), writes=())
                    r.act("activation", out=sbB[:], in_=pB, func=AF.Gelu,
                          bias=b1g_d[l][:, ht, :],
                          reads=(f"b_b1g{l}",), writes=(kB,))
                    r.dve("tensor_scalar_add", out=sbA[:], in0=pA,
                          scalar1=b1a_d[l][:, ht, :],
                          reads=(f"b_b1a{l}",), writes=(kA,))
                    r.dve("tensor_tensor", out=hfull[:, ht, :],
                          in0=sbA[:], in1=sbB[:], op=ALU.mult,
                          reads=(kA, kB), writes=("hfull",))
                for mo in range(8):
                    pr = mo % 2
                    w2p_ = w2t[0]
                    k2 = "w2t0"
                    pdst = pW if pr == 0 else psAll[:, 2048:2560]
                    mosl = slice(mo * 128, (mo + 1) * 128)
                    r.load(w2p_[:, :, :],
                           w2_d[l][:, mosl].rearrange("(c p) m -> p c m", p=128),
                           k2)
                    for c in range(NHT):
                        r.pe("matmul", out=pdst, lhsT=w2p_[:, c, :],
                             rhs=hfull[:, c, :], start=(c == 0),
                             stop=(c == NHT - 1), reads=(k2, "hfull"),
                             writes=())
                    r.dve("scalar_tensor_tensor",
                          out=z[:, mo, 1 + nsl.start:1 + nsl.stop], in0=pdst,
                          scalar=b2_d[l][:, mo, :],
                          in1=z[:, mo, 1 + nsl.start:1 + nsl.stop],
                          op0=ALU.add, op1=ALU.add,
                          reads=(f"b_b2{l}", "z"), writes=("z",))
            if l == 0:
                for c in range(8):
                    r.act("activation", out=zb[:, c, 1:T + 1],
                          in_=z[:, c, 1:T + 1], func=AF.Copy, reads=("z",),
                          writes=("zb",))
        for t0 in range(4):
            sl = slice(t0 * 512, (t0 + 1) * 512)
            r.load(xs[:, :, :], x[b, :, sl].rearrange("(c p) t -> p c t", p=128), "xs")
            for c in range(8):
                r.dve("tensor_tensor", out=xs[:, c, :], in0=xs[:, c, :],
                      in1=z[:, c, 1 + sl.start:1 + sl.stop], op=ALU.add,
                      reads=("xs", "z"), writes=("xs",))
            r.store(xo[b, :, sl].rearrange("(c p) t -> p c t", p=128), xs[:, :, :],
                    reads=("xs",))
    return L


def host_prep_ffn1(inp):
    w = {}
    HID = 2816
    for l in range(2):
        w[f"wqkv{l}"] = np.ascontiguousarray(inp["L1_ffn_wqkv"][l]).astype(BF)
        w[f"wo{l}"] = np.ascontiguousarray(inp["L1_ffn_wo"][l]).astype(BF)
        w1 = np.asarray(inp["L1_ffn_w1"][l], np.float32)
        a, g = w1[:, :2730], w1[:, 2730:]
        ap = np.zeros((1024, HID), np.float32); ap[:, :2730] = a
        gp = np.zeros((1024, HID), np.float32); gp[:, :2730] = g
        w1p = np.zeros((1024, 2 * HID), np.float32)
        for t in range(HID // 128):
            w1p[:, t * 256:t * 256 + 128] = ap[:, t * 128:(t + 1) * 128]
            w1p[:, t * 256 + 128:t * 256 + 256] = gp[:, t * 128:(t + 1) * 128]
        w[f"w1p{l}"] = w1p.astype(BF)
        w2 = np.asarray(inp["L1_ffn_w2"][l], np.float32)
        w2p = np.zeros((HID, 1024), np.float32); w2p[:2730] = w2
        w[f"w2p{l}"] = w2p.astype(BF)
        b1 = np.asarray(inp["L1_ffn_b1"][l], np.float32)
        b1ap = np.zeros(HID, np.float32); b1ap[:2730] = b1[:2730]
        b1gp = np.zeros(HID, np.float32); b1gp[:2730] = b1[2730:]
        w[f"b1a{l}"] = host_bias(b1ap)
        w[f"b1g{l}"] = host_bias(b1gp)
        w[f"b2{l}"] = host_bias(inp["L1_ffn_b2"][l])
        w[f"lg{l}"] = host_bias(inp["L1_ffn_ln_g"][l])
        w[f"lb{l}"] = host_bias(inp["L1_ffn_ln_b"][l])
        w[f"qs{l}"] = np.ascontiguousarray(
            np.tile(np.asarray(inp["L1_ffn_q_scale"][l], np.float32), 2)).reshape(128, 1)
        w[f"ks{l}"] = np.ascontiguousarray(
            np.tile(np.asarray(inp["L1_ffn_k_scale"][l], np.float32), 2)).reshape(128, 1)
    DH, WIN = 64, 512
    half = np.arange(0, DH, 2, dtype=np.float64)
    inv_freq = 1.0 / 10000.0 ** (half / DH)
    t = np.arange(T, dtype=np.float64)
    freqs = t[:, None] * inv_freq[None, :]
    freqs = np.concatenate([freqs, freqs], -1)
    base = (half + 0.4 * DH) / (1.4 * DH)
    power = (t - T // 2) / float(WIN // 2)
    scale = base[None, :] ** power[:, None]
    scale = np.concatenate([scale, scale], -1)
    cs, sn = np.cos(freqs), np.sin(freqs)
    qa = (cs * scale).T; qb = (sn * scale).T
    ka = (cs / scale).T; kb = (sn / scale).T
    w["rotq"] = np.stack([np.tile(qa, (2, 1)), np.tile(qb, (2, 1))]).astype(BF)
    w["rotk"] = np.stack([np.tile(ka, (2, 1)), np.tile(kb, (2, 1))]).astype(BF)
    kk, qq = np.meshgrid(np.arange(128), np.arange(128), indexing="ij")
    w["triB"] = (kk <= qq).astype(BF)
    w["triT"] = (kk >= qq).astype(BF)
    R = np.zeros((64, 64), np.float32)
    for d in range(32):
        R[d + 32, d] = -1.0
        R[d, d + 32] = 1.0
    R2 = np.zeros((128, 128), np.float32)
    R2[0:64, 0:64] = R
    R2[64:128, 64:128] = R
    w["r2"] = R2.astype(BF)
    w["idf_in"] = np.eye(128, dtype=np.float32)
    e2 = np.zeros((128, 2), np.float32)
    e2[0:64, 0] = 1.0
    e2[64:128, 1] = 1.0
    w["e2f"] = e2
    w["f2f"] = np.ascontiguousarray(e2.T)
    return w


def launch_ffn1(x_shards, inp):
    L = Launch("ffn1")
    x = L.dram_in("x", [B_PER_CORE, D, T], F32)
    xo = L.dram_out("xo", [B_PER_CORE, D, T], F32)
    build_ffn1(L, x, xo)
    w = host_prep_ffn1(inp)
    in_maps = [{"x": x_shards[c], **w} for c in range(NCORES)]
    res = L.run(in_maps)
    return [res[c]["xo"] for c in range(NCORES)]



def build_a1a(L, x, uo):
    r = L.rec
    lsc = load_bias_col(L, "lsc", D)
    lbi = load_bias_col(L, "lbi", D)
    consts = emit_const_setup(L)
    ensure_ln_scratch(L, 512)
    xs = L.sb("xs", [128, 8, 512], F32)
    us = L.sb("us", [128, 8, 512], BF16)
    for b in range(B_PER_CORE):
        for t0 in range(4):
            sl = slice(t0 * 512, (t0 + 1) * 512)
            r.load(xs[:, :, :], x[b, :, sl].rearrange("(c p) t -> p c t", p=128), "xs")
            emit_ln_cm(L, xs, "xs", xs, "xs", 512, consts)
            emit_ln_cm(L, xs, "xs", us, "us", 512, consts,
                       gain=lsc, bias=lbi, gkey=("b_lsc", "b_lbi"))
            r.store(uo[b, :, sl].rearrange("(c p) t -> p c t", p=128), us[:, :, :],
                    reads=("us",))
    return L


def build_a1b(L, u, co):
    r = L.rec
    NT = 4096
    qkvk = L.dram_in("qkvk", [D, 3072], BF16)
    bdm = L.dram_in("bdm", [128, 128], BF16)
    idf_d = L.dram_in("idf_in", [128, 128], F32)

    consts = emit_const_setup(L)
    ones_col, ones_row, eps_ln = consts
    wq = L.sb("wq", [128, 8, 3072], BF16)
    ug = L.sb("ug", [128, 8, 1024], BF16)
    qcm = L.sb("qcm", [128, 8, 1024], BF16)
    kcm = L.sb("kcm", [128, 8, 1024], BF16)
    Vt = L.sb("Vt", [128, 8, 1040], BF16)
    ctxg = L.sb("ctxg", [128, 8, 1024], BF16)
    pb = L.sb("pb", [128, 2, 128], BF16)
    mask = L.sb("mask", [128, 128], BF16)
    idf = L.sb("idf", [128, 128], F32)
    mx1 = L.sb("mx1", [128, 1], F32)
    m1 = L.sb("m1", [1, 1], F32)
    nmx = L.sb("nmx", [128, 1], F32)
    r64 = L.sb("r64", [1, 128], F32)
    nb = L.sb("nb", [64, 128], F32)

    psAll = L.ps("psAll", [128, 4096], F32)
    pS0 = psAll[:, 0:128]
    pS1 = psAll[:, 2048:2176]
    pV = psAll[:, 512:1024]
    pAV = psAll[0:65, 1024:1152]
    pBC = psAll[0:64, 1536:1664]
    pT1 = psAll[0:1, 2560:2688]
    pM = psAll[0:128, 2816:2817]
    pP = psAll[:, 3072:3584]

    r.load(wq[:, :, :], qkvk.rearrange("(c p) m -> p c m", p=128), "wq")
    r.load(mask[:], bdm[:, :], "mask")
    r.load(idf[:], idf_d[:, :], "idf")
    r.dve("memset", ap=Vt[:, :, :], constant=0.0, writes=("Vt",))
    for h in range(16):
        r.dve("memset", ap=Vt[:, :, h * 65 + 64:h * 65 + 65], constant=1.0,
              writes=("Vt",))

    for g in range(4):
        gsl = slice(g * 1024, (g + 1) * 1024)
        r.load(ug[:, :, :], u[:, gsl].rearrange("(c p) t -> p c t", p=128), "ug")
        for qk in range(2):
            dst = qcm if qk == 0 else kcm
            dkey = "qcm" if qk == 0 else "kcm"
            for m in range(8):
                for n2 in range(2):
                    nsl = slice(n2 * 512, (n2 + 1) * 512)
                    for c in range(8):
                        r.pe("matmul", out=pP,
                             lhsT=wq[:, c, qk * 1024 + m * 128:qk * 1024 + (m + 1) * 128],
                             rhs=ug[:, c, nsl], start=(c == 0), stop=(c == 7),
                             reads=("wq", "ug"), writes=("PS",))
                    if qk == 0:
                        r.dve("tensor_scalar_mul", out=dst[:, m, nsl], in0=pP,
                              scalar1=0.125, reads=("PS",), writes=(dkey,))
                    else:
                        r.act("activation", out=dst[:, m, nsl], in_=pP,
                              func=AF.Copy, reads=("PS",), writes=(dkey,))
        for tt in range(8):
            tsl = slice(tt * 128, (tt + 1) * 128)
            for half in range(2):
                vsl = slice(2048 + half * 512, 2048 + (half + 1) * 512)
                for c in range(8):
                    r.pe("matmul", out=pV, lhsT=ug[:, c, tsl],
                         rhs=wq[:, c, vsl], start=(c == 0), stop=(c == 7),
                         reads=("ug", "wq"), writes=("PS",))
                for h in range(half * 8, half * 8 + 8):
                    r.act("activation", out=Vt[:, tt, h * 65:h * 65 + 64],
                          in_=pV[:, (h - half * 8) * 64:(h - half * 8) * 64 + 64],
                          func=AF.Copy, reads=("PS",), writes=("Vt",))
        for tt in range(8):
            tsl = slice(tt * 128, (tt + 1) * 128)
            for m in range(8):
                r.pe("matmul", out=pS0, lhsT=kcm[0:64, m, tsl],
                     rhs=qcm[0:64, m, tsl], start=True, stop=True,
                     tile_position=(0, 0), reads=("kcm", "qcm"), writes=("PS",))
                r.pe("matmul", out=pS1, lhsT=kcm[64:128, m, tsl],
                     rhs=qcm[64:128, m, tsl], start=True, stop=True,
                     tile_position=(64, 0), reads=("kcm", "qcm"), writes=("PS",))
                for h2, pS in ((0, pS0), (1, pS1)):
                    h = 2 * m + h2
                    r.dve("tensor_reduce", out=mx1[:, 0:1], in_=pS, axis=AX.X,
                          op=ALU.max, reads=("PS",), writes=("mx1",))
                    r.pe("transpose", out=pT1, in_=mx1[:, 0:1], identity=idf[:],
                         reads=("mx1", "idf"), writes=("PS",))
                    r.dve("tensor_reduce", out=m1[:, 0:1], in_=pT1[0:1, :],
                          axis=AX.X, op=ALU.max, reads=("PS",), writes=("m1",))
                    r.pe("matmul", out=pM, lhsT=ones_row[:], rhs=m1[:],
                         start=True, stop=True, reads=("ones_row", "m1"),
                         writes=("PS",))
                    r.dve("tensor_scalar_mul", out=nmx[:], in0=pM, scalar1=-1.0,
                          reads=("PS",), writes=("nmx",))
                    r.act("activation", out=pb[:, h2, :], in_=pS, func=AF.Exp,
                          bias=nmx[:], reads=("PS", "nmx"), writes=("pb",))
                    r.dve("tensor_tensor", out=pb[:, h2, :], in0=pb[:, h2, :],
                          in1=mask[:], op=ALU.mult, reads=("pb", "mask"),
                          writes=("pb",))
                    r.pe("matmul", out=pAV,
                         lhsT=Vt[:, tt, h * 65:h * 65 + 65],
                         rhs=pb[:, h2, :], start=True, stop=True,
                         reads=("Vt", "pb"), writes=("PS",))
                    r.dve("reciprocal", out=r64[:], in_=pAV[64:65, :],
                          reads=("PS",), writes=("r64",))
                    r.pe("matmul", out=pBC, lhsT=ones_row[:, 0:64], rhs=r64[:],
                         start=True, stop=True, reads=("ones_row", "r64"),
                         writes=("PS",))
                    r.act("activation", out=nb[:], in_=pBC, func=AF.Copy,
                          reads=("PS",), writes=("nb",))
                    r.dve("tensor_tensor",
                          out=ctxg[h2 * 64:(h2 + 1) * 64, m, tsl],
                          in0=pAV[0:64, :], in1=nb[:], op=ALU.mult,
                          reads=("PS", "nb"), writes=("ctxg",))
        r.store(co[:, gsl].rearrange("(c p) t -> p c t", p=128), ctxg[:, :, :],
                reads=("ctxg",))
    return L


def build_a1c(L, x, ci, xo):
    r = L.rec
    wout = L.dram_in("wout", [D, D], BF16)
    wsb = L.sb("wsb", [128, 8, D], BF16)
    xs = L.sb("xs", [128, 8, 512], F32)
    cs = L.sb("cs", [128, 8, 512], BF16)
    pO = L.ps("pO", [128, 512], F32)
    r.load(wsb[:, :, :], wout.rearrange("(c p) m -> p c m", p=128), "wsb")
    for b in range(B_PER_CORE):
        for t0 in range(4):
            sl = slice(t0 * 512, (t0 + 1) * 512)
            r.load(xs[:, :, :], x[b, :, sl].rearrange("(c p) t -> p c t", p=128), "xs")
            r.load(cs[:, :, :], ci[b, :, sl].rearrange("(c p) t -> p c t", p=128), "cs")
            for mo in range(8):
                for c in range(8):
                    r.pe("matmul", out=pO[:], lhsT=wsb[:, c, mo * 128:(mo + 1) * 128],
                         rhs=cs[:, c, :], start=(c == 0), stop=(c == 7),
                         reads=("wsb", "cs"), writes=("pO",))
                r.dve("tensor_tensor", out=xs[:, mo, :], in0=xs[:, mo, :],
                      in1=pO[:], op=ALU.add, reads=("xs", "pO"), writes=("xs",))
            r.store(xo[b, :, sl].rearrange("(c p) t -> p c t", p=128), xs[:, :, :],
                    reads=("xs",))
    return L


def launch_attna1(x_shards, inp):
    La = Launch("a1a")
    xa = La.dram_in("x", [B_PER_CORE, D, T], F32)
    uoa = La.dram_out("uo", [B_PER_CORE, D, T], BF16)
    build_a1a(La, xa, uoa)
    wa = {"lsc": host_bias(inp["L1_attn_a_ln_scale"]),
          "lbi": host_bias(inp["L1_attn_a_ln_bias"])}
    res = La.run([{"x": x_shards[c], **wa} for c in range(NCORES)])
    u = np.concatenate([res[c]["uo"] for c in range(NCORES)], axis=0)
    up = np.transpose(u, (1, 2, 0))
    u_shards = [np.ascontiguousarray(
        up[:, c * 256:(c + 1) * 256, :].reshape(D, 4096)) for c in range(NCORES)]
    Lb = Launch("a1b")
    ub = Lb.dram_in("u", [D, 4096], BF16)
    cob = Lb.dram_out("co", [D, 4096], BF16)
    build_a1b(Lb, ub, cob)
    qkvk = np.ascontiguousarray(
        np.asarray(inp["L1_attn_a_qkv_kernel"], np.float32).reshape(D, 3072)).astype(BF)
    ki, qi = np.meshgrid(np.arange(128), np.arange(128), indexing="ij")
    bdm = ((ki // 16 == qi // 16) & (ki % 16 <= qi % 16)).astype(BF)
    wb = {"qkvk": qkvk, "bdm": bdm, "idf_in": np.eye(128, dtype=np.float32)}
    resb = Lb.run([{"u": u_shards[c], **wb} for c in range(NCORES)])
    ctx = np.concatenate([resb[c]["co"].reshape(D, 256, 16)
                          for c in range(NCORES)], axis=1)
    ctx_b = np.ascontiguousarray(np.transpose(ctx, (2, 0, 1)))
    Lc = Launch("a1c")
    xc = Lc.dram_in("x", [B_PER_CORE, D, T], F32)
    cic = Lc.dram_in("ci", [B_PER_CORE, D, T], BF16)
    xoc = Lc.dram_out("xo", [B_PER_CORE, D, T], F32)
    build_a1c(Lc, xc, cic, xoc)
    wc = {"wout": np.ascontiguousarray(inp["L1_attn_a_out_kernel"]).astype(BF)}
    resc = Lc.run([{"x": x_shards[c], "ci": ctx_b[c * 2:(c + 1) * 2], **wc}
                   for c in range(NCORES)])
    return [resc[c]["xo"] for c in range(NCORES)]


I32 = mybir.dt.int32
try:
    IOA = bass.IndirectOffsetOnAxis
except AttributeError:
    from concourse.bass_types import IndirectOffsetOnAxis as IOA


def build_a1a_f(L, x, u_tok):
    r = L.rec
    lsc = load_bias_col(L, "lsc", D)
    lbi = load_bias_col(L, "lbi", D)
    idb_d = L.dram_in("identB", [128, 128], BF16)
    consts = emit_const_setup(L)
    psAll = L.ps("psAll", [128, 3072], F32)
    pTb = L.ps("pTb", [128, 512], BF16)
    ensure_ln_scratch(L, 512, views={
        "pstat_a": psAll[0:1, 0:512],
        "pstat_b": psAll[0:1, 512:1024],
        "pbc": psAll[:, 1024:2048],
    })
    xs = L.sb("xs", [128, 8, 512], F32)
    us = L.sb("us", [128, 8, 512], BF16)
    ut = L.sb("ut", [128, 1024], BF16)
    idb = L.sb("idb", [128, 128], BF16)
    r.load(idb[:], idb_d[:, :], "idb")
    for b in range(B_PER_CORE):
        for t0 in range(4):
            sl = slice(t0 * 512, (t0 + 1) * 512)
            r.load(xs[:, :, :], x[b, :, sl].rearrange("(c p) t -> p c t", p=128), "xs")
            emit_ln_cm(L, xs, "xs", xs, "xs", 512, consts, psk=("PS",))
            emit_ln_cm(L, xs, "xs", us, "us", 512, consts,
                       gain=lsc, bias=lbi, gkey=("b_lsc", "b_lbi"), psk=("PS",))
            for tt in range(4):
                for c in range(8):
                    r.pe("transpose", out=pTb[:, 0:128],
                         in_=us[:, c, tt * 128:(tt + 1) * 128], identity=idb[:],
                         reads=("us", "idb"), writes=("PS",))
                    r.act("activation", out=ut[:, c * 128:(c + 1) * 128],
                          in_=pTb[:, 0:128], func=AF.Copy, reads=("PS",),
                          writes=("ut",))
                r.store(u_tok[b, t0 * 512 + tt * 128:t0 * 512 + (tt + 1) * 128, :],
                        ut[:, :], reads=("ut",), key="AGUI")
    return L


def build_a1b_f(L, agu_out, agc_in):
    r = L.rec
    qkvk = L.dram_in("qkvk", [D, 3072], BF16)
    bdm = L.dram_in("bdm", [128, 128], BF16)
    idf_d = L.dram_in("idf_in", [128, 128], F32)
    idb_d = L.dram_in("identB", [128, 128], BF16)
    idxu_d = L.dram_in("idxu", [64, 64], I32)

    consts = emit_const_setup(L)
    ones_col, ones_row, eps_ln = consts
    wq = L.sb("wq", [128, 8, 3072], BF16)
    ug = L.sb("ug", [128, 8, 1024], BF16)
    st64 = L.sb("st64", [64, 1024], BF16)
    qcm = L.sb("qcm", [128, 8, 1024], BF16)
    kcm = L.sb("kcm", [128, 8, 1024], BF16)
    Vt = L.sb("Vt", [128, 8, 1040], BF16)
    ctxg = L.sb("ctxg", [128, 8, 1024], BF16)
    ctok = L.sb("ctok", [128, 1024], BF16)
    pb = L.sb("pb", [128, 2, 128], BF16)
    mask = L.sb("mask", [128, 128], BF16)
    idf = L.sb("idf", [128, 128], F32)
    idb = L.sb("idb", [128, 128], BF16)
    idxu = L.sb("idxu", [64, 64], I32)
    mx1 = L.sb("mx1", [128, 1], F32)
    m1 = L.sb("m1", [1, 1], F32)
    nmx = L.sb("nmx", [128, 1], F32)
    r64 = L.sb("r64", [1, 128], F32)
    nb = L.sb("nb", [64, 128], F32)

    psAll = L.ps("psAll", [128, 3584], F32)
    pTb = L.ps("pTb", [128, 1024], BF16)
    pS0 = psAll[:, 0:128]
    pS1 = psAll[:, 2048:2176]
    pV = psAll[:, 512:1024]
    pAV = psAll[0:65, 1024:1152]
    pBC = psAll[0:64, 1536:1664]
    pT1 = psAll[0:1, 2560:2688]
    pM = psAll[0:128, 2816:2817]
    pP = psAll[:, 3072:3584]

    r.load(wq[:, :, :], qkvk.rearrange("(c p) m -> p c m", p=128), "wq")
    r.load(mask[:], bdm[:, :], "mask")
    r.load(idf[:], idf_d[:, :], "idf")
    r.load(idb[:], idb_d[:, :], "idb")
    r.load(idxu[:], idxu_d[:, :], "idxu")
    r.dve("memset", ap=Vt[:, :, :], constant=0.0, writes=("Vt",))
    for h in range(16):
        r.dve("memset", ap=Vt[:, :, h * 65 + 64:h * 65 + 65], constant=1.0,
              writes=("Vt",))

    for g in range(4):
        for s in range(16):
            r.op("pool", "indirect_dma_start", out=st64[:, :], out_offset=None,
                 in_=agu_out[:, :], in_offset=IOA(idxu[:, g * 16 + s:g * 16 + s + 1], 0),
                 reads=("AGUO", "idxu"), writes=("st64",))
            ugv = ug[:].rearrange("p c (pp s) -> p c pp s", s=16)
            for c in range(8):
                r.pe("transpose", out=pTb[:, 0:64],
                     in_=st64[:, c * 128:(c + 1) * 128], identity=idb[0:64, 0:64],
                     reads=("st64", "idb"), writes=("PS",))
                r.act("activation", out=ugv[:, c, :, s], in_=pTb[:, 0:64],
                      func=AF.Copy, reads=("PS",), writes=("ug",))
        for qk in range(2):
            dst = qcm if qk == 0 else kcm
            dkey = "qcm" if qk == 0 else "kcm"
            for m in range(8):
                for n2 in range(2):
                    nsl = slice(n2 * 512, (n2 + 1) * 512)
                    for c in range(8):
                        r.pe("matmul", out=pP,
                             lhsT=wq[:, c, qk * 1024 + m * 128:qk * 1024 + (m + 1) * 128],
                             rhs=ug[:, c, nsl], start=(c == 0), stop=(c == 7),
                             reads=("wq", "ug"), writes=("PS",))
                    if qk == 0:
                        r.dve("tensor_scalar_mul", out=dst[:, m, nsl], in0=pP,
                              scalar1=0.125, reads=("PS",), writes=(dkey,))
                    else:
                        r.act("activation", out=dst[:, m, nsl], in_=pP,
                              func=AF.Copy, reads=("PS",), writes=(dkey,))
        for tt in range(8):
            tsl = slice(tt * 128, (tt + 1) * 128)
            for half in range(2):
                vsl = slice(2048 + half * 512, 2048 + (half + 1) * 512)
                for c in range(8):
                    r.pe("matmul", out=pV, lhsT=ug[:, c, tsl],
                         rhs=wq[:, c, vsl], start=(c == 0), stop=(c == 7),
                         reads=("ug", "wq"), writes=("PS",))
                for h in range(half * 8, half * 8 + 8):
                    r.act("activation", out=Vt[:, tt, h * 65:h * 65 + 64],
                          in_=pV[:, (h - half * 8) * 64:(h - half * 8) * 64 + 64],
                          func=AF.Copy, reads=("PS",), writes=("Vt",))
        for tt in range(8):
            tsl = slice(tt * 128, (tt + 1) * 128)
            for m in range(8):
                r.pe("matmul", out=pS0, lhsT=kcm[0:64, m, tsl],
                     rhs=qcm[0:64, m, tsl], start=True, stop=True,
                     tile_position=(0, 0), reads=("kcm", "qcm"), writes=("PS",))
                r.pe("matmul", out=pS1, lhsT=kcm[64:128, m, tsl],
                     rhs=qcm[64:128, m, tsl], start=True, stop=True,
                     tile_position=(64, 0), reads=("kcm", "qcm"), writes=("PS",))
                for h2, pS in ((0, pS0), (1, pS1)):
                    h = 2 * m + h2
                    r.dve("tensor_reduce", out=mx1[:, 0:1], in_=pS, axis=AX.X,
                          op=ALU.max, reads=("PS",), writes=("mx1",))
                    r.pe("transpose", out=pT1, in_=mx1[:, 0:1], identity=idf[:],
                         reads=("mx1", "idf"), writes=("PS",))
                    r.dve("tensor_reduce", out=m1[:, 0:1], in_=pT1[0:1, :],
                          axis=AX.X, op=ALU.max, reads=("PS",), writes=("m1",))
                    r.pe("matmul", out=pM, lhsT=ones_row[:], rhs=m1[:],
                         start=True, stop=True, reads=("ones_row", "m1"),
                         writes=("PS",))
                    r.dve("tensor_scalar_mul", out=nmx[:], in0=pM, scalar1=-1.0,
                          reads=("PS",), writes=("nmx",))
                    r.act("activation", out=pb[:, h2, :], in_=pS, func=AF.Exp,
                          bias=nmx[:], reads=("PS", "nmx"), writes=("pb",))
                    r.dve("tensor_tensor", out=pb[:, h2, :], in0=pb[:, h2, :],
                          in1=mask[:], op=ALU.mult, reads=("pb", "mask"),
                          writes=("pb",))
                    r.pe("matmul", out=pAV,
                         lhsT=Vt[:, tt, h * 65:h * 65 + 65],
                         rhs=pb[:, h2, :], start=True, stop=True,
                         reads=("Vt", "pb"), writes=("PS",))
                    r.dve("reciprocal", out=r64[:], in_=pAV[64:65, :],
                          reads=("PS",), writes=("r64",))
                    r.pe("matmul", out=pBC, lhsT=ones_row[:, 0:64], rhs=r64[:],
                         start=True, stop=True, reads=("ones_row", "r64"),
                         writes=("PS",))
                    r.act("activation", out=nb[:], in_=pBC, func=AF.Copy,
                          reads=("PS",), writes=("nb",))
                    r.dve("tensor_tensor",
                          out=ctxg[h2 * 64:(h2 + 1) * 64, m, tsl],
                          in0=pAV[0:64, :], in1=nb[:], op=ALU.mult,
                          reads=("PS", "nb"), writes=("ctxg",))
        for tt in range(8):
            tsl = slice(tt * 128, (tt + 1) * 128)
            for c in range(8):
                r.pe("transpose", out=pTb[:, 0:128], in_=ctxg[:, c, tsl],
                     identity=idb[:], reads=("ctxg", "idb"), writes=("PS",))
                r.act("activation", out=ctok[:, c * 128:(c + 1) * 128],
                      in_=pTb[:, 0:128], func=AF.Copy, reads=("PS",),
                      writes=("ctok",))
            r.store(agc_in[g * 1024 + tt * 128:g * 1024 + (tt + 1) * 128, :],
                    ctok[:, :], reads=("ctok",), key="AGCI")
    return L


def build_a1c_f(L, x, agc_out, xo):
    r = L.rec
    wout = L.dram_in("wout", [D, D], BF16)
    idxc_d = L.dram_in("idxc", [128, 32], I32)
    idb_d = L.dram_in("identB", [128, 128], BF16)
    wsb = L.sb("wsb", [128, 8, D], BF16)
    xs = L.sb("xs", [128, 8, 512], F32)
    cs = L.sb("cs", [128, 8, 512], BF16)
    ctok = L.sb("ctok", [128, 1024], BF16)
    idxc = L.sb("idxc", [128, 32], I32)
    idb = L.sb("idb", [128, 128], BF16)
    psAll = L.ps("psAll", [128, 1024], F32)
    pTb = L.ps("pTb", [128, 512], BF16)
    pO = psAll[:, 0:512]
    r.load(wsb[:, :, :], wout.rearrange("(c p) m -> p c m", p=128), "wsb")
    r.load(idxc[:], idxc_d[:, :], "idxc")
    r.load(idb[:], idb_d[:, :], "idb")
    for b in range(B_PER_CORE):
        for t0 in range(4):
            sl = slice(t0 * 512, (t0 + 1) * 512)
            r.load(xs[:, :, :], x[b, :, sl].rearrange("(c p) t -> p c t", p=128), "xs")
            for half4 in range(4):
                tglob = t0 * 4 + half4
                r_blk, hh = tglob // 2, tglob % 2
                col = b * 16 + r_blk * 2 + hh
                r.op("pool", "indirect_dma_start", out=ctok[:, :],
                     out_offset=None, in_=agc_out[:, :],
                     in_offset=IOA(idxc[:, col:col + 1], 0),
                     reads=("AGCO", "idxc"), writes=("ctok",))
                for c in range(8):
                    r.pe("transpose", out=pTb[:, 0:128],
                         in_=ctok[:, c * 128:(c + 1) * 128], identity=idb[:],
                         reads=("ctok", "idb"), writes=("PS",))
                    r.act("activation",
                          out=cs[:, c, half4 * 128:(half4 + 1) * 128],
                          in_=pTb[:, 0:128], func=AF.Copy, reads=("PS",),
                          writes=("cs",))
            for mo in range(8):
                for c in range(8):
                    r.pe("matmul", out=pO, lhsT=wsb[:, c, mo * 128:(mo + 1) * 128],
                         rhs=cs[:, c, :], start=(c == 0), stop=(c == 7),
                         reads=("wsb", "cs"), writes=("PS",))
                r.dve("tensor_tensor", out=xs[:, mo, :], in0=xs[:, mo, :],
                      in1=pO, op=ALU.add, reads=("xs", "PS"), writes=("xs",))
            r.store(xo[b, :, sl].rearrange("(c p) t -> p c t", p=128), xs[:, :, :],
                    reads=("xs",))
    return L


def build_fused(L):
    x0 = L.dram_in("x", [B_PER_CORE, D, T], F32)
    out = L.dram_out("out", [B_PER_CORE, D, T], F32)
    xA = L.dram_tmp("xA", [B_PER_CORE, D, T], F32)
    xB = L.dram_tmp("xB", [B_PER_CORE, D, T], F32)
    agu_in = L.dram_tmp("agu_in", [B_PER_CORE, T, D], BF16)
    agu_out = L.dram_tmp("agu_out", [16 * T, D], BF16)
    agc_in = L.dram_tmp("agc_in", [4096, D], BF16)
    agc_out = L.dram_tmp("agc_out", [8 * 4096, D], BF16)
    r = L.rec
    with L.scope("s0"):
        build_gla(L, x0, xA)
    with L.scope("s1"):
        build_l0ffn(L, xA, xB)
    with L.scope("s2"):
        build_attnb(L, xB, xA)
    with L.scope("s3"):
        build_l0ffn(L, xA, xB)
    with L.scope("s4"):
        build_a1a_f(L, xB, agu_in)
    r.op("pool", "collective_compute", kind="AllGather", op=ALU.bypass,
         replica_groups=[list(range(NCORES))],
         ins=[agu_in.rearrange("b t d -> (b t) d").opt()], outs=[agu_out.opt()],
         reads=("AGUI",), writes=("AGUO",))
    with L.scope("s5"):
        build_a1b_f(L, agu_out, agc_in)
    r.op("pool", "collective_compute", kind="AllGather", op=ALU.bypass,
         replica_groups=[list(range(NCORES))],
         ins=[agc_in.opt()], outs=[agc_out.opt()],
         reads=("AGCI",), writes=("AGCO",))
    with L.scope("s6"):
        build_a1c_f(L, xB, agc_out, xA)
    with L.scope("s7"):
        build_ffn1(L, xA, xB)
    with L.scope("s8"):
        build_rwkv(L, xB, xA)
    with L.scope("s9"):
        build_ffn1(L, xA, out)
    return L


def host_prep_fused(inp, core):
    w = {}
    w.update(host_prep_gla(inp))
    w.update(host_prep_l0ffn(inp))
    w.update(host_prep_attnb(inp))
    w.update(host_prep_ffn1(inp))
    w.update(host_prep_rwkv(inp))
    w["lsc"] = host_bias(inp["L1_attn_a_ln_scale"])
    w["lbi"] = host_bias(inp["L1_attn_a_ln_bias"])
    w["qkvk"] = np.ascontiguousarray(
        np.asarray(inp["L1_attn_a_qkv_kernel"], np.float32).reshape(D, 3072)).astype(BF)
    ki, qi = np.meshgrid(np.arange(128), np.arange(128), indexing="ij")
    w["bdm"] = ((ki // 16 == qi // 16) & (ki % 16 <= qi % 16)).astype(BF)
    w["identB"] = np.eye(128).astype(BF)
    w["wout"] = np.ascontiguousarray(inp["L1_attn_a_out_kernel"]).astype(BF)
    idxu = np.zeros((64, 64), np.int32)
    for g in range(4):
        for s in range(16):
            pp = np.arange(64)
            idxu[:, g * 16 + s] = s * 2048 + 256 * core + 64 * g + pp
    w["idxu"] = idxu
    idxc = np.zeros((128, 32), np.int32)
    for b in range(2):
        for rb in range(8):
            for hh in range(2):
                pp = np.arange(128)
                idxc[:, b * 16 + rb * 2 + hh] = (
                    rb * 4096 + (128 * hh + pp) * 16 + (2 * core + b))
    w["idxc"] = idxc
    return w


def kernel_fused(inputs):
    x = to_cm_shards(inputs["L0_attn_a_x"])
    L = Launch("fused")
    build_fused(L)
    in_maps = [{"x": x[c], **host_prep_fused(inputs, c)} for c in range(NCORES)]
    res = L.run(in_maps)
    return from_cm_shards([res[c]["out"] for c in range(NCORES)])



def to_cm_shards(x):
    xc = np.ascontiguousarray(np.transpose(np.asarray(x, np.float32), (0, 2, 1)))
    return [xc[c * 2:(c + 1) * 2] for c in range(NCORES)]


def from_cm_shards(shards):
    xc = np.concatenate(shards, axis=0)
    return np.ascontiguousarray(np.transpose(xc, (0, 2, 1)))


def kernel(**inputs):
    return kernel_fused(inputs)


def kernel_multilaunch(**inputs):
    x = to_cm_shards(inputs["L0_attn_a_x"])
    x = launch_gla(x, inputs)
    x = launch_l0ffn(x, inputs)
    x = launch_attnb(x, inputs)
    x = launch_l0ffn(x, inputs)
    x = launch_attna1(x, inputs)
    x = launch_ffn1(x, inputs)
    x = launch_rwkv(x, inputs)
    x = launch_ffn1(x, inputs)
    return from_cm_shards(x)


# version 13
# speedup vs baseline: 1.2323x; 1.2323x over previous
import numpy as np
import ml_dtypes
from contextlib import ExitStack
import concourse.bass as bass
import concourse.mybir as mybir
from concourse.bass_utils import run_bass_kernel_spmd

BF = ml_dtypes.bfloat16
F32 = mybir.dt.float32
BF16 = mybir.dt.bfloat16
AF = mybir.ActivationFunctionType
ALU = mybir.AluOpType
AX = mybir.AxisListType

ENGINES = ("pe", "act", "dve", "pool", "sp")
BLOCK_ATTR = {"pe": "tensor", "act": "scalar", "dve": "vector",
              "pool": "gpsimd", "sp": "sync"}
DMA_QUEUES = ("pool", "sp")

NCORES = 8
B_PER_CORE = 2
T = 2048
D = 1024


class Rec:

    def __init__(self, nc):
        self.nc = nc
        self.ops = []

    def barrier(self):
        self.ops.append(["BARRIER"])

    @staticmethod
    def _psum_keys(ap):
        try:
            t = ap.tensor
            if t.__class__.__name__ != "PSumTensorHandle":
                return ()
            dims = list(ap.ap)
            free_len = None
            shape = list(t.shape)
            free_len = 1
            for d in shape[1:]:
                free_len *= d
            off = ap.offset % free_len if free_len else 0
            ext = 1
            for step, count in dims[1:]:
                ext += (count - 1) * abs(step)
            isz = 2 if str(t.dtype) == "dt.bfloat16" else 4
            b0 = (off * isz) // 2048
            b1 = ((off + ext) * isz - 1) // 2048
            nm = t.name
            return tuple(f"{nm}:b{i}" for i in range(b0, b1 + 1))
        except Exception:
            return ("PSANY",)

    def op(self, eng, method, *, reads=(), writes=(), **kwargs):
        inc = 16 if method in ("dma_start", "dma_start_transpose", "indirect_dma_start") else 1
        reads = tuple(k for k in reads if k != "PS")
        writes = tuple(k for k in writes if k != "PS")
        wk = ("out", "ap", "accum_out")
        for kname, v in kwargs.items():
            if v.__class__.__name__ == "AP":
                pk = self._psum_keys(v)
                if pk:
                    if kname in wk:
                        writes = writes + pk
                    else:
                        reads = reads + pk
        def fn(engine, method=method, kwargs=kwargs):
            return getattr(engine, method)(**kwargs)
        self.ops.append([eng, fn, tuple(reads), tuple(writes), inc])

    def pe(self, method, **kw):   self.op("pe", method, **kw)
    def act(self, method, **kw):  self.op("act", method, **kw)
    def dve(self, method, **kw):  self.op("dve", method, **kw)
    def pool(self, method, **kw): self.op("pool", method, **kw)

    def load(self, out, in_, key, reads=(), q="sp"):
        self.op(q, "dma_start", out=out, in_=in_, reads=reads, writes=(key,))

    def store(self, out, in_, reads, key="::out", q="pool"):
        self.op(q, "dma_start", out=out, in_=in_, reads=tuple(reads), writes=(key,))

    def finalize(self, ctx):
        nc = self.nc
        sems = {e: ctx.enter_context(nc.semaphore(f"sem_{e}"))
                for e in ENGINES}
        cum = {e: 0 for e in ENGINES}
        last_writer = {}
        readers = {}
        waits_for_op = []
        barrier_snap = None
        for opi in self.ops:
            if opi[0] == "BARRIER":
                barrier_snap = dict(cum)
                waits_for_op.append(None)
                continue
            eng, fn, reads, writes, inc = opi
            w = {}
            if barrier_snap:
                for e2, c2 in barrier_snap.items():
                    if c2 > 0 and (e2 != eng or eng != "pe") and c2 > w.get(e2, 0):
                        w[e2] = c2
            def need(e2, c2):
                if e2 != eng or eng != "pe":
                    if c2 > w.get(e2, 0):
                        w[e2] = c2
            for k in reads:
                lw = last_writer.get(k)
                if lw:
                    need(*lw)
            for k in writes:
                lw = last_writer.get(k)
                if lw:
                    need(*lw)
                for e2, c2 in readers.get(k, {}).items():
                    need(e2, c2)
            cum[eng] += inc
            for k in reads:
                readers.setdefault(k, {})[eng] = cum[eng]
            for k in writes:
                last_writer[k] = (eng, cum[eng])
                readers[k] = {}
            if eng in w:
                pass
            waits_for_op.append(w)
        totals = dict(cum)
        block = ctx.enter_context(nc.Block())
        for eng_name in ENGINES:
            idxs = [i for i, o in enumerate(self.ops) if o[0] == eng_name]
            if not idxs and eng_name != "pool":
                continue

            def emit(engine, idxs=idxs, eng_name=eng_name):
                satisfied = {e: 0 for e in ENGINES}
                own = 0
                for i in idxs:
                    _, fn, _, _, inc = self.ops[i]
                    if eng_name in DMA_QUEUES and own > 0 and satisfied.get(eng_name, 0) < own:
                        engine.wait_ge(sems[eng_name], own)
                        satisfied[eng_name] = own
                    for e2, v in sorted(waits_for_op[i].items()):
                        if satisfied.get(e2, 0) < v:
                            engine.wait_ge(sems[e2], v)
                            satisfied[e2] = v
                    try:
                        inst = fn(engine)
                    except Exception as ex:
                        raise RuntimeError(f"op #{i} eng={eng_name} tag={self.ops[i][2]}/{self.ops[i][3]}: {ex}") from ex
                    inst.then_inc(sems[eng_name], inc)
                    own += inc
                if eng_name == "pool":
                    engine.wait_ge(sems["pool"], totals["pool"])

            getattr(block, BLOCK_ATTR[eng_name])(emit)


class Launch:

    def __init__(self, name):
        self.nc = bass.Bass("TRN2", target_bir_lowering=False)
        self.rec = Rec(self.nc)
        self.ctx = ExitStack()
        self.ins = {}
        self.outs = {}
        self._sb = {}
        self._ps = {}
        self._scope = None
        self._prefix = ""

    def scope(self, prefix):
        from contextlib import contextmanager

        @contextmanager
        def _s():
            old_stack, old_pref = self._scope, self._prefix
            self._scope = ExitStack()
            self._prefix = prefix + "."
            try:
                yield
            finally:
                self.rec.barrier()
                self._scope.close()
                self._scope, self._prefix = old_stack, old_pref
        return _s()

    def dram_in(self, name, shape, dtype=F32):
        if name in self.ins:
            return self.ins[name]
        ap = self.nc.dram_tensor(name, list(shape), dtype, kind="ExternalInput").ap()
        self.ins[name] = ap
        return ap

    def dram_tmp(self, name, shape, dtype=F32):
        return self.nc.dram_tensor(name, list(shape), dtype, kind="Internal").ap()

    def dram_out(self, name, shape, dtype=F32):
        ap = self.nc.dram_tensor(name, list(shape), dtype, kind="ExternalOutput").ap()
        self.outs[name] = ap
        return ap

    def sb(self, name, shape, dtype):
        stack = self._scope if self._scope is not None else self.ctx
        t = stack.enter_context(
            self.nc.sbuf_tensor(self._prefix + name, list(shape), dtype))
        self._sb[name] = t
        return t

    def ps(self, name, shape, dtype=F32):
        stack = self._scope if self._scope is not None else self.ctx
        t = stack.enter_context(
            self.nc.psum_tensor(self._prefix + name, list(shape), dtype))
        self._ps[name] = t
        return t

    def run(self, in_maps, **kw):
        self.rec.finalize(self.ctx)
        res = run_bass_kernel_spmd(self.nc, in_maps, core_ids=list(range(NCORES)), **kw)
        self.ctx.close()
        return res.results



def emit_const_setup(L):
    r = L.rec
    ones_col = L.sb("ones_col", [128, 1], F32)
    ones_row = L.sb("ones_row", [1, 128], F32)
    eps_ln = L.sb("eps_ln", [1, 1], F32)
    r.dve("memset", ap=ones_col[:], constant=1.0, writes=("ones_col",))
    r.dve("memset", ap=ones_row[:], constant=1.0, writes=("ones_row",))
    r.dve("memset", ap=eps_ln[:], constant=1e-5, writes=("eps_ln",))
    return ones_col, ones_row, eps_ln


def emit_ln_cm(L, xs, xkey, ys, ykey, Tc, consts, nchunk=8, eps_key="eps_ln",
               gain=None, bias=None, gkey=(), psk=()):
    r = L.rec
    ones_col, ones_row, eps_ln = consts
    pstat_a = L._ps["pstat_a"]; pstat_b = L._ps["pstat_b"]; pbc = L._ps["pbc"]
    lnrow = L._sb["lnrow"]; sqt = L._sb["sqt"]
    if isinstance(xs, (list, tuple)):
        xin = lambda c: xs[c]
    else:
        xin = lambda c: xs[:, c, :Tc]
    for c in range(nchunk):
        r.act("activation", out=sqt[:, :Tc], in_=xin(c), func=AF.Square,
              reads=(xkey,), writes=("sqt",))
        r.pe("matmul", out=pstat_a[:, 0:Tc], lhsT=ones_col[:], rhs=xin(c),
             start=(c == 0), stop=(c == nchunk - 1),
             reads=("ones_col", xkey), writes=("pstat0",) + psk)
        r.pe("matmul", out=pstat_b[:, 0:Tc], lhsT=ones_col[:], rhs=sqt[:, :Tc],
             start=(c == 0), stop=(c == nchunk - 1),
             reads=("ones_col", "sqt"), writes=("pstat1",) + psk)
    r.dve("tensor_scalar_mul", out=lnrow[:, 0:Tc], in0=pstat_a[:, 0:Tc],
          scalar1=1.0 / (128 * nchunk), reads=("pstat0",) + psk, writes=("lnmu",))
    r.dve("tensor_scalar_mul", out=lnrow[:, 2 * Tc:3 * Tc], in0=pstat_b[:, 0:Tc],
          scalar1=1.0 / (128 * nchunk), reads=("pstat1",) + psk, writes=("lnex2",))
    r.dve("tensor_tensor", out=lnrow[:, 3 * Tc:4 * Tc], in0=lnrow[:, 0:Tc],
          in1=lnrow[:, 0:Tc], op=ALU.mult, reads=("lnmu",), writes=("lnmu2",))
    r.dve("tensor_tensor", out=lnrow[:, 2 * Tc:3 * Tc], in0=lnrow[:, 2 * Tc:3 * Tc],
          in1=lnrow[:, 3 * Tc:4 * Tc], op=ALU.subtract,
          reads=("lnex2", "lnmu2"), writes=("lnvar",))
    r.act("activation", out=lnrow[:, 3 * Tc:4 * Tc], in_=lnrow[:, 2 * Tc:3 * Tc],
          func=AF.Sqrt, bias=L._sb[eps_key][:], reads=("lnvar", eps_key), writes=("lnstd",))
    r.dve("reciprocal", out=lnrow[:, Tc:2 * Tc], in_=lnrow[:, 3 * Tc:4 * Tc],
          reads=("lnstd",), writes=("lnrstd",))
    r.pe("matmul", out=pbc[:, 0:Tc], lhsT=ones_row[:], rhs=lnrow[:, 0:Tc],
         start=True, stop=True, reads=("ones_row", "lnmu"), writes=("pbc0",) + psk)
    r.pe("matmul", out=pbc[:, Tc:2 * Tc], lhsT=ones_row[:], rhs=lnrow[:, Tc:2 * Tc],
         start=True, stop=True, reads=("ones_row", "lnrstd"), writes=("pbc1",) + psk)
    for c in range(nchunk):
        r.dve("tensor_tensor", out=sqt[:, :Tc], in0=xin(c), in1=pbc[:, 0:Tc],
              op=ALU.subtract, reads=(xkey, "pbc0") + psk, writes=("sqt",))
        if gain is None:
            r.dve("tensor_tensor", out=ys[:, c, :Tc], in0=sqt[:, :Tc],
                  in1=pbc[:, Tc:2 * Tc], op=ALU.mult,
                  reads=("sqt", "pbc1") + psk, writes=(ykey,))
        else:
            r.dve("tensor_tensor", out=sqt[:, :Tc], in0=sqt[:, :Tc],
                  in1=pbc[:, Tc:2 * Tc], op=ALU.mult,
                  reads=("sqt", "pbc1") + psk, writes=("sqt",))
            r.dve("tensor_scalar", out=ys[:, c, :Tc], in0=sqt[:, :Tc],
                  scalar1=gain[:, c, :], scalar2=bias[:, c, :],
                  op0=ALU.mult, op1=ALU.add,
                  reads=("sqt",) + tuple(gkey), writes=(ykey,))


def ensure_ln_scratch(L, Tc, views=None):
    if views is None:
        L.ps("pstat_a", [1, 512], F32)
        L.ps("pstat_b", [1, 512], F32)
        L.ps("pbc", [128, 2 * Tc], F32)
    else:
        L._ps["pstat_a"] = views["pstat_a"]
        L._ps["pstat_b"] = views["pstat_b"]
        L._ps["pbc"] = views["pbc"]
    L.sb("lnrow", [1, 4 * Tc], F32)
    L.sb("sqt", [128, Tc], F32)


def load_weight_cm(L, name, host_shape, nchunk=None, dtype=BF16):
    K, M = host_shape
    ap = L.dram_in(name, [K, M], dtype)
    if K < 128:
        t = L.sb("w_" + name, [K, 1, M], dtype)
        L.rec.load(t[:, 0, :], ap[:, :], "w_" + name)
        return t
    nchunk = K // 128
    t = L.sb("w_" + name, [128, nchunk, M], dtype)
    L.rec.load(t[:, :, :], ap.rearrange("(c p) m -> p c m", p=128), "w_" + name)
    return t


def load_bias_col(L, name, K, dtype=F32):
    nchunk = K // 128
    ap = L.dram_in(name, [128, nchunk], dtype)
    t = L.sb("b_" + name, [128, nchunk, 1], dtype)
    L.rec.load(t[:, :, 0], ap[:, :], "b_" + name)
    return t


def host_bias(v):
    v = np.ascontiguousarray(v).astype(np.float32)
    return np.ascontiguousarray(v.reshape(-1, 128).T)



def build_l0ffn(L, x, xo):
    Tc = 256
    nT = T // Tc
    r = L.rec

    wU1 = load_weight_cm(L, "U1", (1024, 256))
    wV1 = load_weight_cm(L, "V1", (256, 4096))
    wU2 = load_weight_cm(L, "U2", (4096, 256))
    wV2 = load_weight_cm(L, "V2", (256, 1024))
    b1 = load_bias_col(L, "b1", 4096)
    b2 = load_bias_col(L, "b2", 1024)

    consts = emit_const_setup(L)
    ensure_ln_scratch(L, Tc)

    xs = L.sb("xs", [128, 8, Tc], F32)
    ys = L.sb("ys", [128, 8, Tc], BF16)
    Psb = L.sb("Psb", [128, 2, Tc], BF16)
    Hsb = L.sb("Hsb", [128, Tc], BF16)
    Qsb = L.sb("Qsb", [128, 2, Tc], BF16)
    pP = L.ps("pP", [128, 512], F32)
    pH = L.ps("pH", [128, 512], F32)
    pQ0 = L.ps("pQ0", [128, 512], F32)
    pQ1 = L.ps("pQ1", [128, 512], F32)
    pO = L.ps("pO", [128, 512], F32)
    pQm = [pQ0, pQ1]

    for b in range(B_PER_CORE):
        for t0 in range(nT):
            sl = slice(t0 * Tc, (t0 + 1) * Tc)
            r.load(xs[:, :, :], x[b, :, sl].rearrange("(c p) t -> p c t", p=128), "xs")
            emit_ln_cm(L, xs, "xs", ys, "ys", Tc, consts)
            for m in range(2):
                for c in range(8):
                    r.pe("matmul", out=pP[:, :Tc], lhsT=wU1[:, c, m * 128:(m + 1) * 128],
                         rhs=ys[:, c, :], start=(c == 0), stop=(c == 7),
                         reads=("w_U1", "ys"), writes=("pP",))
                r.act("activation", out=Psb[:, m, :], in_=pP[:, :Tc], func=AF.Copy,
                      reads=("pP",), writes=("Psb",))
            for ht in range(32):
                for c in range(2):
                    r.pe("matmul", out=pH[:, :Tc], lhsT=wV1[:, c, ht * 128:(ht + 1) * 128],
                         rhs=Psb[:, c, :], start=(c == 0), stop=(c == 1),
                         reads=("w_V1", "Psb"), writes=("pH",))
                r.act("activation", out=Hsb[:], in_=pH[:, :Tc], func=AF.Gelu_apprx_tanh,
                      bias=b1[:, ht, :], reads=("pH", "b_b1"), writes=("Hsb",))
                for m in range(2):
                    r.pe("matmul", out=pQm[m][:, :Tc], lhsT=wU2[:, ht, m * 128:(m + 1) * 128],
                         rhs=Hsb[:], start=(ht == 0), stop=(ht == 31),
                         reads=("w_U2", "Hsb"), writes=("pQ%d" % m,))
            for m in range(2):
                r.act("activation", out=Qsb[:, m, :], in_=pQm[m][:, :Tc], func=AF.Copy,
                      reads=("pQ%d" % m,), writes=("Qsb",))
            for mo in range(8):
                for c in range(2):
                    r.pe("matmul", out=pO[:, :Tc], lhsT=wV2[:, c, mo * 128:(mo + 1) * 128],
                         rhs=Qsb[:, c, :], start=(c == 0), stop=(c == 1),
                         reads=("w_V2", "Qsb"), writes=("pO",))
                r.dve("scalar_tensor_tensor", out=xs[:, mo, :], in0=pO[:, :Tc],
                      scalar=b2[:, mo, :], in1=xs[:, mo, :],
                      op0=ALU.add, op1=ALU.add,
                      reads=("pO", "b_b2", "xs"), writes=("xs",))
            r.store(xo[b, :, sl].rearrange("(c p) t -> p c t", p=128), xs[:, :, :],
                    reads=("xs",))
    return L


def host_prep_l0ffn(inp):
    w = {}
    w["U1"] = np.ascontiguousarray(inp["L0_ffn_U1"]).astype(BF)
    w["V1"] = np.ascontiguousarray(inp["L0_ffn_V1"]).astype(BF)
    w["U2"] = np.ascontiguousarray(inp["L0_ffn_U2"]).astype(BF)
    w["V2"] = np.ascontiguousarray(inp["L0_ffn_V2"]).astype(BF)
    w["b1"] = host_bias(inp["L0_ffn_b1"])
    w["b2"] = host_bias(inp["L0_ffn_b2"])
    return w


def launch_l0ffn(x_shards, inp):
    L = Launch("l0ffn")
    x = L.dram_in("x", [B_PER_CORE, D, T], F32)
    xo = L.dram_out("xo", [B_PER_CORE, D, T], F32)
    build_l0ffn(L, x, xo)
    w = host_prep_l0ffn(inp)
    in_maps = [{"x": x_shards[c], **w} for c in range(NCORES)]
    res = L.run(in_maps)
    return [res[c]["xo"] for c in range(NCORES)]



def build_gla(L, x, xo):
    r = L.rec
    QT = 512
    NC = 4
    wQ = load_weight_cm(L, "Wq", (D, D))
    wK = load_weight_cm(L, "Wk", (D, D))
    wV = load_weight_cm(L, "Wv", (D, D))
    wO = load_weight_cm(L, "Wo", (D, D))
    wF1 = load_weight_cm(L, "Wf1", (D, 64))
    wF2 = load_weight_cm(L, "Wf2", (64, D))
    wG1 = load_weight_cm(L, "Wg1", (D, 64))
    wG2 = load_weight_cm(L, "Wg2", (64, D))
    nw = load_bias_col(L, "nw", D)
    tri = L.dram_in("tri", [128, 128], F32)
    e2f = L.dram_in("e2f", [128, 2], F32)
    f2f = L.dram_in("f2f", [2, 128], F32)

    consts = emit_const_setup(L)
    ones_col, ones_row, eps_ln = consts

    xs = L.sb("xs", [128, 8, QT], F32)
    ys = L.sb("ys", [128, 8, QT], BF16)
    qt = L.sb("qt", [128, 8, QT], BF16)
    kt = L.sb("kt", [128, 8, QT], BF16)
    vt = L.sb("vt", [128, NC, D], BF16)
    gg = L.sb("gg", [128, 8, QT], F32)
    ogb = L.sb("ogb", [128, 8, QT], BF16)
    Gf = L.sb("Gf", [128, 8, QT], F32)
    logf = L.sb("logf", [128, QT], F32)
    scr = L.sb("scr", [128, QT], F32)
    scr2 = L.sb("scr2", [128, QT], F32)
    Af = L.sb("Af", [128, QT], F32)
    Ai = L.sb("Ai", [128, QT], F32)
    zf1 = L.sb("zf1", [64, QT], BF16)
    Ssb = L.sb("Ssb", [128, 8, 64], F32)
    Sbf = L.sb("Sbf", [128, 8, 64], BF16)
    Glast = L.sb("Glast", [128, 8, 1], F32)
    Glast2 = L.sb("Glast2", [128, 8, 1], F32)
    eps2 = L.sb("eps2", [2, 1], F32)
    ecol = L.sb("ecol", [128, 8, 2], F32)
    probs = L.sb("probs", [128, 2, 128], BF16)
    ktok = L.sb("ktok", [128, 2, 64], BF16)
    trim = L.sb("trim", [128, 128], F32)
    e2s = L.sb("e2s", [128, 2], F32)
    f2s = L.sb("f2s", [2, 128], F32)
    ident = L.sb("ident", [128, 128], BF16)
    ln8 = L.sb("ln8", [128, 1], F32)
    one1 = L.sb("one1", [128, 1], F32)
    rr = L.sb("rr", [2, QT], F32)
    identF = L.dram_in("identF", [128, 128], BF16)

    psAll = L.ps("psAll", [128, 3584], F32)
    pTr = L.ps("pTr", [128, 1024], BF16)
    ensure_ln_scratch(L, QT, views={
        "pstat_a": psAll[0:1, 0:512],
        "pstat_b": psAll[0:1, 512:1024],
        "pbc": psAll[:, 1024:2048],
    })
    pA = psAll[:, 2048:2560]
    pV2 = psAll[:, 2048:3072]
    pS0 = psAll[:, 0:128]
    pS1 = psAll[:, 512:640]
    pJ0 = psAll[:, 1024:1152]
    pJ1 = psAll[:, 1536:1664]
    pSt = psAll[:, 2048:2112]
    pSt2 = psAll[:, 2304:2368]
    pR = psAll[0:2, 2560:3072]
    pB = psAll[:, 3072:3584]
    pO = psAll[:, 1024:1536]

    r.load(trim[:], tri[:, :], "trim")
    r.load(e2s[:], e2f[:, :], "e2s")
    r.load(f2s[:], f2f[:, :], "f2s")
    r.load(ident[:], identF[:, :], "ident")
    r.dve("memset", ap=ln8[:], constant=-2.0794415416798357, writes=("ln8",))
    r.dve("memset", ap=one1[:], constant=1.0, writes=("one1",))

    for b in range(B_PER_CORE):
        r.dve("memset", ap=Ssb[:], constant=0.0, writes=("Ssb",))
        r.dve("memset", ap=Glast[:], constant=0.0, writes=("Glast",))
        r.dve("memset", ap=Glast2[:], constant=0.0, writes=("Glast2",))
        r.dve("memset", ap=eps2[:], constant=1e-5, writes=("eps2",))
        for qtr in range(T // QT):
            sl = slice(qtr * QT, (qtr + 1) * QT)
            r.load(xs[:, :, :], x[b, :, sl].rearrange("(c p) t -> p c t", p=128), "xs")
            emit_ln_cm(L, xs, "xs", ys, "ys", QT, consts, psk=("PS",))
            for c in range(8):
                r.pe("matmul", out=pA[:64, 0:QT], lhsT=wF1[:, c, :], rhs=ys[:, c, :],
                     start=(c == 0), stop=(c == 7), reads=("w_Wf1", "ys"),
                     writes=("PS",))
            r.act("activation", out=zf1[:], in_=pA[0:64, 0:QT], func=AF.Copy,
                  reads=("PS",), writes=("zf1",))
            for m in range(8):
                r.pe("matmul", out=pA[:, 0:QT], lhsT=wF2[0:64, 0, m * 128:(m + 1) * 128],
                     rhs=zf1[:], start=True, stop=True,
                     reads=("w_Wf2", "zf1"), writes=("PS",))
                r.act("activation", out=scr[:], in_=pA[:, 0:QT], func=AF.Abs,
                      reads=("PS",), writes=("scr",))
                r.act("activation", out=scr[:], in_=scr[:], func=AF.Exp, scale=-1.0,
                      reads=("scr",), writes=("scr",))
                r.act("activation", out=scr[:], in_=scr[:], func=AF.Ln, bias=one1[:],
                      reads=("scr", "one1"), writes=("scr",))
                r.act("activation", out=scr2[:], in_=pA[:, 0:QT], func=AF.Relu,
                      scale=-1.0, reads=("PS",), writes=("scr2",))
                r.dve("tensor_tensor", out=scr[:], in0=scr[:], in1=scr2[:],
                      op=ALU.add, reads=("scr", "scr2"), writes=("scr",))
                r.dve("tensor_scalar_mul", out=logf[:], in0=scr[:], scalar1=-1.0,
                      reads=("scr",), writes=("logf",))
                r.dve("tensor_tensor_scan", out=Gf[:, m, :],
                      data0=one1[:].broadcast_to((128, QT)), data1=logf[:],
                      initial=Glast[:, m, :], op0=ALU.mult, op1=ALU.add,
                      reads=("logf", "Glast", "one1"), writes=("Gf",))
                r.dve("tensor_copy", out=Glast[:, m, :], in_=Gf[:, m, QT - 1:QT],
                      reads=("Gf",), writes=("Glast",))
            for c in range(8):
                r.pe("matmul", out=pA[:64, 0:QT], lhsT=wG1[:, c, :], rhs=ys[:, c, :],
                     start=(c == 0), stop=(c == 7), reads=("w_Wg1", "ys"),
                     writes=("PS",))
            r.act("activation", out=zf1[:], in_=pA[0:64, 0:QT], func=AF.Copy,
                  reads=("PS",), writes=("zf1",))
            for m in range(8):
                r.pe("matmul", out=pA[:, 0:QT], lhsT=wG2[0:64, 0, m * 128:(m + 1) * 128],
                     rhs=zf1[:], start=True, stop=True,
                     reads=("w_Wg2", "zf1"), writes=("PS",))
                r.act("activation", out=gg[:, m, :], in_=pA[:, 0:QT], func=AF.Sigmoid,
                      reads=("PS",), writes=("gg",))
            for m in range(8):
                for cs in range(NC):
                    csl = slice(cs * 128, (cs + 1) * 128)
                    gcol = Gf[:, m, cs * 128 + 63:cs * 128 + 64]
                    r.dve("tensor_scalar", out=scr[:, csl], in0=Gf[:, m, csl],
                          scalar1=gcol, scalar2=None, op0=ALU.subtract,
                          reads=("Gf",), writes=("scr",))
                r.act("activation", out=Af[:], in_=scr[:], func=AF.Exp, bias=ln8[:],
                      reads=("scr", "ln8"), writes=("Af",))
                r.act("activation", out=Ai[:], in_=scr[:], func=AF.Exp, scale=-1.0,
                      reads=("scr",), writes=("Ai",))
                for c in range(8):
                    r.pe("matmul", out=pA[:, 0:QT], lhsT=wQ[:, c, m * 128:(m + 1) * 128],
                         rhs=ys[:, c, :], start=(c == 0), stop=(c == 7),
                         reads=("w_Wq", "ys"), writes=("PS",))
                r.act("activation", out=scr2[:], in_=pA[:, 0:QT], func=AF.Silu,
                      reads=("PS",), writes=("scr2",))
                r.dve("tensor_tensor", out=qt[:, m, :], in0=scr2[:], in1=Af[:],
                      op=ALU.mult, reads=("scr2", "Af"), writes=("qt",))
                for c in range(8):
                    r.pe("matmul", out=pA[:, 0:QT], lhsT=wK[:, c, m * 128:(m + 1) * 128],
                         rhs=ys[:, c, :], start=(c == 0), stop=(c == 7),
                         reads=("w_Wk", "ys"), writes=("PS",))
                r.act("activation", out=scr2[:], in_=pA[:, 0:QT], func=AF.Silu,
                      reads=("PS",), writes=("scr2",))
                r.dve("tensor_tensor", out=kt[:, m, :], in0=scr2[:], in1=Ai[:],
                      op=ALU.mult, reads=("scr2", "Ai"), writes=("kt",))
            for tc in range(NC):
                tsl = slice(tc * 128, (tc + 1) * 128)
                for half in range(2):
                    vsl = slice(half * 512, (half + 1) * 512)
                    for c in range(8):
                        r.pe("matmul", out=pV2[:, half * 512:half * 512 + 512],
                             lhsT=ys[:, c, tsl], rhs=wV[:, c, vsl],
                             start=(c == 0), stop=(c == 7),
                             reads=("ys", "w_Wv"), writes=("PS",))
                    r.act("activation", out=vt[:, tc, vsl], in_=pV2[:, half * 512:half * 512 + 512],
                          func=AF.Copy, reads=("PS",), writes=("vt",))
            for cs in range(NC):
                isl = slice(cs * 128, (cs + 1) * 128)
                for m in range(8):
                    gcol = Gf[:, m, cs * 128 + 63:cs * 128 + 64]
                    if cs == 0 and qtr == 0:
                        r.dve("tensor_copy", out=ecol[:, m, 0:1], in_=gcol,
                              reads=("Gf",), writes=("ecol",))
                    else:
                        pcol = (Gf[:, m, cs * 128 - 1:cs * 128] if cs > 0
                                else Glast2[:, m, :])
                        r.dve("tensor_tensor", out=ecol[:, m, 0:1], in0=gcol,
                              in1=pcol, op=ALU.subtract,
                              reads=("Gf", "Glast2"), writes=("ecol",))
                    r.dve("tensor_tensor", out=ecol[:, m, 1:2],
                          in0=Gf[:, m, cs * 128 + 127:cs * 128 + 128], in1=gcol,
                          op=ALU.subtract, reads=("Gf",), writes=("ecol",))
                    r.act("activation", out=ecol[:, m, :], in_=ecol[:, m, :],
                          func=AF.Exp, reads=("ecol",), writes=("ecol",))
                    r.dve("tensor_scalar_mul", out=Ssb[:, m, :], in0=Ssb[:, m, :],
                          scalar1=ecol[:, m, 0:1], reads=("Ssb", "ecol"),
                          writes=("Ssb",))
                    r.act("activation", out=Sbf[:, m, :], in_=Ssb[:, m, :],
                          func=AF.Copy, reads=("Ssb",), writes=("Sbf",))
                    r.pe("matmul", out=pS0[:, :], lhsT=kt[0:64, m, isl],
                         rhs=qt[0:64, m, isl], start=True, stop=True,
                         tile_position=(0, 0), reads=("kt", "qt"), writes=("PS",))
                    r.pe("matmul", out=pS1[:, :], lhsT=kt[64:128, m, isl],
                         rhs=qt[64:128, m, isl], start=True, stop=True,
                         tile_position=(64, 0), reads=("kt", "qt"), writes=("PS",))
                    r.dve("tensor_tensor", out=probs[:, 0, :], in0=pS0[:, :],
                          in1=trim[:], op=ALU.mult, reads=("PS", "trim"),
                          writes=("probs",))
                    r.dve("tensor_tensor", out=probs[:, 1, :], in0=pS1[:, :],
                          in1=trim[:], op=ALU.mult, reads=("PS", "trim"),
                          writes=("probs",))
                    r.pe("matmul", out=pJ0[0:64, :], lhsT=vt[:, cs, m * 128:m * 128 + 64],
                         rhs=probs[:, 0, :], start=True, stop=False,
                         reads=("vt", "probs"), writes=("PS",))
                    r.pe("matmul", out=pJ0[0:64, :], lhsT=Sbf[0:64, m, :],
                         rhs=qt[0:64, m, isl], start=False, stop=True,
                         tile_position=(0, 0), reads=("Sbf", "qt"), writes=("PS",))
                    r.pe("matmul", out=pJ1[0:64, :], lhsT=vt[:, cs, m * 128 + 64:m * 128 + 128],
                         rhs=probs[:, 1, :], start=True, stop=False,
                         reads=("vt", "probs"), writes=("PS",))
                    r.pe("matmul", out=pJ1[0:64, :], lhsT=Sbf[64:128, m, :],
                         rhs=qt[64:128, m, isl], start=False, stop=True,
                         tile_position=(64, 0), reads=("Sbf", "qt"), writes=("PS",))
                    r.dve("tensor_tensor", out=gg[0:64, m, isl], in0=pJ0[0:64, :],
                          in1=gg[0:64, m, isl], op=ALU.mult,
                          reads=("PS", "gg"), writes=("gg",))
                    r.dve("tensor_tensor", out=gg[64:128, m, isl], in0=pJ1[0:64, :],
                          in1=gg[64:128, m, isl], op=ALU.mult,
                          reads=("PS", "gg"), writes=("gg",))
                    r.pe("transpose", out=pTr[:, 0:64], in_=kt[0:64, m, isl],
                         identity=ident[0:64, 0:64], reads=("kt", "ident"), writes=("PS",))
                    r.act("activation", out=ktok[:, 0, :], in_=pTr[:, 0:64],
                          func=AF.Copy, reads=("PS",), writes=("ktok",))
                    r.pe("transpose", out=pTr[:, 512:576], in_=kt[64:128, m, isl],
                         identity=ident[64:128, 64:128], reads=("kt", "ident"), writes=("PS",))
                    r.act("activation", out=ktok[:, 1, :], in_=pTr[:, 512:576],
                          func=AF.Copy, reads=("PS",), writes=("ktok",))
                    r.pe("matmul", out=pSt[0:64, :], lhsT=ktok[:, 0, :],
                         rhs=vt[:, cs, m * 128:m * 128 + 64], start=True, stop=True,
                         reads=("ktok", "vt"), writes=("PS",))
                    r.pe("matmul", out=pSt2[0:64, :], lhsT=ktok[:, 1, :],
                         rhs=vt[:, cs, m * 128 + 64:m * 128 + 128], start=True,
                         stop=True, reads=("ktok", "vt"), writes=("PS",))
                    r.dve("tensor_tensor", out=Ssb[0:64, m, :], in0=Ssb[0:64, m, :],
                          in1=pSt[0:64, :], op=ALU.add, reads=("Ssb", "PS"),
                          writes=("Ssb",))
                    r.dve("tensor_tensor", out=Ssb[64:128, m, :], in0=Ssb[64:128, m, :],
                          in1=pSt2[0:64, :], op=ALU.add, reads=("Ssb", "PS"),
                          writes=("Ssb",))
                    r.dve("tensor_scalar_mul", out=Ssb[:, m, :], in0=Ssb[:, m, :],
                          scalar1=ecol[:, m, 1:2], reads=("Ssb", "ecol"),
                          writes=("Ssb",))
            r.dve("tensor_copy", out=Glast2[:, :, 0:1], in_=Gf[:, :, QT - 1:QT],
                  reads=("Gf",), writes=("Glast2",))
            for m in range(8):
                r.act("activation", out=scr[:], in_=gg[:, m, :], func=AF.Square,
                      reads=("gg",), writes=("scr",))
                r.pe("matmul", out=pR[:, 0:QT], lhsT=e2s[:], rhs=scr[:],
                     start=True, stop=True, reads=("e2s", "scr"), writes=("PS",))
                r.dve("tensor_scalar", out=rr[:], in0=pR[:, 0:QT],
                      scalar1=1.0 / 64, scalar2=None, op0=ALU.mult,
                      reads=("PS",), writes=("rr",))
                r.act("activation", out=rr[:], in_=rr[:], func=AF.Sqrt,
                      bias=eps2[:], reads=("rr", "eps2"), writes=("rr",))
                r.dve("reciprocal", out=rr[:], in_=rr[:], reads=("rr",),
                      writes=("rr",))
                r.pe("matmul", out=pB[:, 0:QT], lhsT=f2s[:], rhs=rr[:],
                     start=True, stop=True, reads=("f2s", "rr"), writes=("PS",))
                r.dve("tensor_tensor", out=scr[:], in0=gg[:, m, :], in1=pB[:, 0:QT],
                      op=ALU.mult, reads=("gg", "PS"), writes=("scr",))
                r.dve("tensor_scalar_mul", out=ogb[:, m, :], in0=scr[:],
                      scalar1=nw[:, m, :], reads=("scr", "b_nw"), writes=("ogb",))
            for mo in range(8):
                for c in range(8):
                    r.pe("matmul", out=pO[:, 0:QT], lhsT=wO[:, c, mo * 128:(mo + 1) * 128],
                         rhs=ogb[:, c, :], start=(c == 0), stop=(c == 7),
                         reads=("w_Wo", "ogb"), writes=("PS",))
                r.dve("tensor_tensor", out=xs[:, mo, :], in0=xs[:, mo, :],
                      in1=pO[:, 0:QT], op=ALU.add, reads=("xs", "PS"),
                      writes=("xs",))
            r.store(xo[b, :, sl].rearrange("(c p) t -> p c t", p=128), xs[:, :, :],
                    reads=("xs",))
    return L


def host_prep_gla(inp):
    w = {}
    for nm, key in (("Wq", "L0_attn_a_Wq"), ("Wk", "L0_attn_a_Wk"),
                    ("Wv", "L0_attn_a_Wv"), ("Wo", "L0_attn_a_Wo"),
                    ("Wf1", "L0_attn_a_Wf1"), ("Wf2", "L0_attn_a_Wf2"),
                    ("Wg1", "L0_attn_a_Wg1"), ("Wg2", "L0_attn_a_Wg2")):
        w[nm] = np.ascontiguousarray(inp[key]).astype(BF)
    w["nw"] = host_bias(inp["L0_attn_a_norm_weight"])
    jj, ii = np.meshgrid(np.arange(128), np.arange(128), indexing="ij")
    w["tri"] = (jj <= ii).astype(np.float32)
    e2 = np.zeros((128, 2), np.float32)
    e2[0:64, 0] = 1.0
    e2[64:128, 1] = 1.0
    w["e2f"] = e2
    w["f2f"] = np.ascontiguousarray(e2.T)
    w["identF"] = np.eye(128).astype(BF)
    return w


def launch_gla(x_shards, inp):
    L = Launch("gla")
    x = L.dram_in("x", [B_PER_CORE, D, T], F32)
    xo = L.dram_out("xo", [B_PER_CORE, D, T], F32)
    build_gla(L, x, xo)
    w = host_prep_gla(inp)
    in_maps = [{"x": x_shards[c], **w} for c in range(NCORES)]
    res = L.run(in_maps)
    return [res[c]["xo"] for c in range(NCORES)]



def build_attnb(L, x, xo):
    r = L.rec
    wq = L.dram_in("WqT", [D, D], BF16)
    wk = L.dram_in("WkT", [D, D], BF16)
    wv = L.dram_in("WvT", [D, D], BF16)
    wo = L.dram_in("WoT", [D, D], BF16)
    gband = L.dram_in("gband", [16, 128, 256], F32)
    lam = L.dram_in("lam", [1, 1], F32)
    bq = load_bias_col(L, "bq", D)
    bk = load_bias_col(L, "bk", D)
    bv_row = L.dram_in("bv_row", [1, D], BF16)
    bo = load_bias_col(L, "bo", D)

    consts = emit_const_setup(L)
    ones_col, ones_row, eps_ln = consts

    xs = L.sb("xs", [128, 8, T], F32)
    ys = L.sb("ys", [128, 8, T], BF16)
    Gb = L.sb("Gb", [128, 16, 256], F32)
    lam1 = L.sb("lam1", [1, 1], F32)
    lam128 = L.sb("lam128", [128, 1], F32)
    bs40 = L.sb("bs40", [128, 1], F32)
    bq8 = L.sb("bq8", [128, 8, 1], F32)
    wqkv = L.sb("wqkv", [128, 8, 3, 128], BF16)
    wosb = L.sb("wosb", [128, 8, 128], BF16)
    bvr = L.sb("bvr", [1, D], BF16)
    onesbf = L.sb("onesbf", [1, 128], BF16)
    Qp = L.sb("Qp", [128, T], BF16)
    Kp = L.sb("Kp", [128, T], BF16)
    Vt = L.sb("Vt", [128, 16, 130], BF16)
    probsT = L.sb("probsT", [128, 2, 16, 128], BF16)
    r64 = L.sb("r64", [1, 128], F32)
    nbc = L.sb("nbc", [64, 128], F32)
    ctxu = L.sb("ctxu", [128, 8, T], BF16)

    psAll = L.ps("psAll", [128, 4096], F32)
    pSc = psAll[:].rearrange("p (h kt q) -> p h kt q", h=2, kt=16)
    pAV0 = psAll[:, 0:128]
    pAV1 = psAll[:, 512:640]
    pBC = psAll[:, 1024:1152]
    ensure_ln_scratch(L, 512, views={
        "pstat_a": psAll[0:1, 0:512],
        "pstat_b": psAll[0:1, 512:1024],
        "pbc": psAll[:, 1024:2048],
    })

    r.load(lam1[:], lam[:, :], "lam1")
    r.load(bvr[:], bv_row[:, :], "bvr")
    r.dve("memset", ap=bs40[:], constant=-40.0, writes=("bs40",))
    r.dve("memset", ap=onesbf[:], constant=1.0, writes=("onesbf",))
    r.dve("memset", ap=Vt[:, :, 64:65], constant=1.0, writes=("Vt",))
    r.dve("memset", ap=Vt[:, :, 129:130], constant=1.0, writes=("Vt",))
    r.load(Gb[:, :, :], gband.rearrange("kt p m -> p kt m"), "Gb_raw")
    r.pe("matmul", out=psAll[0:128, 0:1], lhsT=ones_row[:], rhs=lam1[:],
         start=True, stop=True, reads=("ones_row", "lam1"), writes=("PS",))
    r.act("activation", out=lam128[:], in_=psAll[0:128, 0:1], func=AF.Copy,
          reads=("PS",), writes=("lam128",))
    for kt in range(16):
        r.dve("tensor_scalar_mul", out=Gb[:, kt, :], in0=Gb[:, kt, :],
              scalar1=lam128[:], reads=("Gb_raw", "lam128"), writes=("Gb",))
    for c in range(8):
        r.dve("tensor_scalar_mul", out=bq8[:, c, :], in0=L._sb["b_bq"][:, c, :],
              scalar1=0.125, reads=("b_bq",), writes=("bq8",))

    for b in range(B_PER_CORE):
        for t0 in range(4):
            sl = slice(t0 * 512, (t0 + 1) * 512)
            r.load(xs[:, :, sl], x[b, :, sl].rearrange("(c p) t -> p c t", p=128), "xs")
            emit_ln_cm(L, xs[:, :, sl], "xs", ys[:, :, sl], "ys", 512, consts,
                       psk=("PS",))
        for hp in range(8):
            hsl = slice(hp * 128, (hp + 1) * 128)
            r.load(wqkv[:, :, 0, :], wq[:, hsl].rearrange("(c p) m -> p c m", p=128), "wqkv")
            r.load(wqkv[:, :, 1, :], wk[:, hsl].rearrange("(c p) m -> p c m", p=128), "wqkv")
            r.load(wqkv[:, :, 2, :], wv[:, hsl].rearrange("(c p) m -> p c m", p=128), "wqkv")
            for t0 in range(4):
                sl = slice(t0 * 512, (t0 + 1) * 512)
                for c in range(8):
                    r.pe("matmul", out=psAll[:, 2048:2560], lhsT=wqkv[:, c, 0, :],
                         rhs=ys[:, c, sl], start=(c == 0), stop=(c == 7),
                         reads=("wqkv", "ys"), writes=("PS",))
                r.dve("tensor_scalar", out=Qp[:, sl], in0=psAll[:, 2048:2560],
                      scalar1=0.125, scalar2=bq8[:, hp, :],
                      op0=ALU.mult, op1=ALU.add,
                      reads=("PS", "bq8"), writes=("Qp",))
                for c in range(8):
                    r.pe("matmul", out=psAll[:, 2560:3072], lhsT=wqkv[:, c, 1, :],
                         rhs=ys[:, c, sl], start=(c == 0), stop=(c == 7),
                         reads=("wqkv", "ys"), writes=("PS",))
                r.dve("tensor_scalar_add", out=Kp[:, sl], in0=psAll[:, 2560:3072],
                      scalar1=L._sb["b_bk"][:, hp, :],
                      reads=("PS", "b_bk"), writes=("Kp",))
            for kt in range(16):
                tsl = slice(kt * 128, (kt + 1) * 128)
                for c in range(8):
                    r.pe("matmul", out=psAll[:, 3072:3200], lhsT=ys[:, c, tsl],
                         rhs=wqkv[:, c, 2, :], start=(c == 0), stop=False,
                         reads=("ys", "wqkv"), writes=("PS",))
                r.pe("matmul", out=psAll[:, 3072:3200], lhsT=onesbf[:, 0:128],
                     rhs=bvr[:, hsl], start=False, stop=True,
                     reads=("onesbf", "bvr"), writes=("PS",))
                r.act("activation", out=Vt[:, kt, 0:64], in_=psAll[:, 3072:3136],
                      func=AF.Copy, reads=("PS",), writes=("Vt",))
                r.act("activation", out=Vt[:, kt, 65:129], in_=psAll[:, 3136:3200],
                      func=AF.Copy, reads=("PS",), writes=("Vt",))
            for qc in range(16):
                qsl = slice(qc * 128, (qc + 1) * 128)
                for kt in range(16):
                    tsl = slice(kt * 128, (kt + 1) * 128)
                    r.pe("matmul", out=pSc[:, 0, kt, :], lhsT=Kp[0:64, tsl],
                         rhs=Qp[0:64, qsl], start=True, stop=True,
                         tile_position=(0, 0),
                         reads=("Kp", "Qp"), writes=("PS",))
                    r.pe("matmul", out=pSc[:, 1, kt, :], lhsT=Kp[64:128, tsl],
                         rhs=Qp[64:128, qsl], start=True, stop=True,
                         tile_position=(64, 0),
                         reads=("Kp", "Qp"), writes=("PS",))
                for kt, gsl, ssl in (
                    (qc - 1, slice(192, 256), slice(0, 64)),
                    (qc, slice(64, 192), slice(0, 128)),
                    (qc + 1, slice(0, 64), slice(64, 128)),
                ):
                    if 0 <= kt < 16:
                        for h in range(2):
                            r.dve("tensor_tensor", out=pSc[:, h, kt, ssl],
                                  in0=pSc[:, h, kt, ssl], in1=Gb[:, kt, gsl],
                                  op=ALU.add, reads=("PS", "Gb"), writes=("PS",))
                r.act("activation", out=probsT[:], in_=psAll[:], func=AF.Exp,
                      bias=bs40[:], reads=("PS", "bs40"), writes=("probsT",))
                for h, pAV in ((0, pAV0), (1, pAV1)):
                    for kt in range(16):
                        r.pe("matmul", out=pAV[0:65, :],
                             lhsT=Vt[:, kt, h * 65:h * 65 + 65],
                             rhs=probsT[:, h, kt, :], start=(kt == 0), stop=(kt == 15),
                             reads=("Vt", "probsT"), writes=("PS",))
                    r.dve("reciprocal", out=r64[:], in_=pAV[64:65, :],
                          reads=("PS",), writes=("r64",))
                    r.pe("matmul", out=pBC[0:64, :], lhsT=ones_row[:, 0:64],
                         rhs=r64[:], start=True, stop=True,
                         reads=("ones_row", "r64"), writes=("PS",))
                    r.act("activation", out=nbc[:], in_=pBC[0:64, :], func=AF.Copy,
                          reads=("PS",), writes=("nbc",))
                    r.dve("tensor_tensor", out=ctxu[h * 64:(h + 1) * 64, hp, qsl],
                          in0=pAV[0:64, :], in1=nbc[:], op=ALU.mult,
                          reads=("PS", "nbc"), writes=("ctxu",))
        for mo in range(8):
            msl = slice(mo * 128, (mo + 1) * 128)
            r.load(wosb[:, :, :], wo[:, msl].rearrange("(c p) m -> p c m", p=128), "wosb")
            for t0 in range(4):
                sl = slice(t0 * 512, (t0 + 1) * 512)
                for c in range(8):
                    r.pe("matmul", out=psAll[:, 3584:4096], lhsT=wosb[:, c, :],
                         rhs=ctxu[:, c, sl], start=(c == 0), stop=(c == 7),
                         reads=("wosb", "ctxu"), writes=("PS",))
                r.dve("scalar_tensor_tensor", out=xs[:, mo, sl],
                      in0=psAll[:, 3584:4096], scalar=L._sb["b_bo"][:, mo, :],
                      in1=xs[:, mo, sl], op0=ALU.add, op1=ALU.add,
                      reads=("PS", "b_bo", "xs"), writes=("xs",))
        r.store(xo[b, :, :].rearrange("(c p) t -> p c t", p=128), xs[:, :, :],
                reads=("xs",))
    return L


def host_band_attnb():
    idx = np.arange(T, dtype=np.float64)
    g = np.zeros((16, 128, 256), np.float32)
    for kt in range(16):
        kk = 128 * kt + np.arange(128)
        qq = 128 * kt - 64 + np.arange(256)
        d = np.abs(kk[:, None] - qq[None, :]).astype(np.float64)
        m = 0.8 * np.exp(-((d - 1.0) ** 2) / 0.5) + 0.4 * np.exp(-((d - 30.0) ** 2) / 8.0)
        m[d > 64] = 0.0
        m[:, (qq < 0) | (qq >= T)] = 0.0
        g[kt] = m.astype(np.float32)
    return g


def host_prep_attnb(inp):
    w = {}
    w["WqT"] = np.ascontiguousarray(inp["L0_attn_b_Wq"].T).astype(BF)
    w["WkT"] = np.ascontiguousarray(inp["L0_attn_b_Wk"].T).astype(BF)
    w["WvT"] = np.ascontiguousarray(inp["L0_attn_b_Wv"].T).astype(BF)
    w["WoT"] = np.ascontiguousarray(inp["L0_attn_b_Wo"].T).astype(BF)
    w["gband"] = host_band_attnb()
    w["lam"] = np.asarray(inp["L0_attn_b_prior_lambda"], np.float32).reshape(1, 1)
    w["bq"] = host_bias(inp["L0_attn_b_bq"])
    w["bk"] = host_bias(inp["L0_attn_b_bk"])
    w["bv_row"] = np.ascontiguousarray(inp["L0_attn_b_bv"]).astype(BF).reshape(1, D)
    w["bo"] = host_bias(inp["L0_attn_b_bo"])
    return w


def launch_attnb(x_shards, inp):
    L = Launch("attnb")
    x = L.dram_in("x", [B_PER_CORE, D, T], F32)
    xo = L.dram_out("xo", [B_PER_CORE, D, T], F32)
    build_attnb(L, x, xo)
    w = host_prep_attnb(inp)
    in_maps = [{"x": x_shards[c], **w} for c in range(NCORES)]
    res = L.run(in_maps)
    return [res[c]["xo"] for c in range(NCORES)]



def build_rwkv(L, x, xo):
    r = L.rec
    HU = 4096
    NHT = 32
    tWk = L.dram_in("tWk", [D, HU], BF16)
    tWv = L.dram_in("tWv", [D, HU], BF16)
    tWr = L.dram_in("tWr", [D, HU], BF16)
    tWo = L.dram_in("tWo", [HU, D], BF16)
    cWk = L.dram_in("cWk", [D, HU], BF16)
    cWv = L.dram_in("cWv", [HU, D], BF16)
    cWr = L.dram_in("cWr", [D, D], BF16)
    ln0g = load_bias_col(L, "ln0g", D)
    ln0b = load_bias_col(L, "ln0b", D)
    preg = load_bias_col(L, "preg", D)
    preb = load_bias_col(L, "preb", D)
    postg = load_bias_col(L, "postg", D)
    postb = load_bias_col(L, "postb", D)
    dcay = L.dram_in("dcay", [128, 32], F32)
    frst = L.dram_in("frst", [128, 32], F32)
    mixk = load_bias_col(L, "mixk", D)
    mixv = load_bias_col(L, "mixv", D)
    mixr = load_bias_col(L, "mixr", D)
    cmixk = load_bias_col(L, "cmixk", D)
    cmixr = load_bias_col(L, "cmixr", D)

    consts = emit_const_setup(L)
    ones_col, ones_row, eps_ln = consts

    xs = L.sb("xs", [128, 8, 512], F32)
    xp = L.sb("xp", [128, 8, T], F32)
    xn = L.sb("xn", [128, 8, T + 1], BF16)
    wt = [L.sb(f"wt{i}", [128, 8, 3, 128], BF16) for i in range(2)]
    wA = [L.sb(f"wA{i}", [128, 8, 3, 128], BF16) for i in range(2)]
    wB = [L.sb(f"wB{i}", [128, 8, 3, 128], BF16) for i in range(2)]
    wOt = L.sb("wOt", [128, 1, 1024], BF16)
    ks = L.sb("ks", [128, T], F32)
    vs = L.sb("vs", [128, T], F32)
    srs = L.sb("srs", [128, T], F32)
    sck = L.sb("sck", [128, T + 1], F32)
    sckv = L.sb("sckv", [128, T + 1], F32)
    rwk = L.sb("rwk", [128, T], BF16)
    hb = rwk
    dcs = L.sb("dcs", [128, 32], F32)
    frs = L.sb("frs", [128, 32], F32)
    imixs = L.sb("imixs", [128, 8, 5], F32)
    mix_all = [mixk, mixv, mixr]

    psAll = L.ps("psAll", [128, 4096], F32)
    ensure_ln_scratch(L, 512, views={
        "pstat_a": psAll[0:1, 0:512],
        "pstat_b": psAll[0:1, 512:1024],
        "pbc": psAll[:, 1024:2048],
    })
    pP = psAll[:, 2048:2560]
    pW = psAll[:, 2560:3072]
    pR2 = psAll[:, 3072:3584]

    r.load(dcs[:], dcay[:, :], "dcs")
    r.load(frs[:], frst[:, :], "frs")
    r.act("activation", out=dcs[:], in_=dcs[:], func=AF.Exp, reads=("dcs",),
          writes=("dcs",))
    r.act("activation", out=dcs[:], in_=dcs[:], func=AF.Exp, scale=-1.0,
          reads=("dcs",), writes=("dcs",))
    r.act("activation", out=frs[:], in_=frs[:], func=AF.Exp, reads=("frs",),
          writes=("frs",))
    for i, mx in enumerate([mixk, mixv, mixr, cmixk, cmixr]):
        for c in range(8):
            r.dve("tensor_scalar", out=imixs[:, c, i:i + 1], in0=mx[:, c, :],
                  scalar1=-1.0, scalar2=1.0, op0=ALU.mult, op1=ALU.add,
                  reads=("b_mixk", "b_mixv", "b_mixr", "b_cmixk", "b_cmixr"),
                  writes=("imixs",))

    def dual_proj(dst_ps, wa, wb, msl, nsl, extra=("wA0", "wB0")):
        for c in range(8):
            r.pe("matmul", out=dst_ps, lhsT=wa[:, c, msl],
                 rhs=xn[:, c, 1 + nsl.start:1 + nsl.stop],
                 start=(c == 0), stop=False, reads=("xn",) + extra,
                 writes=())
        for c in range(8):
            r.pe("matmul", out=dst_ps, lhsT=wb[:, c, msl],
                 rhs=xn[:, c, nsl.start:nsl.stop],
                 start=False, stop=(c == 7), reads=("xn",) + extra,
                 writes=())

    for b in range(B_PER_CORE):
        r.dve("memset", ap=xn[:, :, 0:1], constant=0.0, writes=("xn",))
        r.dve("memset", ap=sck[:, 0:1], constant=0.0, writes=("sck",))
        r.dve("memset", ap=sckv[:, 0:1], constant=0.0, writes=("sckv",))
        for t0 in range(4):
            sl = slice(t0 * 512, (t0 + 1) * 512)
            r.load(xs[:, :, :], x[b, :, sl].rearrange("(c p) t -> p c t", p=128), "xs")
            emit_ln_cm(L, xs, "xs", xs, "xs", 512, consts, psk=("PS",))
            emit_ln_cm(L, xs, "xs", xp[:, :, sl], "xp", 512, consts,
                       gain=ln0g, bias=ln0b, gkey=("b_ln0g", "b_ln0b"),
                       psk=("PS",))
            emit_ln_cm(L, xp[:, :, sl], "xp", xn[:, :, 1 + sl.start:1 + sl.stop],
                       "xn", 512, consts, gain=preg, bias=preb,
                       gkey=("b_preg", "b_preb"), psk=("PS",))
        for ht in range(NHT):
            pr = ht % 2
            wtp, wAp, wBp = wt[pr], wA[pr], wB[pr]
            kw, kA, kB = f"wt{pr}", f"wA{pr}", f"wB{pr}"
            hsl = slice(ht * 128, (ht + 1) * 128)
            r.load(wtp[:, :, 0, :], tWk[:, hsl].rearrange("(c p) m -> p c m", p=128), kw)
            r.load(wtp[:, :, 1, :], tWv[:, hsl].rearrange("(c p) m -> p c m", p=128), kw)
            r.load(wtp[:, :, 2, :], tWr[:, hsl].rearrange("(c p) m -> p c m", p=128), kw)
            r.load(wOt[:, 0, :], tWo[hsl, :], "wOt")
            for mat in range(3):
                mx = mix_all[mat]
                for c in range(8):
                    r.act("activation", out=wAp[:, c, mat, :],
                          in_=wtp[:, c, mat, :], func=AF.Copy,
                          scale=mx[:, c, :],
                          reads=(kw, "b_mixk", "b_mixv", "b_mixr"),
                          writes=(kA,))
                    r.act("activation", out=wBp[:, c, mat, :],
                          in_=wtp[:, c, mat, :], func=AF.Copy,
                          scale=imixs[:, c, mat:mat + 1],
                          reads=(kw, "imixs"), writes=(kB,))
            msl = slice(0, 128)
            for n4 in range(4):
                nsl = slice(n4 * 512, (n4 + 1) * 512)
                dual_proj(pP, wAp[:, :, 0, :], wBp[:, :, 0, :], msl, nsl,
                          extra=(kA, kB))
                r.dve("tensor_scalar_min", out=ks[:, nsl], in0=pP,
                      scalar1=60.0, reads=(), writes=("ks",))
                dual_proj(pW, wAp[:, :, 1, :], wBp[:, :, 1, :], msl, nsl,
                          extra=(kA, kB))
                r.act("activation", out=vs[:, nsl], in_=pW, func=AF.Copy,
                      reads=(), writes=("vs",))
                dual_proj(pR2, wAp[:, :, 2, :], wBp[:, :, 2, :], msl, nsl,
                          extra=(kA, kB))
                r.act("activation", out=srs[:, nsl], in_=pR2, func=AF.Sigmoid,
                      reads=(), writes=("srs",))
            r.act("activation", out=ks[:], in_=ks[:], func=AF.Exp,
                  reads=("ks",), writes=("ks",))
            r.dve("tensor_tensor", out=vs[:], in0=ks[:], in1=vs[:], op=ALU.mult,
                  reads=("ks", "vs"), writes=("vs",))
            dcol = dcs[:, ht:ht + 1]
            fcol = frs[:, ht:ht + 1]
            r.dve("tensor_tensor_scan", out=sck[:, 1:T + 1],
                  data0=dcol.broadcast_to((128, T)), data1=ks[:],
                  initial=0.0, op0=ALU.mult, op1=ALU.add,
                  reads=("ks", "dcs"), writes=("sck",))
            r.dve("tensor_tensor_scan", out=sckv[:, 1:T + 1],
                  data0=dcol.broadcast_to((128, T)), data1=vs[:],
                  initial=0.0, op0=ALU.mult, op1=ALU.add,
                  reads=("vs", "dcs"), writes=("sckv",))
            r.dve("scalar_tensor_tensor", out=ks[:], in0=ks[:], scalar=fcol,
                  in1=sck[:, 0:T], op0=ALU.mult, op1=ALU.add,
                  reads=("ks", "frs", "sck"), writes=("ks",))
            r.dve("scalar_tensor_tensor", out=vs[:], in0=vs[:], scalar=fcol,
                  in1=sckv[:, 0:T], op0=ALU.mult, op1=ALU.add,
                  reads=("vs", "frs", "sckv"), writes=("vs",))
            r.dve("tensor_scalar_add", out=ks[:], in0=ks[:], scalar1=1e-8,
                  reads=("ks",), writes=("ks",))
            r.dve("reciprocal", out=ks[:], in_=ks[:], reads=("ks",), writes=("ks",))
            r.dve("tensor_tensor", out=vs[:], in0=vs[:], in1=ks[:], op=ALU.mult,
                  reads=("vs", "ks"), writes=("vs",))
            r.dve("tensor_tensor", out=rwk[:], in0=vs[:], in1=srs[:], op=ALU.mult,
                  reads=("vs", "srs"), writes=("rwk",))
            for mo in range(8):
                mosl = slice(mo * 128, (mo + 1) * 128)
                for n4 in range(4):
                    nsl = slice(n4 * 512, (n4 + 1) * 512)
                    pdst = pW if (mo * 4 + n4) % 2 == 0 else pR2
                    r.pe("matmul", out=pdst, lhsT=wOt[:, 0, mosl], rhs=rwk[:, nsl],
                         start=True, stop=True, reads=("wOt", "rwk"),
                         writes=())
                    r.dve("tensor_tensor", out=xp[:, mo, nsl], in0=xp[:, mo, nsl],
                          in1=pdst, op=ALU.add, reads=("xp",), writes=("xp",))
        for t0 in range(4):
            sl = slice(t0 * 512, (t0 + 1) * 512)
            emit_ln_cm(L, xp[:, :, sl], "xp", xn[:, :, 1 + sl.start:1 + sl.stop],
                       "xn", 512, consts, gain=postg, bias=postb,
                       gkey=("b_postg", "b_postb"), psk=("PS",))
        for ht in range(NHT):
            pr = ht % 2
            wtp, wAp, wBp = wt[pr], wA[pr], wB[pr]
            kw, kA, kB = f"wt{pr}", f"wA{pr}", f"wB{pr}"
            hsl = slice(ht * 128, (ht + 1) * 128)
            r.load(wtp[:, :, 0, :], cWk[:, hsl].rearrange("(c p) m -> p c m", p=128), kw)
            r.load(wOt[:, 0, :], cWv[hsl, :], "wOt")
            for c in range(8):
                r.act("activation", out=wAp[:, c, 0, :],
                      in_=wtp[:, c, 0, :], func=AF.Copy, scale=cmixk[:, c, :],
                      reads=(kw, "b_cmixk"), writes=(kA,))
                r.act("activation", out=wBp[:, c, 0, :],
                      in_=wtp[:, c, 0, :], func=AF.Copy, scale=imixs[:, c, 3:4],
                      reads=(kw, "imixs"), writes=(kB,))
            for n4 in range(4):
                nsl = slice(n4 * 512, (n4 + 1) * 512)
                dual_proj(pP if n4 % 2 == 0 else pR2,
                          wAp[:, :, 0, :], wBp[:, :, 0, :], slice(0, 128), nsl,
                          extra=(kA, kB))
                r.act("activation", out=hb[:, nsl],
                      in_=pP if n4 % 2 == 0 else pR2, func=AF.Silu,
                      reads=(), writes=("rwk",))
            for mo in range(8):
                mosl = slice(mo * 128, (mo + 1) * 128)
                for n4 in range(4):
                    nsl = slice(n4 * 512, (n4 + 1) * 512)
                    pdst = pW if (mo * 4 + n4) % 2 == 0 else pR2
                    r.pe("matmul", out=pdst, lhsT=wOt[:, 0, mosl], rhs=hb[:, nsl],
                         start=True, stop=True, reads=("wOt", "rwk"),
                         writes=())
                    r.dve("tensor_tensor", out=xp[:, mo, nsl], in0=xp[:, mo, nsl],
                          in1=pdst, op=ALU.add, reads=("xp",), writes=("xp",))
        for mo in range(8):
            pr = mo % 2
            wtp, wAp, wBp = wt[pr], wA[pr], wB[pr]
            kw, kA, kB = f"wt{pr}", f"wA{pr}", f"wB{pr}"
            mosl = slice(mo * 128, (mo + 1) * 128)
            r.load(wtp[:, :, 0, :],
                   cWr[:, mosl].rearrange("(c p) m -> p c m", p=128), kw)
            for c in range(8):
                r.act("activation", out=wAp[:, c, 0, :],
                      in_=wtp[:, c, 0, :], func=AF.Copy, scale=cmixr[:, c, :],
                      reads=(kw, "b_cmixr"), writes=(kA,))
                r.act("activation", out=wBp[:, c, 0, :],
                      in_=wtp[:, c, 0, :], func=AF.Copy, scale=imixs[:, c, 4:5],
                      reads=(kw, "imixs"), writes=(kB,))
            for n4 in range(4):
                nsl = slice(n4 * 512, (n4 + 1) * 512)
                pdst = pP if n4 % 2 == 0 else pR2
                dual_proj(pdst, wAp[:, :, 0, :], wBp[:, :, 0, :], slice(0, 128),
                          nsl, extra=(kA, kB))
                r.act("activation", out=srs[:, n4 * 512:n4 * 512 + 512],
                      in_=pdst, func=AF.Sigmoid, reads=(), writes=("srs",))
                r.dve("tensor_tensor", out=xp[:, mo, nsl], in0=xp[:, mo, nsl],
                      in1=srs[:, n4 * 512:n4 * 512 + 512], op=ALU.add,
                      reads=("xp", "srs"), writes=("xp",))
        for t0 in range(4):
            sl = slice(t0 * 512, (t0 + 1) * 512)
            r.load(xs[:, :, :], x[b, :, sl].rearrange("(c p) t -> p c t", p=128), "xs")
            for c in range(8):
                r.dve("tensor_tensor", out=xs[:, c, :], in0=xs[:, c, :],
                      in1=xp[:, c, sl], op=ALU.add, reads=("xs", "xp"),
                      writes=("xs",))
            r.store(xo[b, :, sl].rearrange("(c p) t -> p c t", p=128), xs[:, :, :],
                    reads=("xs",))
    return L


def host_prep_rwkv(inp):
    w = {}
    w["tWk"] = np.ascontiguousarray(inp["L1_attn_b_tm_Wk"]).astype(BF)
    w["tWv"] = np.ascontiguousarray(inp["L1_attn_b_tm_Wv"]).astype(BF)
    w["tWr"] = np.ascontiguousarray(inp["L1_attn_b_tm_Wr"]).astype(BF)
    w["tWo"] = np.ascontiguousarray(inp["L1_attn_b_tm_Wo"]).astype(BF)
    w["cWk"] = np.ascontiguousarray(inp["L1_attn_b_cm_Wk"]).astype(BF)
    w["cWv"] = np.ascontiguousarray(inp["L1_attn_b_cm_Wv"]).astype(BF)
    w["cWr"] = np.ascontiguousarray(inp["L1_attn_b_cm_Wr"]).astype(BF)
    for nm, key in (("ln0g", "L1_attn_b_ln0_g"), ("ln0b", "L1_attn_b_ln0_b"),
                    ("preg", "L1_attn_b_pre_g"), ("preb", "L1_attn_b_pre_b"),
                    ("postg", "L1_attn_b_post_g"), ("postb", "L1_attn_b_post_b"),
                    ("mixk", "L1_attn_b_tm_mix_k"), ("mixv", "L1_attn_b_tm_mix_v"),
                    ("mixr", "L1_attn_b_tm_mix_r"), ("cmixk", "L1_attn_b_cm_mix_k"),
                    ("cmixr", "L1_attn_b_cm_mix_r")):
        w[nm] = host_bias(inp[key])
    w["dcay"] = np.ascontiguousarray(
        np.asarray(inp["L1_attn_b_tm_decay"], np.float32).reshape(32, 128).T)
    w["frst"] = np.ascontiguousarray(
        np.asarray(inp["L1_attn_b_tm_first"], np.float32).reshape(32, 128).T)
    return w


def launch_rwkv(x_shards, inp):
    L = Launch("rwkv")
    x = L.dram_in("x", [B_PER_CORE, D, T], F32)
    xo = L.dram_out("xo", [B_PER_CORE, D, T], F32)
    build_rwkv(L, x, xo)
    w = host_prep_rwkv(inp)
    in_maps = [{"x": x_shards[c], **w} for c in range(NCORES)]
    res = L.run(in_maps)
    return [res[c]["xo"] for c in range(NCORES)]



def build_ffn1(L, x, xo, dbg=False):
    r = L.rec
    if dbg:
        zdump = L.dram_out("zdump", [8, 128, T], F32)
        cdump = L.dram_out("cdump", [4, 128, T], BF16)
    HID = 2816
    NHT = HID // 128
    wqkv_d = [L.dram_in(f"wqkv{l}", [D, 1536], BF16) for l in range(2)]
    wo_d = [L.dram_in(f"wo{l}", [512, D], BF16) for l in range(2)]
    w1_d = [L.dram_in(f"w1p{l}", [D, 2 * HID], BF16) for l in range(2)]
    w2_d = [L.dram_in(f"w2p{l}", [HID, D], BF16) for l in range(2)]
    b1a_d = [load_bias_col(L, f"b1a{l}", 128 * NHT) for l in range(2)]
    b1g_d = [load_bias_col(L, f"b1g{l}", 128 * NHT) for l in range(2)]
    b2_d = [load_bias_col(L, f"b2{l}", D) for l in range(2)]
    lg_d = [load_bias_col(L, f"lg{l}", D) for l in range(2)]
    lb_d = [load_bias_col(L, f"lb{l}", D) for l in range(2)]
    qs_d = [L.dram_in(f"qs{l}", [128, 1], F32) for l in range(2)]
    ks_d = [L.dram_in(f"ks{l}", [128, 1], F32) for l in range(2)]
    rotq_d = L.dram_in("rotq", [2, 128, T], BF16)
    rotk_d = L.dram_in("rotk", [2, 128, T], BF16)
    tri_d = L.dram_in("triB", [128, 128], BF16)
    triT_d = L.dram_in("triT", [128, 128], BF16)
    r2_d = L.dram_in("r2", [128, 128], BF16)
    idf_d = L.dram_in("idf_in", [128, 128], F32)
    e2f = L.dram_in("e2f", [128, 2], F32)
    f2f = L.dram_in("f2f", [2, 128], F32)

    consts = emit_const_setup(L)
    ones_col, ones_row, eps_ln = consts

    z = L.sb("z", [128, 8, T + 1], F32)
    zb = L.sb("zb", [128, 8, T + 1], BF16)
    xs = L.sb("xs", [128, 8, 512], F32)
    raw = xs[:].rearrange("p c t -> p (c t)")[:, 0:2048]
    h2b = L.sb("h2b", [128, 8, 512], BF16)
    wqkvt = L.sb("wqkvt", [128, 8, 384], BF16)
    wosb = L.sb("wosb", [128, 1024], BF16)
    w1t = [L.sb(f"w1t{i}", [128, 8, 256], BF16) for i in range(2)]
    w2t = [L.sb("w2t0", [128, NHT, 128], BF16)] * 2
    hfull = L.sb("hfull", [128, NHT, 512], BF16)
    Vt = L.sb("Vt", [128, 16, 130], BF16)
    qp = L.sb("qp", [128, T], BF16)
    kp = L.sb("kp", [128, T], BF16)
    ctx = L.sb("ctx", [128, T], BF16)
    probsT = L.sb("probsT", [128, 2, 8, 256], BF16)
    rot = L.sb("rot", [128, 2, 512], BF16)
    trim = L.sb("trim", [128, 128], BF16)
    trimT = L.sb("trimT", [128, 128], BF16)
    r2s = L.sb("r2s", [128, 128], BF16)
    idf = L.sb("idf", [128, 128], F32)
    e2s = L.sb("e2s", [128, 2], F32)
    f2s = L.sb("f2s", [2, 128], F32)
    qscol = L.sb("qscol", [128, 2], F32)
    rrnb = L.sb("rrnb", [64, 512], F32)
    rr = rrnb[0:2, 0:512]
    sb512 = L.sb("sb512", [128, 512], BF16)
    sb512b = L.sb("sb512b", [128, 512], BF16)
    sb512c, sb512d = sb512, sb512b
    nb = rrnb[0:64, 0:256]
    r64 = rrnb[0:1, 256:512]
    mx1 = L.sb("mx1", [128, 1], F32)
    nmx2 = L.sb("nmx2", [128, 2], F32)
    negrow = L.sb("negrow", [1, 128], F32)
    sqs = None
    m1 = L.sb("m1", [1, 1], F32)
    nmx = L.sb("nmx", [128, 1], F32)

    psAll = L.ps("psAll", [128, 4096], F32)
    ensure_ln_scratch(L, 512, views={
        "pstat_a": psAll[0:1, 0:512],
        "pstat_b": psAll[0:1, 512:1024],
        "pbc": psAll[:, 1024:2048],
    })
    vA = psAll[:, 0:2048]
    vB = psAll[:, 2048:4096]
    pV = psAll[:, 2048:2176]
    pP = psAll[:, 2560:3072]
    pR = psAll[0:2, 3072:3584]
    pB = psAll[:, 3584:4096]
    pRh = psAll[:, 2048:2560]
    pT1 = psAll[0:1, 3072:3200]
    pM = psAll[0:128, 3328:3329]
    pAVa = psAll[0:65, 0:256]
    pAVb = psAll[0:65, 2048:2304]
    pBC = psAll[0:64, 512:768]
    psA = psAll[:, 0:512]
    psB = psAll[:, 512:1024]
    pW = psAll[:, 2560:3072]

    r.load(trim[:], tri_d[:, :], "trim")
    r.load(trimT[:], triT_d[:, :], "trimT")
    r.load(r2s[:], r2_d[:, :], "r2s")
    r.load(idf[:], idf_d[:, :], "idf")
    r.load(e2s[:], e2f[:, :], "e2s")
    r.load(f2s[:], f2f[:, :], "f2s")
    r.dve("memset", ap=negrow[:], constant=-1.0, writes=("negrow",))

    def zsl(c, nsl):
        if c >= 4:
            return zb[:, c, nsl.start:nsl.stop]
        return zb[:, c, 1 + nsl.start:1 + nsl.stop]

    def zslf(c, nsl):
        if c >= 4:
            return z[:, c, nsl.start:nsl.stop]
        return z[:, c, 1 + nsl.start:1 + nsl.stop]

    for b in range(B_PER_CORE):
        r.dve("memset", ap=z[:, :, 0:1], constant=0.0, writes=("z",))
        r.dve("memset", ap=zb[:, :, 0:1], constant=0.0, writes=("zb",))
        r.dve("memset", ap=Vt[:, :, 64:65], constant=1.0, writes=("Vt",))
        r.dve("memset", ap=Vt[:, :, 129:130], constant=1.0, writes=("Vt",))
        for t0 in range(4):
            sl = slice(t0 * 512, (t0 + 1) * 512)
            r.load(xs[:, :, :], x[b, :, sl].rearrange("(c p) t -> p c t", p=128), "xs")
            emit_ln_cm(L, xs, "xs", z[:, :, 1 + sl.start:1 + sl.stop], "z",
                       512, consts, psk=("PS",))
            for c in range(8):
                r.act("activation", out=zb[:, c, 1 + sl.start:1 + sl.stop],
                      in_=z[:, c, 1 + sl.start:1 + sl.stop], func=AF.Copy,
                      reads=("z",), writes=("zb",))
        for l in range(2):
            r.load(qscol[:, 0:1], qs_d[l][:, :], "qscol")
            r.load(qscol[:, 1:2], ks_d[l][:, :], "qscol")
            r.dve("tensor_scalar_mul", out=qscol[:, 0:1], in0=qscol[:, 0:1],
                  scalar1=8.0, reads=("qscol",), writes=("qscol",))
            for hp in range(4):
                for i3, base in enumerate((0, 512, 1024)):
                    csl = slice(base + hp * 128, base + (hp + 1) * 128)
                    r.load(wqkvt[:, :, i3 * 128:(i3 + 1) * 128],
                           wqkv_d[l][:, csl].rearrange("(c p) m -> p c m", p=128),
                           "wqkvt")
                r.load(wosb[:, :], wo_d[l][hp * 128:(hp + 1) * 128, :], "wosb")
                for tt in range(16):
                    tsl = slice(tt * 128, (tt + 1) * 128)
                    for c in range(8):
                        r.pe("matmul", out=pV, lhsT=zsl(c, tsl),
                             rhs=wqkvt[:, c, 256:384],
                             start=(c == 0), stop=(c == 7),
                             reads=("zb", "wqkvt"), writes=("PS",))
                    r.act("activation", out=Vt[:, tt, 0:64], in_=pV[:, 0:64],
                          func=AF.Copy, reads=("PS",), writes=("Vt",))
                    r.act("activation", out=Vt[:, tt, 65:129], in_=pV[:, 64:128],
                          func=AF.Copy, reads=("PS",), writes=("Vt",))
                for qk in range(2):
                    dst = qp if qk == 0 else kp
                    dkey = "qp" if qk == 0 else "kp"
                    for n4 in range(4):
                        nsl = slice(n4 * 512, (n4 + 1) * 512)
                        for c in range(8):
                            r.pe("matmul", out=pP,
                                 lhsT=wqkvt[:, c, qk * 128:(qk + 1) * 128],
                                 rhs=zsl(c, nsl), start=(c == 0), stop=(c == 7),
                                 reads=("wqkvt", "zb"), writes=("PS",))
                        r.act("activation", out=raw[:, nsl], in_=pP, func=AF.Copy,
                              reads=("PS",), writes=("xs",))
                    sqt_ = L._sb["sqt"]
                    for n4 in range(4):
                        nsl = slice(n4 * 512, (n4 + 1) * 512)
                        r.load(rot[:, :, :],
                               (rotq_d if qk == 0 else rotk_d)[:, :, nsl].rearrange(
                                   "two p t -> p two t"), "rot")
                        r.act("activation", out=sqt_[:, 0:512], in_=raw[:, nsl],
                              func=AF.Square, reads=("xs",), writes=("sqt",))
                        r.pe("matmul", out=pR[:, 0:512], lhsT=e2s[:],
                             rhs=sqt_[:, 0:512], start=True, stop=True,
                             reads=("e2s", "sqt"), writes=("PS",))
                        r.act("activation", out=rr[:], in_=pR[:, 0:512],
                              func=AF.Sqrt, reads=("PS",), writes=("rrnb",))
                        r.dve("tensor_scalar_max", out=rr[:], in0=rr[:],
                              scalar1=1e-12, reads=("rrnb",), writes=("rrnb",))
                        r.dve("reciprocal", out=rr[:], in_=rr[:], reads=("rrnb",),
                              writes=("rrnb",))
                        r.pe("matmul", out=pB[:, 0:512], lhsT=f2s[:], rhs=rr[:],
                             start=True, stop=True, reads=("f2s", "rrnb"),
                             writes=("PS",))
                        r.dve("tensor_tensor", out=sb512[:], in0=raw[:, nsl],
                              in1=pB[:, 0:512], op=ALU.mult,
                              reads=("xs", "PS"), writes=("sb512",))
                        r.dve("tensor_scalar_mul", out=sb512[:], in0=sb512[:],
                              scalar1=qscol[:, qk:qk + 1],
                              reads=("sb512", "qscol"), writes=("sb512",))
                        r.pe("matmul", out=pRh, lhsT=r2s[:], rhs=sb512[:],
                             start=True, stop=True, reads=("r2s", "sb512"),
                             writes=("PS",))
                        r.act("activation", out=sb512b[:], in_=pRh, func=AF.Copy,
                              reads=("PS",), writes=("sb512b",))
                        r.dve("tensor_tensor", out=sb512[:], in0=sb512[:],
                              in1=rot[:, 0, :], op=ALU.mult,
                              reads=("sb512", "rot"), writes=("sb512",))
                        r.dve("tensor_tensor", out=sb512b[:], in0=sb512b[:],
                              in1=rot[:, 1, :], op=ALU.mult,
                              reads=("sb512b", "rot"), writes=("sb512b",))
                        r.dve("tensor_tensor", out=dst[:, nsl], in0=sb512[:],
                              in1=sb512b[:], op=ALU.add,
                              reads=("sb512", "sb512b"), writes=(dkey,))
                for w in range(4):
                    for qh in range(2):
                        q0 = w * 512 + qh * 256
                        kts = []
                        for i in range(2 * qh, 4):
                            if w > 0:
                                kts.append((i, 4 * (w - 1) + i))
                        for j in range(0, 2 * qh + 2):
                            kts.append((4 + j, 4 * w + j))
                        for h2, vv, tp in ((0, vA, (0, 0)), (1, vB, (64, 0))):
                            hb_ = slice(h2 * 64, (h2 + 1) * 64)
                            for ktL, kta in kts:
                                r.pe("matmul",
                                     out=vv[:, ktL * 256:(ktL + 1) * 256],
                                     lhsT=kp[hb_, kta * 128:(kta + 1) * 128],
                                     rhs=qp[hb_, q0:q0 + 256],
                                     start=True, stop=True, tile_position=tp,
                                     reads=("kp", "qp"), writes=("PS",))
                        if qh == 0:
                            pT1q = psAll[0:1, 1536:1664]
                            pMq = psAll[0:128, 1792:1793]
                        else:
                            pT1q = psAll[0:1, 0:128]
                            pMq = psAll[0:128, 256:257]
                        for h2, vv, pAV in ((0, vA, pAVa), (1, vB, pAVb)):
                            lo = kts[0][0] * 256
                            hi = (kts[-1][0] + 1) * 256
                            if qh == 0:
                                r.dve("tensor_reduce", out=mx1[:, 0:1],
                                      in_=vv[:, lo:hi], axis=AX.X, op=ALU.max,
                                      reads=(), writes=("mx1",))
                                r.pe("transpose", out=pT1q, in_=mx1[:, 0:1],
                                     identity=idf[:], reads=("mx1", "idf"),
                                     writes=())
                                r.dve("tensor_reduce", out=m1[:, 0:1],
                                      in_=pT1q[0:1, :], axis=AX.X, op=ALU.max,
                                      reads=(), writes=("m1",))
                                r.pe("matmul", out=pMq, lhsT=negrow[:], rhs=m1[:],
                                     start=True, stop=True,
                                     reads=("negrow", "m1"), writes=())
                                r.act("activation", out=nmx2[:, h2:h2 + 1],
                                      in_=pMq, func=AF.Copy, reads=(),
                                      writes=("nmx2",))
                            r.act("activation", out=probsT[:, h2].rearrange(
                                      "p kt q -> p (kt q)")[:, lo:hi],
                                  in_=vv[:, lo:hi], func=AF.Exp,
                                  bias=nmx2[:, h2:h2 + 1],
                                  reads=("nmx2",), writes=("probsT",))
                            for j in (2 * qh, 2 * qh + 1):
                                qq = slice((j - 2 * qh) * 128,
                                           (j - 2 * qh + 1) * 128)
                                r.dve("tensor_tensor",
                                      out=probsT[:, h2, 4 + j, qq],
                                      in0=probsT[:, h2, 4 + j, qq], in1=trim[:],
                                      op=ALU.mult, reads=("probsT", "trim"),
                                      writes=("probsT",))
                                if w > 0:
                                    r.dve("tensor_tensor",
                                          out=probsT[:, h2, j, qq],
                                          in0=probsT[:, h2, j, qq], in1=trimT[:],
                                          op=ALU.mult, reads=("probsT", "trimT"),
                                          writes=("probsT",))
                            r.dve("memset",
                                  ap=probsT[:, h2, 4 + 2 * qh + 1, 0:128],
                                  constant=0.0, writes=("probsT",))
                            if w > 0:
                                r.dve("memset",
                                      ap=probsT[:, h2, 2 * qh, 128:256],
                                      constant=0.0, writes=("probsT",))
                            for ii, (ktL, kta) in enumerate(kts):
                                r.pe("matmul", out=pAV,
                                     lhsT=Vt[:, kta, h2 * 65:h2 * 65 + 65],
                                     rhs=probsT[:, h2, ktL, :],
                                     start=(ii == 0), stop=(ii == len(kts) - 1),
                                     reads=("Vt", "probsT"), writes=("PS",))
                            r.dve("reciprocal", out=r64[:, 0:256], in_=pAV[64:65, :],
                                  reads=("PS",), writes=("rrnb",))
                            r.pe("matmul", out=pBC, lhsT=ones_row[:, 0:64],
                                 rhs=r64[:, 0:256], start=True, stop=True,
                                 reads=("ones_row", "rrnb"), writes=("PS",))
                            r.act("activation", out=nb[:], in_=pBC, func=AF.Copy,
                                  reads=("PS",), writes=("rrnb",))
                            r.dve("tensor_tensor",
                                  out=ctx[h2 * 64:(h2 + 1) * 64, q0:q0 + 256],
                                  in0=pAV[0:64, :], in1=nb[:], op=ALU.mult,
                                  reads=("PS", "rrnb"), writes=("ctx",))
                if dbg and b == 0 and l == 0:
                    r.store(cdump[hp, :, :], ctx[:, :], reads=("ctx",))
                for mo in range(8):
                    mosl = slice(mo * 128, (mo + 1) * 128)
                    for n4 in range(4):
                        nsl = slice(n4 * 512, (n4 + 1) * 512)
                        r.pe("matmul", out=pW, lhsT=wosb[:, mosl],
                             rhs=ctx[:, nsl], start=True, stop=True,
                             reads=("wosb", "ctx"), writes=("PS",))
                        r.dve("tensor_tensor",
                              out=z[:, mo, 1 + nsl.start:1 + nsl.stop],
                              in0=z[:, mo, 1 + nsl.start:1 + nsl.stop],
                              in1=pW, op=ALU.add, reads=("z", "PS"),
                              writes=("z",))
            if dbg and b == 0 and l == 0:
                r.store(zdump.rearrange("c p t -> p c t"), z[:, :, 1:T + 1],
                        reads=("z",))
            for c in range(8):
                r.act("activation", out=zb[:, c, 1:T + 1], in_=z[:, c, 1:T + 1],
                      func=AF.Copy, reads=("z",), writes=("zb",))
            for n4 in reversed(range(4)):
                nsl = slice(n4 * 512, (n4 + 1) * 512)
                chunk_aps = [zslf(c, nsl) for c in range(8)]
                emit_ln_cm(L, chunk_aps, "z", h2b, "h2b", 512, consts,
                           gain=lg_d[l], bias=lb_d[l],
                           gkey=(f"b_lg{l}", f"b_lb{l}"), psk=("PS",))
                for ht in range(NHT):
                    pr = ht % 2
                    w1p = w1t[pr]
                    k1 = f"w1t{pr}"
                    pA = psA if pr == 0 else psAll[:, 1024:1536]
                    pB = psB if pr == 0 else psAll[:, 1536:2048]
                    sbA, sbB = sb512, sb512b
                    kA, kB = "sb512", "sb512b"
                    r.load(w1p[:, :, :],
                           w1_d[l][:, ht * 256:(ht + 1) * 256].rearrange(
                               "(c p) m -> p c m", p=128), k1)
                    for c in range(8):
                        r.pe("matmul", out=pA, lhsT=w1p[:, c, 0:128],
                             rhs=h2b[:, c, :], start=(c == 0), stop=(c == 7),
                             reads=(k1, "h2b"), writes=())
                    for c in range(8):
                        r.pe("matmul", out=pB, lhsT=w1p[:, c, 128:256],
                             rhs=h2b[:, c, :], start=(c == 0), stop=(c == 7),
                             reads=(k1, "h2b"), writes=())
                    r.act("activation", out=sbB[:], in_=pB, func=AF.Gelu,
                          bias=b1g_d[l][:, ht, :],
                          reads=(f"b_b1g{l}",), writes=(kB,))
                    r.dve("tensor_scalar_add", out=sbA[:], in0=pA,
                          scalar1=b1a_d[l][:, ht, :],
                          reads=(f"b_b1a{l}",), writes=(kA,))
                    r.dve("tensor_tensor", out=hfull[:, ht, :],
                          in0=sbA[:], in1=sbB[:], op=ALU.mult,
                          reads=(kA, kB), writes=("hfull",))
                for mo in range(8):
                    pr = mo % 2
                    w2p_ = w2t[0]
                    k2 = "w2t0"
                    pdst = pW if pr == 0 else psAll[:, 2048:2560]
                    mosl = slice(mo * 128, (mo + 1) * 128)
                    r.load(w2p_[:, :, :],
                           w2_d[l][:, mosl].rearrange("(c p) m -> p c m", p=128),
                           k2)
                    for c in range(NHT):
                        r.pe("matmul", out=pdst, lhsT=w2p_[:, c, :],
                             rhs=hfull[:, c, :], start=(c == 0),
                             stop=(c == NHT - 1), reads=(k2, "hfull"),
                             writes=())
                    r.dve("scalar_tensor_tensor",
                          out=z[:, mo, 1 + nsl.start:1 + nsl.stop], in0=pdst,
                          scalar=b2_d[l][:, mo, :],
                          in1=z[:, mo, 1 + nsl.start:1 + nsl.stop],
                          op0=ALU.add, op1=ALU.add,
                          reads=(f"b_b2{l}", "z"), writes=("z",))
            if l == 0:
                for c in range(8):
                    r.act("activation", out=zb[:, c, 1:T + 1],
                          in_=z[:, c, 1:T + 1], func=AF.Copy, reads=("z",),
                          writes=("zb",))
        for t0 in range(4):
            sl = slice(t0 * 512, (t0 + 1) * 512)
            r.load(xs[:, :, :], x[b, :, sl].rearrange("(c p) t -> p c t", p=128), "xs")
            for c in range(8):
                r.dve("tensor_tensor", out=xs[:, c, :], in0=xs[:, c, :],
                      in1=z[:, c, 1 + sl.start:1 + sl.stop], op=ALU.add,
                      reads=("xs", "z"), writes=("xs",))
            r.store(xo[b, :, sl].rearrange("(c p) t -> p c t", p=128), xs[:, :, :],
                    reads=("xs",))
    return L


def host_prep_ffn1(inp):
    w = {}
    HID = 2816
    for l in range(2):
        w[f"wqkv{l}"] = np.ascontiguousarray(inp["L1_ffn_wqkv"][l]).astype(BF)
        w[f"wo{l}"] = np.ascontiguousarray(inp["L1_ffn_wo"][l]).astype(BF)
        w1 = np.asarray(inp["L1_ffn_w1"][l], np.float32)
        a, g = w1[:, :2730], w1[:, 2730:]
        ap = np.zeros((1024, HID), np.float32); ap[:, :2730] = a
        gp = np.zeros((1024, HID), np.float32); gp[:, :2730] = g
        w1p = np.zeros((1024, 2 * HID), np.float32)
        for t in range(HID // 128):
            w1p[:, t * 256:t * 256 + 128] = ap[:, t * 128:(t + 1) * 128]
            w1p[:, t * 256 + 128:t * 256 + 256] = gp[:, t * 128:(t + 1) * 128]
        w[f"w1p{l}"] = w1p.astype(BF)
        w2 = np.asarray(inp["L1_ffn_w2"][l], np.float32)
        w2p = np.zeros((HID, 1024), np.float32); w2p[:2730] = w2
        w[f"w2p{l}"] = w2p.astype(BF)
        b1 = np.asarray(inp["L1_ffn_b1"][l], np.float32)
        b1ap = np.zeros(HID, np.float32); b1ap[:2730] = b1[:2730]
        b1gp = np.zeros(HID, np.float32); b1gp[:2730] = b1[2730:]
        w[f"b1a{l}"] = host_bias(b1ap)
        w[f"b1g{l}"] = host_bias(b1gp)
        w[f"b2{l}"] = host_bias(inp["L1_ffn_b2"][l])
        w[f"lg{l}"] = host_bias(inp["L1_ffn_ln_g"][l])
        w[f"lb{l}"] = host_bias(inp["L1_ffn_ln_b"][l])
        w[f"qs{l}"] = np.ascontiguousarray(
            np.tile(np.asarray(inp["L1_ffn_q_scale"][l], np.float32), 2)).reshape(128, 1)
        w[f"ks{l}"] = np.ascontiguousarray(
            np.tile(np.asarray(inp["L1_ffn_k_scale"][l], np.float32), 2)).reshape(128, 1)
    DH, WIN = 64, 512
    half = np.arange(0, DH, 2, dtype=np.float64)
    inv_freq = 1.0 / 10000.0 ** (half / DH)
    t = np.arange(T, dtype=np.float64)
    freqs = t[:, None] * inv_freq[None, :]
    freqs = np.concatenate([freqs, freqs], -1)
    base = (half + 0.4 * DH) / (1.4 * DH)
    power = (t - T // 2) / float(WIN // 2)
    scale = base[None, :] ** power[:, None]
    scale = np.concatenate([scale, scale], -1)
    cs, sn = np.cos(freqs), np.sin(freqs)
    qa = (cs * scale).T; qb = (sn * scale).T
    ka = (cs / scale).T; kb = (sn / scale).T
    w["rotq"] = np.stack([np.tile(qa, (2, 1)), np.tile(qb, (2, 1))]).astype(BF)
    w["rotk"] = np.stack([np.tile(ka, (2, 1)), np.tile(kb, (2, 1))]).astype(BF)
    kk, qq = np.meshgrid(np.arange(128), np.arange(128), indexing="ij")
    w["triB"] = (kk <= qq).astype(BF)
    w["triT"] = (kk >= qq).astype(BF)
    R = np.zeros((64, 64), np.float32)
    for d in range(32):
        R[d + 32, d] = -1.0
        R[d, d + 32] = 1.0
    R2 = np.zeros((128, 128), np.float32)
    R2[0:64, 0:64] = R
    R2[64:128, 64:128] = R
    w["r2"] = R2.astype(BF)
    w["idf_in"] = np.eye(128, dtype=np.float32)
    e2 = np.zeros((128, 2), np.float32)
    e2[0:64, 0] = 1.0
    e2[64:128, 1] = 1.0
    w["e2f"] = e2
    w["f2f"] = np.ascontiguousarray(e2.T)
    return w


def launch_ffn1(x_shards, inp):
    L = Launch("ffn1")
    x = L.dram_in("x", [B_PER_CORE, D, T], F32)
    xo = L.dram_out("xo", [B_PER_CORE, D, T], F32)
    build_ffn1(L, x, xo)
    w = host_prep_ffn1(inp)
    in_maps = [{"x": x_shards[c], **w} for c in range(NCORES)]
    res = L.run(in_maps)
    return [res[c]["xo"] for c in range(NCORES)]



def build_a1a(L, x, uo):
    r = L.rec
    lsc = load_bias_col(L, "lsc", D)
    lbi = load_bias_col(L, "lbi", D)
    consts = emit_const_setup(L)
    ensure_ln_scratch(L, 512)
    xs = L.sb("xs", [128, 8, 512], F32)
    us = L.sb("us", [128, 8, 512], BF16)
    for b in range(B_PER_CORE):
        for t0 in range(4):
            sl = slice(t0 * 512, (t0 + 1) * 512)
            r.load(xs[:, :, :], x[b, :, sl].rearrange("(c p) t -> p c t", p=128), "xs")
            emit_ln_cm(L, xs, "xs", xs, "xs", 512, consts)
            emit_ln_cm(L, xs, "xs", us, "us", 512, consts,
                       gain=lsc, bias=lbi, gkey=("b_lsc", "b_lbi"))
            r.store(uo[b, :, sl].rearrange("(c p) t -> p c t", p=128), us[:, :, :],
                    reads=("us",))
    return L


def build_a1b(L, u, co):
    r = L.rec
    NT = 4096
    qkvk = L.dram_in("qkvk", [D, 3072], BF16)
    bdm = L.dram_in("bdm", [128, 128], BF16)
    idf_d = L.dram_in("idf_in", [128, 128], F32)

    consts = emit_const_setup(L)
    ones_col, ones_row, eps_ln = consts
    wq = L.sb("wq", [128, 8, 3072], BF16)
    ug = L.sb("ug", [128, 8, 1024], BF16)
    qcm = L.sb("qcm", [128, 8, 1024], BF16)
    kcm = L.sb("kcm", [128, 8, 1024], BF16)
    Vt = L.sb("Vt", [128, 8, 1040], BF16)
    ctxg = L.sb("ctxg", [128, 8, 1024], BF16)
    pb = L.sb("pb", [128, 2, 128], BF16)
    mask = L.sb("mask", [128, 128], BF16)
    idf = L.sb("idf", [128, 128], F32)
    mx1 = L.sb("mx1", [128, 1], F32)
    m1 = L.sb("m1", [1, 1], F32)
    nmx = L.sb("nmx", [128, 1], F32)
    r64 = L.sb("r64", [1, 128], F32)
    nb = L.sb("nb", [64, 128], F32)

    psAll = L.ps("psAll", [128, 4096], F32)
    pS0 = psAll[:, 0:128]
    pS1 = psAll[:, 2048:2176]
    pV = psAll[:, 512:1024]
    pAV = psAll[0:65, 1024:1152]
    pBC = psAll[0:64, 2560:2688]
    pT1 = psAll[0:1, 2560:2688]
    pM = psAll[0:128, 2816:2817]
    pP = psAll[:, 3072:3584]

    r.load(wq[:, :, :], qkvk.rearrange("(c p) m -> p c m", p=128), "wq")
    r.load(mask[:], bdm[:, :], "mask")
    r.load(idf[:], idf_d[:, :], "idf")
    r.dve("memset", ap=Vt[:, :, :], constant=0.0, writes=("Vt",))
    for h in range(16):
        r.dve("memset", ap=Vt[:, :, h * 65 + 64:h * 65 + 65], constant=1.0,
              writes=("Vt",))

    for g in range(4):
        gsl = slice(g * 1024, (g + 1) * 1024)
        r.load(ug[:, :, :], u[:, gsl].rearrange("(c p) t -> p c t", p=128), "ug")
        for qk in range(2):
            dst = qcm if qk == 0 else kcm
            dkey = "qcm" if qk == 0 else "kcm"
            for m in range(8):
                for n2 in range(2):
                    nsl = slice(n2 * 512, (n2 + 1) * 512)
                    for c in range(8):
                        r.pe("matmul", out=pP,
                             lhsT=wq[:, c, qk * 1024 + m * 128:qk * 1024 + (m + 1) * 128],
                             rhs=ug[:, c, nsl], start=(c == 0), stop=(c == 7),
                             reads=("wq", "ug"), writes=("PS",))
                    if qk == 0:
                        r.dve("tensor_scalar_mul", out=dst[:, m, nsl], in0=pP,
                              scalar1=0.125, reads=("PS",), writes=(dkey,))
                    else:
                        r.act("activation", out=dst[:, m, nsl], in_=pP,
                              func=AF.Copy, reads=("PS",), writes=(dkey,))
        for tt in range(8):
            tsl = slice(tt * 128, (tt + 1) * 128)
            for half in range(2):
                vsl = slice(2048 + half * 512, 2048 + (half + 1) * 512)
                for c in range(8):
                    r.pe("matmul", out=pV, lhsT=ug[:, c, tsl],
                         rhs=wq[:, c, vsl], start=(c == 0), stop=(c == 7),
                         reads=("ug", "wq"), writes=("PS",))
                for h in range(half * 8, half * 8 + 8):
                    r.act("activation", out=Vt[:, tt, h * 65:h * 65 + 64],
                          in_=pV[:, (h - half * 8) * 64:(h - half * 8) * 64 + 64],
                          func=AF.Copy, reads=("PS",), writes=("Vt",))
        for tt in range(8):
            tsl = slice(tt * 128, (tt + 1) * 128)
            par = tt % 2
            pS0t = pS0 if par == 0 else psAll[:, 512:640]
            pS1t = pS1 if par == 0 else psAll[:, 3072:3200]
            pAVt = (psAll[0:65, 1024:1152] if par == 0
                    else psAll[0:65, 1536:1664])
            for m in range(8):
                r.pe("matmul", out=pS0t, lhsT=kcm[0:64, m, tsl],
                     rhs=qcm[0:64, m, tsl], start=True, stop=True,
                     tile_position=(0, 0), reads=("kcm", "qcm"), writes=())
                r.pe("matmul", out=pS1t, lhsT=kcm[64:128, m, tsl],
                     rhs=qcm[64:128, m, tsl], start=True, stop=True,
                     tile_position=(64, 0), reads=("kcm", "qcm"), writes=())
                for h2, pS in ((0, pS0t), (1, pS1t)):
                    h = 2 * m + h2
                    if tt == 0:
                        r.dve("tensor_reduce", out=mx1[:, 0:1], in_=pS,
                              axis=AX.X, op=ALU.max, reads=(), writes=("mx1",))
                        r.pe("transpose", out=psAll[0:1, 640:768],
                             in_=mx1[:, 0:1], identity=idf[:],
                             reads=("mx1", "idf"), writes=())
                        r.dve("tensor_reduce", out=m1[:, 0:1],
                              in_=psAll[0:1, 640:768], axis=AX.X, op=ALU.max,
                              reads=(), writes=("m1",))
                        r.pe("matmul", out=psAll[0:128, 768:769], lhsT=negrow[:],
                             rhs=m1[:], start=True, stop=True,
                             reads=("negrow", "m1"), writes=())
                        r.act("activation", out=nmx2[:, h * 2:h * 2 + 1],
                              in_=psAll[0:128, 768:769], func=AF.Copy,
                              reads=(), writes=("nmx2",))
                    r.act("activation", out=pb[:, h2, :], in_=pS, func=AF.Exp,
                          bias=nmx2[:, h * 2:h * 2 + 1], reads=("nmx2",),
                          writes=("pb",))
                    r.dve("tensor_tensor", out=pb[:, h2, :], in0=pb[:, h2, :],
                          in1=mask[:], op=ALU.mult, reads=("pb", "mask"),
                          writes=("pb",))
                    r.pe("matmul", out=pAVt,
                         lhsT=Vt[:, tt, h * 65:h * 65 + 65],
                         rhs=pb[:, h2, :], start=True, stop=True,
                         reads=("Vt", "pb"), writes=())
                    r.dve("reciprocal", out=r64[:], in_=pAVt[64:65, :],
                          reads=(), writes=("r64",))
                    r.pe("matmul", out=pBC, lhsT=ones_row[:, 0:64], rhs=r64[:],
                         start=True, stop=True, reads=("ones_row", "r64"),
                         writes=())
                    r.act("activation", out=nb[:], in_=pBC, func=AF.Copy,
                          reads=(), writes=("nb",))
                    r.dve("tensor_tensor",
                          out=ctxg[h2 * 64:(h2 + 1) * 64, m, tsl],
                          in0=pAVt[0:64, :], in1=nb[:], op=ALU.mult,
                          reads=("nb",), writes=("ctxg",))
        r.store(co[:, gsl].rearrange("(c p) t -> p c t", p=128), ctxg[:, :, :],
                reads=("ctxg",))
    return L


def build_a1c(L, x, ci, xo):
    r = L.rec
    wout = L.dram_in("wout", [D, D], BF16)
    wsb = L.sb("wsb", [128, 8, D], BF16)
    xs = L.sb("xs", [128, 8, 512], F32)
    cs = L.sb("cs", [128, 8, 512], BF16)
    pO = L.ps("pO", [128, 512], F32)
    r.load(wsb[:, :, :], wout.rearrange("(c p) m -> p c m", p=128), "wsb")
    for b in range(B_PER_CORE):
        for t0 in range(4):
            sl = slice(t0 * 512, (t0 + 1) * 512)
            r.load(xs[:, :, :], x[b, :, sl].rearrange("(c p) t -> p c t", p=128), "xs")
            r.load(cs[:, :, :], ci[b, :, sl].rearrange("(c p) t -> p c t", p=128), "cs")
            for mo in range(8):
                for c in range(8):
                    r.pe("matmul", out=pO[:], lhsT=wsb[:, c, mo * 128:(mo + 1) * 128],
                         rhs=cs[:, c, :], start=(c == 0), stop=(c == 7),
                         reads=("wsb", "cs"), writes=("pO",))
                r.dve("tensor_tensor", out=xs[:, mo, :], in0=xs[:, mo, :],
                      in1=pO[:], op=ALU.add, reads=("xs", "pO"), writes=("xs",))
            r.store(xo[b, :, sl].rearrange("(c p) t -> p c t", p=128), xs[:, :, :],
                    reads=("xs",))
    return L


def launch_attna1(x_shards, inp):
    La = Launch("a1a")
    xa = La.dram_in("x", [B_PER_CORE, D, T], F32)
    uoa = La.dram_out("uo", [B_PER_CORE, D, T], BF16)
    build_a1a(La, xa, uoa)
    wa = {"lsc": host_bias(inp["L1_attn_a_ln_scale"]),
          "lbi": host_bias(inp["L1_attn_a_ln_bias"])}
    res = La.run([{"x": x_shards[c], **wa} for c in range(NCORES)])
    u = np.concatenate([res[c]["uo"] for c in range(NCORES)], axis=0)
    up = np.transpose(u, (1, 2, 0))
    u_shards = [np.ascontiguousarray(
        up[:, c * 256:(c + 1) * 256, :].reshape(D, 4096)) for c in range(NCORES)]
    Lb = Launch("a1b")
    ub = Lb.dram_in("u", [D, 4096], BF16)
    cob = Lb.dram_out("co", [D, 4096], BF16)
    build_a1b(Lb, ub, cob)
    qkvk = np.ascontiguousarray(
        np.asarray(inp["L1_attn_a_qkv_kernel"], np.float32).reshape(D, 3072)).astype(BF)
    ki, qi = np.meshgrid(np.arange(128), np.arange(128), indexing="ij")
    bdm = ((ki // 16 == qi // 16) & (ki % 16 <= qi % 16)).astype(BF)
    wb = {"qkvk": qkvk, "bdm": bdm, "idf_in": np.eye(128, dtype=np.float32)}
    resb = Lb.run([{"u": u_shards[c], **wb} for c in range(NCORES)])
    ctx = np.concatenate([resb[c]["co"].reshape(D, 256, 16)
                          for c in range(NCORES)], axis=1)
    ctx_b = np.ascontiguousarray(np.transpose(ctx, (2, 0, 1)))
    Lc = Launch("a1c")
    xc = Lc.dram_in("x", [B_PER_CORE, D, T], F32)
    cic = Lc.dram_in("ci", [B_PER_CORE, D, T], BF16)
    xoc = Lc.dram_out("xo", [B_PER_CORE, D, T], F32)
    build_a1c(Lc, xc, cic, xoc)
    wc = {"wout": np.ascontiguousarray(inp["L1_attn_a_out_kernel"]).astype(BF)}
    resc = Lc.run([{"x": x_shards[c], "ci": ctx_b[c * 2:(c + 1) * 2], **wc}
                   for c in range(NCORES)])
    return [resc[c]["xo"] for c in range(NCORES)]


I32 = mybir.dt.int32
try:
    IOA = bass.IndirectOffsetOnAxis
except AttributeError:
    from concourse.bass_types import IndirectOffsetOnAxis as IOA


def build_a1a_f(L, x, u_tok):
    r = L.rec
    lsc = load_bias_col(L, "lsc", D)
    lbi = load_bias_col(L, "lbi", D)
    idb_d = L.dram_in("identB", [128, 128], BF16)
    consts = emit_const_setup(L)
    psAll = L.ps("psAll", [128, 3072], F32)
    pTb = L.ps("pTb", [128, 512], BF16)
    ensure_ln_scratch(L, 512, views={
        "pstat_a": psAll[0:1, 0:512],
        "pstat_b": psAll[0:1, 512:1024],
        "pbc": psAll[:, 1024:2048],
    })
    xs = L.sb("xs", [128, 8, 512], F32)
    us = L.sb("us", [128, 8, 512], BF16)
    ut = L.sb("ut", [128, 1024], BF16)
    idb = L.sb("idb", [128, 128], BF16)
    r.load(idb[:], idb_d[:, :], "idb")
    for b in range(B_PER_CORE):
        for t0 in range(4):
            sl = slice(t0 * 512, (t0 + 1) * 512)
            r.load(xs[:, :, :], x[b, :, sl].rearrange("(c p) t -> p c t", p=128), "xs")
            emit_ln_cm(L, xs, "xs", xs, "xs", 512, consts, psk=("PS",))
            emit_ln_cm(L, xs, "xs", us, "us", 512, consts,
                       gain=lsc, bias=lbi, gkey=("b_lsc", "b_lbi"), psk=("PS",))
            for tt in range(4):
                for c in range(8):
                    r.pe("transpose", out=pTb[:, 0:128],
                         in_=us[:, c, tt * 128:(tt + 1) * 128], identity=idb[:],
                         reads=("us", "idb"), writes=("PS",))
                    r.act("activation", out=ut[:, c * 128:(c + 1) * 128],
                          in_=pTb[:, 0:128], func=AF.Copy, reads=("PS",),
                          writes=("ut",))
                r.store(u_tok[b, t0 * 512 + tt * 128:t0 * 512 + (tt + 1) * 128, :],
                        ut[:, :], reads=("ut",), key="AGUI")
    return L


def build_a1b_f(L, agu_out, agc_in):
    r = L.rec
    qkvk = L.dram_in("qkvk", [D, 3072], BF16)
    bdm = L.dram_in("bdm", [128, 128], BF16)
    idf_d = L.dram_in("idf_in", [128, 128], F32)
    idb_d = L.dram_in("identB", [128, 128], BF16)
    idxu_d = L.dram_in("idxu", [64, 64], I32)

    consts = emit_const_setup(L)
    ones_col, ones_row, eps_ln = consts
    wq = L.sb("wq", [128, 8, 3072], BF16)
    ug = L.sb("ug", [128, 8, 1024], BF16)
    st64 = L.sb("st64", [64, 1024], BF16)
    qcm = L.sb("qcm", [128, 8, 1024], BF16)
    kcm = L.sb("kcm", [128, 8, 1024], BF16)
    Vt = L.sb("Vt", [128, 8, 1040], BF16)
    ctxg = L.sb("ctxg", [128, 8, 1024], BF16)
    ctok = L.sb("ctok", [128, 1024], BF16)
    pb = L.sb("pb", [128, 2, 128], BF16)
    mask = L.sb("mask", [128, 128], BF16)
    idf = L.sb("idf", [128, 128], F32)
    idb = L.sb("idb", [128, 128], BF16)
    idxu = L.sb("idxu", [64, 64], I32)
    mx1 = L.sb("mx1", [128, 1], F32)
    m1 = L.sb("m1", [1, 1], F32)
    nmx2 = L.sb("nmx2", [128, 32], F32)
    negrow = L.sb("negrow", [1, 128], F32)
    r64 = L.sb("r64", [1, 128], F32)
    nb = L.sb("nb", [64, 128], F32)

    psAll = L.ps("psAll", [128, 3584], F32)
    pTb = L.ps("pTb", [128, 1024], BF16)
    pS0 = psAll[:, 0:128]
    pS1 = psAll[:, 2048:2176]
    pV = psAll[:, 512:1024]
    pAV = psAll[0:65, 1024:1152]
    pBC = psAll[0:64, 2560:2688]
    pT1 = psAll[0:1, 2560:2688]
    pM = psAll[0:128, 2816:2817]
    pP = psAll[:, 3072:3584]

    r.load(wq[:, :, :], qkvk.rearrange("(c p) m -> p c m", p=128), "wq")
    r.load(mask[:], bdm[:, :], "mask")
    r.load(idf[:], idf_d[:, :], "idf")
    r.load(idb[:], idb_d[:, :], "idb")
    r.load(idxu[:], idxu_d[:, :], "idxu")
    r.dve("memset", ap=negrow[:], constant=-1.0, writes=("negrow",))
    r.dve("memset", ap=Vt[:, :, :], constant=0.0, writes=("Vt",))
    for h in range(16):
        r.dve("memset", ap=Vt[:, :, h * 65 + 64:h * 65 + 65], constant=1.0,
              writes=("Vt",))

    for g in range(4):
        for s in range(16):
            r.op("pool", "indirect_dma_start", out=st64[:, :], out_offset=None,
                 in_=agu_out[:, :], in_offset=IOA(idxu[:, g * 16 + s:g * 16 + s + 1], 0),
                 reads=("AGUO", "idxu"), writes=("st64",))
            ugv = ug[:].rearrange("p c (pp s) -> p c pp s", s=16)
            for c in range(8):
                r.pe("transpose", out=pTb[:, 0:64],
                     in_=st64[:, c * 128:(c + 1) * 128], identity=idb[0:64, 0:64],
                     reads=("st64", "idb"), writes=("PS",))
                r.act("activation", out=ugv[:, c, :, s], in_=pTb[:, 0:64],
                      func=AF.Copy, reads=("PS",), writes=("ug",))
        for qk in range(2):
            dst = qcm if qk == 0 else kcm
            dkey = "qcm" if qk == 0 else "kcm"
            for m in range(8):
                for n2 in range(2):
                    nsl = slice(n2 * 512, (n2 + 1) * 512)
                    for c in range(8):
                        r.pe("matmul", out=pP,
                             lhsT=wq[:, c, qk * 1024 + m * 128:qk * 1024 + (m + 1) * 128],
                             rhs=ug[:, c, nsl], start=(c == 0), stop=(c == 7),
                             reads=("wq", "ug"), writes=("PS",))
                    if qk == 0:
                        r.dve("tensor_scalar_mul", out=dst[:, m, nsl], in0=pP,
                              scalar1=0.125, reads=("PS",), writes=(dkey,))
                    else:
                        r.act("activation", out=dst[:, m, nsl], in_=pP,
                              func=AF.Copy, reads=("PS",), writes=(dkey,))
        for tt in range(8):
            tsl = slice(tt * 128, (tt + 1) * 128)
            for half in range(2):
                vsl = slice(2048 + half * 512, 2048 + (half + 1) * 512)
                for c in range(8):
                    r.pe("matmul", out=pV, lhsT=ug[:, c, tsl],
                         rhs=wq[:, c, vsl], start=(c == 0), stop=(c == 7),
                         reads=("ug", "wq"), writes=("PS",))
                for h in range(half * 8, half * 8 + 8):
                    r.act("activation", out=Vt[:, tt, h * 65:h * 65 + 64],
                          in_=pV[:, (h - half * 8) * 64:(h - half * 8) * 64 + 64],
                          func=AF.Copy, reads=("PS",), writes=("Vt",))
        for tt in range(8):
            tsl = slice(tt * 128, (tt + 1) * 128)
            par = tt % 2
            pS0t = pS0 if par == 0 else psAll[:, 512:640]
            pS1t = pS1 if par == 0 else psAll[:, 3072:3200]
            pAVt = (psAll[0:65, 1024:1152] if par == 0
                    else psAll[0:65, 1536:1664])
            for m in range(8):
                r.pe("matmul", out=pS0t, lhsT=kcm[0:64, m, tsl],
                     rhs=qcm[0:64, m, tsl], start=True, stop=True,
                     tile_position=(0, 0), reads=("kcm", "qcm"), writes=())
                r.pe("matmul", out=pS1t, lhsT=kcm[64:128, m, tsl],
                     rhs=qcm[64:128, m, tsl], start=True, stop=True,
                     tile_position=(64, 0), reads=("kcm", "qcm"), writes=())
                for h2, pS in ((0, pS0t), (1, pS1t)):
                    h = 2 * m + h2
                    if tt == 0:
                        r.dve("tensor_reduce", out=mx1[:, 0:1], in_=pS,
                              axis=AX.X, op=ALU.max, reads=(), writes=("mx1",))
                        r.pe("transpose", out=psAll[0:1, 640:768],
                             in_=mx1[:, 0:1], identity=idf[:],
                             reads=("mx1", "idf"), writes=())
                        r.dve("tensor_reduce", out=m1[:, 0:1],
                              in_=psAll[0:1, 640:768], axis=AX.X, op=ALU.max,
                              reads=(), writes=("m1",))
                        r.pe("matmul", out=psAll[0:128, 768:769], lhsT=negrow[:],
                             rhs=m1[:], start=True, stop=True,
                             reads=("negrow", "m1"), writes=())
                        r.act("activation", out=nmx2[:, h * 2:h * 2 + 1],
                              in_=psAll[0:128, 768:769], func=AF.Copy,
                              reads=(), writes=("nmx2",))
                    r.act("activation", out=pb[:, h2, :], in_=pS, func=AF.Exp,
                          bias=nmx2[:, h * 2:h * 2 + 1], reads=("nmx2",),
                          writes=("pb",))
                    r.dve("tensor_tensor", out=pb[:, h2, :], in0=pb[:, h2, :],
                          in1=mask[:], op=ALU.mult, reads=("pb", "mask"),
                          writes=("pb",))
                    r.pe("matmul", out=pAVt,
                         lhsT=Vt[:, tt, h * 65:h * 65 + 65],
                         rhs=pb[:, h2, :], start=True, stop=True,
                         reads=("Vt", "pb"), writes=())
                    r.dve("reciprocal", out=r64[:], in_=pAVt[64:65, :],
                          reads=(), writes=("r64",))
                    r.pe("matmul", out=pBC, lhsT=ones_row[:, 0:64], rhs=r64[:],
                         start=True, stop=True, reads=("ones_row", "r64"),
                         writes=())
                    r.act("activation", out=nb[:], in_=pBC, func=AF.Copy,
                          reads=(), writes=("nb",))
                    r.dve("tensor_tensor",
                          out=ctxg[h2 * 64:(h2 + 1) * 64, m, tsl],
                          in0=pAVt[0:64, :], in1=nb[:], op=ALU.mult,
                          reads=("nb",), writes=("ctxg",))
        for tt in range(8):
            tsl = slice(tt * 128, (tt + 1) * 128)
            for c in range(8):
                r.pe("transpose", out=pTb[:, 0:128], in_=ctxg[:, c, tsl],
                     identity=idb[:], reads=("ctxg", "idb"), writes=("PS",))
                r.act("activation", out=ctok[:, c * 128:(c + 1) * 128],
                      in_=pTb[:, 0:128], func=AF.Copy, reads=("PS",),
                      writes=("ctok",))
            r.store(agc_in[g * 1024 + tt * 128:g * 1024 + (tt + 1) * 128, :],
                    ctok[:, :], reads=("ctok",), key="AGCI")
    return L


def build_a1c_f(L, x, agc_out, xo):
    r = L.rec
    wout = L.dram_in("wout", [D, D], BF16)
    idxc_d = L.dram_in("idxc", [128, 32], I32)
    idb_d = L.dram_in("identB", [128, 128], BF16)
    wsb = L.sb("wsb", [128, 8, D], BF16)
    xs = L.sb("xs", [128, 8, 512], F32)
    cs = L.sb("cs", [128, 8, 512], BF16)
    ctok = L.sb("ctok", [128, 1024], BF16)
    idxc = L.sb("idxc", [128, 32], I32)
    idb = L.sb("idb", [128, 128], BF16)
    psAll = L.ps("psAll", [128, 1024], F32)
    pTb = L.ps("pTb", [128, 512], BF16)
    pO = psAll[:, 0:512]
    r.load(wsb[:, :, :], wout.rearrange("(c p) m -> p c m", p=128), "wsb")
    r.load(idxc[:], idxc_d[:, :], "idxc")
    r.load(idb[:], idb_d[:, :], "idb")
    for b in range(B_PER_CORE):
        for t0 in range(4):
            sl = slice(t0 * 512, (t0 + 1) * 512)
            r.load(xs[:, :, :], x[b, :, sl].rearrange("(c p) t -> p c t", p=128), "xs")
            for half4 in range(4):
                tglob = t0 * 4 + half4
                r_blk, hh = tglob // 2, tglob % 2
                col = b * 16 + r_blk * 2 + hh
                r.op("pool", "indirect_dma_start", out=ctok[:, :],
                     out_offset=None, in_=agc_out[:, :],
                     in_offset=IOA(idxc[:, col:col + 1], 0),
                     reads=("AGCO", "idxc"), writes=("ctok",))
                for c in range(8):
                    r.pe("transpose", out=pTb[:, 0:128],
                         in_=ctok[:, c * 128:(c + 1) * 128], identity=idb[:],
                         reads=("ctok", "idb"), writes=("PS",))
                    r.act("activation",
                          out=cs[:, c, half4 * 128:(half4 + 1) * 128],
                          in_=pTb[:, 0:128], func=AF.Copy, reads=("PS",),
                          writes=("cs",))
            for mo in range(8):
                for c in range(8):
                    r.pe("matmul", out=pO, lhsT=wsb[:, c, mo * 128:(mo + 1) * 128],
                         rhs=cs[:, c, :], start=(c == 0), stop=(c == 7),
                         reads=("wsb", "cs"), writes=("PS",))
                r.dve("tensor_tensor", out=xs[:, mo, :], in0=xs[:, mo, :],
                      in1=pO, op=ALU.add, reads=("xs", "PS"), writes=("xs",))
            r.store(xo[b, :, sl].rearrange("(c p) t -> p c t", p=128), xs[:, :, :],
                    reads=("xs",))
    return L


def build_fused(L):
    x0 = L.dram_in("x", [B_PER_CORE, D, T], F32)
    out = L.dram_out("out", [B_PER_CORE, D, T], F32)
    xA = L.dram_tmp("xA", [B_PER_CORE, D, T], F32)
    xB = L.dram_tmp("xB", [B_PER_CORE, D, T], F32)
    agu_in = L.dram_tmp("agu_in", [B_PER_CORE, T, D], BF16)
    agu_out = L.dram_tmp("agu_out", [16 * T, D], BF16)
    agc_in = L.dram_tmp("agc_in", [4096, D], BF16)
    agc_out = L.dram_tmp("agc_out", [8 * 4096, D], BF16)
    r = L.rec
    with L.scope("s0"):
        build_gla(L, x0, xA)
    with L.scope("s1"):
        build_l0ffn(L, xA, xB)
    with L.scope("s2"):
        build_attnb(L, xB, xA)
    with L.scope("s3"):
        build_l0ffn(L, xA, xB)
    with L.scope("s4"):
        build_a1a_f(L, xB, agu_in)
    r.op("pool", "collective_compute", kind="AllGather", op=ALU.bypass,
         replica_groups=[list(range(NCORES))],
         ins=[agu_in.rearrange("b t d -> (b t) d").opt()], outs=[agu_out.opt()],
         reads=("AGUI",), writes=("AGUO",))
    with L.scope("s5"):
        build_a1b_f(L, agu_out, agc_in)
    r.op("pool", "collective_compute", kind="AllGather", op=ALU.bypass,
         replica_groups=[list(range(NCORES))],
         ins=[agc_in.opt()], outs=[agc_out.opt()],
         reads=("AGCI",), writes=("AGCO",))
    with L.scope("s6"):
        build_a1c_f(L, xB, agc_out, xA)
    with L.scope("s7"):
        build_ffn1(L, xA, xB)
    with L.scope("s8"):
        build_rwkv(L, xB, xA)
    with L.scope("s9"):
        build_ffn1(L, xA, out)
    return L


def host_prep_fused(inp, core):
    w = {}
    w.update(host_prep_gla(inp))
    w.update(host_prep_l0ffn(inp))
    w.update(host_prep_attnb(inp))
    w.update(host_prep_ffn1(inp))
    w.update(host_prep_rwkv(inp))
    w["lsc"] = host_bias(inp["L1_attn_a_ln_scale"])
    w["lbi"] = host_bias(inp["L1_attn_a_ln_bias"])
    w["qkvk"] = np.ascontiguousarray(
        np.asarray(inp["L1_attn_a_qkv_kernel"], np.float32).reshape(D, 3072)).astype(BF)
    ki, qi = np.meshgrid(np.arange(128), np.arange(128), indexing="ij")
    w["bdm"] = ((ki // 16 == qi // 16) & (ki % 16 <= qi % 16)).astype(BF)
    w["identB"] = np.eye(128).astype(BF)
    w["wout"] = np.ascontiguousarray(inp["L1_attn_a_out_kernel"]).astype(BF)
    idxu = np.zeros((64, 64), np.int32)
    for g in range(4):
        for s in range(16):
            pp = np.arange(64)
            idxu[:, g * 16 + s] = s * 2048 + 256 * core + 64 * g + pp
    w["idxu"] = idxu
    idxc = np.zeros((128, 32), np.int32)
    for b in range(2):
        for rb in range(8):
            for hh in range(2):
                pp = np.arange(128)
                idxc[:, b * 16 + rb * 2 + hh] = (
                    rb * 4096 + (128 * hh + pp) * 16 + (2 * core + b))
    w["idxc"] = idxc
    return w


def kernel_fused(inputs):
    x = to_cm_shards(inputs["L0_attn_a_x"])
    L = Launch("fused")
    build_fused(L)
    in_maps = [{"x": x[c], **host_prep_fused(inputs, c)} for c in range(NCORES)]
    res = L.run(in_maps)
    return from_cm_shards([res[c]["out"] for c in range(NCORES)])



def to_cm_shards(x):
    xc = np.ascontiguousarray(np.transpose(np.asarray(x, np.float32), (0, 2, 1)))
    return [xc[c * 2:(c + 1) * 2] for c in range(NCORES)]


def from_cm_shards(shards):
    xc = np.concatenate(shards, axis=0)
    return np.ascontiguousarray(np.transpose(xc, (0, 2, 1)))


def kernel(**inputs):
    return kernel_fused(inputs)


def kernel_multilaunch(**inputs):
    x = to_cm_shards(inputs["L0_attn_a_x"])
    x = launch_gla(x, inputs)
    x = launch_l0ffn(x, inputs)
    x = launch_attnb(x, inputs)
    x = launch_l0ffn(x, inputs)
    x = launch_attna1(x, inputs)
    x = launch_ffn1(x, inputs)
    x = launch_rwkv(x, inputs)
    x = launch_ffn1(x, inputs)
    return from_cm_shards(x)
